# version 11
# speedup vs baseline: 1.2900x; 1.0386x over previous
import os
import numpy as np
import ml_dtypes
import concourse.bass as bass
import concourse.mybir as mybir
from concourse.bass_utils import run_bass_kernel_spmd

F32 = mybir.dt.float32
BF16 = mybir.dt.bfloat16
AF = mybir.ActivationFunctionType
ALU = mybir.AluOpType
BF = ml_dtypes.bfloat16

B, T, D = 8, 2048, 1024
TQ = T // 4
NKD = D // 128
DIN, DST, DTR = 2048, 16, 64
NDI = DIN // 128
NH, DH, WIN = 8, 64, 512
QK_SCALE = 8.0
HID, HIDP = 2730, 2816
NOGH = HIDP // 128
EPS = 1e-5
NEG = -1073741824.0
AA_LIM = int(os.environ.get('AA_LIM', '99'))
AA_PH = int(os.environ.get('AA_PH', '99'))
AA_G = int(os.environ.get('AA_G', '99'))
AA_D = int(os.environ.get('AA_D', '99'))
AA_S = int(os.environ.get('AA_S', '31'))
FF_L = int(os.environ.get('FF_L', '2'))
FF_PH = int(os.environ.get('FF_PH', '99'))
FF_DBG = int(os.environ.get('FF_DBG', '0'))
_DBL = int(os.environ.get('DBL', '0'))
_DBH = int(os.environ.get('DBH', '0'))
ARENA = 43392

_built = {}



def _tile_lhsT(wT, ncols=128):
    K, M = wT.shape
    nkc, nog = K // 128, M // ncols
    a = wT.reshape(nkc, 128, nog, ncols).transpose(2, 0, 1, 3)
    return np.ascontiguousarray(a.transpose(0, 2, 1, 3))

def _cols(v):
    return np.ascontiguousarray(np.asarray(v, np.float32).reshape(-1, 128).T)

def _band_masks():
    m = np.zeros((8, 128, 512), np.float32)
    for j in range(8):
        ki = np.arange(128)[:, None] + 128 * j - 512
        q = np.arange(512)[None, :]
        ok = (q - ki >= 0) & (q - ki <= WIN)
        m[j] = np.where(ok, 0.0, NEG)
    return m

def _rotary_tables():
    half_idx = np.arange(0, DH, 2, dtype=np.float64)
    inv_freq = 1.0 / 10000.0 ** (half_idx / DH)
    t = np.arange(T, dtype=np.float64)
    freqs = np.concatenate([t[:, None] * inv_freq[None, :]] * 2, axis=-1)
    base = (half_idx + 0.4 * DH) / (1.4 * DH)
    power = (t - T // 2) / float(WIN // 2)
    scale = np.concatenate([base[None, :] ** power[:, None]] * 2, axis=-1)
    cos, sin = np.cos(freqs), np.sin(freqs)
    out = []
    for a in (cos * scale, sin * scale, cos / scale, sin / scale):
        a = a.T.astype(np.float32)
        out.append(np.concatenate([a, a], axis=0))
    return out

def _swap_sign():
    M = np.zeros((128, 128), np.float32)
    for h in range(2):
        b = 64 * h
        for p in range(32):
            M[b + p + 32, b + p] = -1.0
            M[b + p, b + p + 32] = 1.0
    return M


def stage_shared(inp):
    g = {}
    f32 = lambda a: np.ascontiguousarray(np.asarray(a), dtype=np.float32)
    for d in ("f", "b"):
        p = lambda k: f32(inp[f"L0_attn_a_{d}_{k}"])
        g[f"a{d}_inW"] = _tile_lhsT(p("in_proj_w").T)
        g[f"a{d}_outW"] = _tile_lhsT(p("out_proj_w").T)
        xpT = p("x_proj_w").T
        g[f"a{d}_xpW1"] = np.ascontiguousarray(
            xpT[:, 0:80].reshape(NDI, 128, 80).transpose(1, 0, 2).reshape(128, NDI * 80))
        g[f"a{d}_xpW2"] = np.ascontiguousarray(
            xpT[:, 80:96].reshape(NDI, 128, 16).transpose(1, 0, 2).reshape(128, NDI * 16))
        g[f"a{d}_dtW"] = f32(p("dt_proj_w").T)
        cw = p("conv_w")
        if d == "b":
            cw = cw[:, ::-1]
        g[f"a{d}_convW"] = np.ascontiguousarray(
            cw.reshape(NDI, 128, 4).transpose(1, 0, 2).reshape(128, NDI * 4))
        g[f"a{d}_convB"] = _cols(p("conv_b"))
        g[f"a{d}_dtB"] = _cols(p("dt_proj_b"))
        g[f"a{d}_D"] = _cols(p("D"))
        g[f"a{d}_Alog"] = np.ascontiguousarray(
            p("A_log").reshape(NDI, 128, DST).transpose(1, 0, 2).reshape(128, NDI * DST))
    for l in range(2):
        wqkv = f32(inp["L0_ffn_wqkv"][l])
        g[f"f{l}_qkW"] = _tile_lhsT(wqkv[:, 0:1024])
        g[f"f{l}_vW"] = np.ascontiguousarray(wqkv[:, 1024:1536].reshape(NKD, 128, 512))
        g[f"f{l}_qs"] = np.ascontiguousarray(np.tile(f32(inp["L0_ffn_q_scale"][l]), 2)[:, None])
        g[f"f{l}_ks"] = np.ascontiguousarray(np.tile(f32(inp["L0_ffn_k_scale"][l]), 2)[:, None])
        g[f"f{l}_woW"] = _tile_lhsT(f32(inp["L0_ffn_wo"][l]))
        g[f"f{l}_lng"] = _cols(inp["L0_ffn_ln_g"][l])
        g[f"f{l}_lnb"] = _cols(inp["L0_ffn_ln_b"][l])
        w1 = f32(inp["L0_ffn_w1"][l])
        w1a = np.zeros((D, HIDP), np.float32); w1a[:, :HID] = w1[:, :HID]
        w1g = np.zeros((D, HIDP), np.float32); w1g[:, :HID] = w1[:, HID:]
        g[f"f{l}_w1a"] = _tile_lhsT(w1a)
        g[f"f{l}_w1g"] = _tile_lhsT(w1g)
        b1 = f32(inp["L0_ffn_b1"][l])
        b1a = np.zeros(HIDP, np.float32); b1a[:HID] = b1[:HID]
        b1g = np.zeros(HIDP, np.float32); b1g[:HID] = b1[HID:]
        g[f"f{l}_b1a"] = _cols(b1a); g[f"f{l}_b1g"] = _cols(b1g)
        w2p = np.zeros((HIDP, D), np.float32); w2p[:HID] = f32(inp["L0_ffn_w2"][l])
        g[f"f{l}_w2W"] = _tile_lhsT(w2p)
        g[f"f{l}_b2"] = _cols(inp["L0_ffn_b2"][l])
    p = lambda k: f32(inp[f"L0_attn_b_{k}"])
    g["b_gamma"] = _cols(p("gamma")); g["b_beta"] = _cols(p("beta"))
    g["b_inW"] = _tile_lhsT(p("W_in"))
    cw3 = p("conv_w")[:, 0, :]
    g["b_convW"] = np.ascontiguousarray(
        cw3.T.reshape(NKD, 128, 3).transpose(1, 0, 2).reshape(128, NKD * 3))
    g["b_convB"] = _cols(p("conv_b"))
    g["b_xpW"] = np.ascontiguousarray(p("W_xp").reshape(NKD, 128, 64))
    g["b_BgW"] = np.ascontiguousarray(p("W_Bg").reshape(NKD, 128, 16))
    g["b_CgW"] = np.ascontiguousarray(p("W_Cg").reshape(NKD, 128, 16))
    g["b_dtW"] = f32(p("W_dt"))
    g["b_dtB"] = _cols(p("b_dt"))
    g["b_BpW"] = f32(p("W_Bp")); g["b_CpW"] = f32(p("W_Cp"))
    g["b_A"] = _cols(p("A")); g["b_Dsk"] = _cols(p("D_skip"))
    g["b_outW"] = _tile_lhsT(p("W_out"))
    g["c_eye"] = np.eye(128, dtype=np.float32).astype(BF)
    g["c_swap"] = _swap_sign().astype(BF)
    es = np.zeros((DST, DST, 128), np.float32)
    for s in range(DST):
        es[:, s][s] = 1.0
    es = np.zeros((DST, DST * 128), np.float32)
    for s in range(DST):
        es[s, s * 128:(s + 1) * 128] = 1.0
    g["c_Es"] = es.astype(BF)
    g["c_masks"] = np.ascontiguousarray(
        _band_masks().transpose(1, 0, 2).reshape(128, 8 * 512)).astype(BF)
    g["c_rot"] = np.ascontiguousarray(
        np.concatenate(_rotary_tables(), axis=1)).astype(BF)
    g["c_ones"] = np.ones((128, 2), np.float32)
    g["c_orow"] = np.ones((128, 128), np.float32)
    g["c_orow_bf"] = np.ones((128, 128), np.float32).astype(BF)
    g["c_ones_bf"] = np.ones((128, 2), np.float32).astype(BF)
    hsel = np.zeros((128, 2), np.float32)
    hsel[0:64, 0] = 1.0; hsel[64:128, 1] = 1.0
    g["c_hsel"] = hsel.astype(BF)
    s2b = np.zeros((2, 128), np.float32)
    s2b[0, 0:64] = 1.0; s2b[1, 64:128] = 1.0
    g["c_s2b"] = s2b
    g["c_s2b_bf"] = s2b.astype(BF)
    rsel = np.zeros((2, 256), np.float32)
    rsel[0, 0:128] = 1.0; rsel[1, 128:256] = 1.0
    g["c_rsel"] = rsel.astype(BF)
    return g



_DTSZ = {"dt.float32": 4, "dt.bfloat16": 2, "dt.float32r": 4, "dt.int32": 4,
         "dt.float16": 2, "dt.int8": 1, "dt.uint8": 1}


def _ap_rect(ap):
    h = ap.tensor
    kind = type(h).__name__
    if "DRam" in kind or "PSum" in kind or "Psum" in kind:
        return None
    sz = _DTSZ[str(ap.dtype)]
    C = h.shape[-1]
    p0 = ap.offset // C + getattr(h, "base_partition", 0)
    c0 = ap.offset % C
    pext = 0
    cext = 0
    for s, n in ap.ap:
        if n == 1:
            continue
        if s % C == 0:
            pext += (s // C) * (n - 1)
        else:
            cext += s * (n - 1)
    return (h.name, p0, p0 + pext + 1, c0 * sz, (c0 + cext + 1) * sz)


class _RecInst:
    def then_inc(self, *a, **k):
        return self


class _Rec:
    def __init__(self):
        self.wr = []
        self.rd = []
        self.reads_psum = False

    def __getattr__(self, mname):
        def call(*args, **kwargs):
            aps = [a for a in args if hasattr(a, "tensor")]
            aps += [v for v in kwargs.values() if hasattr(v, "tensor")]
            if mname == "dma_start":
                aps = [kwargs["out"], kwargs["in_"]]
            if aps:
                r = _ap_rect(aps[0])
                if r:
                    self.wr.append(r)
                for a in aps[1:]:
                    r = _ap_rect(a)
                    if r:
                        self.rd.append(r)
                    elif "Sum" in type(a.tensor).__name__ or \
                            "sum" in type(a.tensor).__name__:
                        self.reads_psum = True
            return _RecInst()
        return call


def _overlap(a, b):
    return (a[0] == b[0] and a[1] < b[2] and b[1] < a[2]
            and a[3] < b[4] and b[3] < a[4])


class EQ:
    HIST = 64

    def __init__(self, key, drain=False):
        self.key = key
        self.drain = drain
        self.items = []
        self.count = 0
        self.n_uncov = 0
        self.waited = {}
        self.hist = []
        self.self_waited = 0

    def wait(self, tok):
        if tok is None:
            return
        if not (isinstance(tok, tuple) and len(tok) == 2 and hasattr(tok[0], "key")):
            for t in tok:
                self.wait(t)
            return
        eq2, v = tok
        if eq2 is self:
            return
        if self.waited.get(eq2.key, 0) >= v:
            return
        self.waited[eq2.key] = v
        self.items.append(("w", eq2, v))

    def do_sem(self, fn, sem_eq, inc, *deps):
        for dp in deps:
            self.wait(dp)
        self.items.append(("i", fn, inc, sem_eq))

    def do(self, fn, *deps, inc=0):
        for dp in deps:
            self.wait(dp)
        if self.drain:
            if getattr(self, "blanket", False) or \
                    (self.key == "dve" and _DBL <= self.count < _DBH):
                if self.count:
                    self.items.append(("w", self, self.count))
                self.items.append(("i", fn, 1))
                self.count += 1
                self.n_uncov = 0
                return (self, self.count) if inc else None
            rec = _Rec()
            fn(rec)
            need = self.count if rec.reads_psum else 0
            for (cum, hw, hr) in self.hist:
                if cum <= self.self_waited:
                    continue
                hit = False
                for a in rec.rd:
                    for b2 in hw:
                        if _overlap(a, b2):
                            hit = True
                            break
                    if hit:
                        break
                if not hit:
                    for a in rec.wr:
                        for b2 in hw:
                            if _overlap(a, b2):
                                hit = True
                                break
                        if not hit:
                            for b2 in hr:
                                if _overlap(a, b2):
                                    hit = True
                                    break
                        if hit:
                            break
                if hit and cum > need:
                    need = cum
            if need > self.self_waited:
                self.items.append(("w", self, need))
                self.self_waited = need
            self.items.append(("i", fn, 1))
            self.count += 1
            self.n_uncov = 0
            self.hist.append((self.count, rec.wr, rec.rd))
            if len(self.hist) > self.HIST:
                self.hist.pop(0)
            return (self, self.count) if inc else None
        self.items.append(("i", fn, inc))
        if inc:
            self.count += inc
            self.n_uncov = 0
            return (self, self.count)
        self.n_uncov += 1
        return None

    def tok(self):
        assert self.n_uncov == 0, f"{self.key}: {self.n_uncov} uncovered instrs at fence"
        return (self, self.count)


class Lane:
    def __init__(self, key):
        self.key = key
        self.count = 0


class Prog:
    NLANES = 8

    def __init__(self, nc):
        self.nc = nc
        blanket = set(os.environ.get("DRAIN_BLANKET", "").split(","))
        self.pe, self.act, self.dve, self.pool, self.sp = (
            EQ("pe"), EQ("act", drain=True), EQ("dve", drain=True),
            EQ("pool", drain=True), EQ("sp"))
        for q in (self.act, self.dve, self.pool):
            if q.key in blanket:
                q.blanket = True
        self.all = [self.pe, self.act, self.dve, self.pool, self.sp]
        self.lanes = [Lane(f"d{i}") for i in range(self.NLANES)]
        self._rr = 0

    def dma(self, out, in_, *deps):
        ln = self.lanes[self._rr % self.NLANES]
        self._rr += 1
        if ln.count:
            self.sp.items.append(("w", ln, ln.count))
        for dp in deps:
            self.sp.wait(dp)
        self.sp.items.append(("i", lambda e: e.dma_start(out=out, in_=in_), 16, ln))
        ln.count += 16
        return (ln, ln.count)

    def barrier(self):
        toks = [q.tok() for q in self.all] + [(ln, ln.count) for ln in self.lanes if ln.count]
        for q in self.all:
            for t in toks:
                q.wait(t)

    def mm(self, out_ap, lhsT, rhs, start, stop, deps=(), inc=0):
        n = rhs.shape[-1]
        maxm = 512 if rhs.dtype != BF16 else 512
        if n <= maxm:
            return self.pe.do(
                lambda e: e.matmul(out_ap, lhsT, rhs, start=start, stop=stop),
                *deps, inc=inc)
        t = None
        nch = (n + maxm - 1) // maxm
        for i in range(nch):
            a, b2 = i * maxm, min((i + 1) * maxm, n)
            def f(e, a=a, b2=b2):
                return e.matmul(out_ap[:, a:b2], lhsT, rhs[:, a:b2], start=start, stop=stop)
            t = self.pe.do(f, *(deps if i == 0 else ()), inc=(inc if i == nch - 1 else 0))
        return t


    def emit(self, block, sems):
        def runner(q):
            def run(eng):
                for it in q.items:
                    if it[0] == "w":
                        eng.wait_ge(sems[it[1].key], it[2])
                    else:
                        inst = it[1](eng)
                        if it[2]:
                            key = it[3].key if len(it) > 3 else q.key
                            inst.then_inc(sems[key], it[2])
            return run
        block.tensor(runner(self.pe))
        block.scalar(runner(self.act))
        block.vector(runner(self.dve))
        block.gpsimd(runner(self.pool))
        block.sync(runner(self.sp))


class Arena:
    def __init__(self, ap):
        self.ap = ap
        self.ap_bf = ap.bitcast(BF16)
        self.frames = [{}]
        self.top = 0
        self.marks = []

    def push(self):
        self.marks.append(self.top)

    def pop(self):
        self.top = self.marks.pop()

    def alloc(self, cols):
        base = self.top
        self.top += cols
        assert self.top <= ARENA, f"arena overflow: {self.top} > {ARENA}"
        return self.ap[:, base:base + cols]

    def alloc_bf(self, bfcols):
        cols = (bfcols + 1) // 2
        f = self.alloc(cols)
        base = 0
        return f.bitcast(BF16)



class Builder:
    def __init__(self, nc, shared, upto="full"):
        self.nc = nc
        self.upto = upto
        self.p = Prog(nc)
        self.dr = {}
        for k, v in shared.items():
            dt = BF16 if v.dtype == BF else F32
            self.dr[k] = nc.dram_tensor(k, list(v.shape), dt, kind="ExternalInput").ap()
        self.dr["xT"] = nc.dram_tensor("xT", [D, T], F32, kind="ExternalInput").ap()
        self.out = nc.dram_tensor("outT", [D, T], F32, kind="ExternalOutput").ap()
        self.hbm_s = nc.dram_tensor("hbm_s", [D, T], F32, kind="Internal").ap()
        self.hbm_f = nc.dram_tensor("hbm_f", [D, T], F32, kind="Internal").ap()
        if FF_DBG:
            self.dbgT = nc.dram_tensor("dbgT", [8 * D, T], F32,
                                       kind="ExternalOutput").ap()
        self.wpre1 = nc.dram_tensor("wpre1", [2 * NOGH * 128, 2048], BF16,
                                    kind="Internal").ap()
        self.wpre2 = nc.dram_tensor("wpre2", [2 * 8 * 128, 2816], BF16,
                                    kind="Internal").ap()
        self.wpreq = nc.dram_tensor("wpreq", [2 * 8 * 128, 1024], BF16,
                                    kind="Internal").ap()
        self.wprea = nc.dram_tensor("wprea", [2 * 32 * 128, 1024], BF16,
                                    kind="Internal").ap()
        self.wpreo = nc.dram_tensor("wpreo", [2 * 8 * 128, 2048], BF16,
                                    kind="Internal").ap()
        self.t_w1pre = {}
        self.t_w2pre = {}
        self.t_qkpre = {}
        self.t_apre = {}
        self.t_opre = {}
        self._ctx = []
        sb = self._sb = lambda n, s, dt=F32: self._keep(nc.sbuf_tensor(n, s, dt))
        self.AR = Arena(sb("arena", [128, ARENA]))
        self.rows_a = sb("rows_a", [128, 2048])
        self.rows_b = sb("rows_b", [128, 2048])
        self.rows_c = sb("rows_c", [128, 2048])
        self.states_a = sb("states_a", [128, 512])
        self.states_b = sb("states_b", [128, 8])
        self.tails = sb("tails", [128, 48])
        self.eye = sb("eye", [128, 128], BF16)
        self.swap = sb("swap", [128, 128], BF16)
        self.Es = sb("Es", [16, 2048], BF16)
        self.ones = sb("ones", [128, 2])
        self.orow_bf = sb("orow_bf", [128, 128], BF16)
        self.ones_bf = sb("ones_bf", [128, 2], BF16)
        self.hsel = sb("hsel", [128, 2], BF16)
        self.s2b = sb("s2b", [2, 128])
        self.s2b_bf = sb("s2b_bf", [2, 128], BF16)
        self.rsel = sb("rsel", [2, 256], BF16)
        self.aw_conv = sb("aw_conv", [128, 64])
        self.aw_cvB = sb("aw_cvB", [128, 16])
        self.aw_dtB = sb("aw_dtB", [128, 16])
        self.aw_D = sb("aw_D", [128, 16])
        self.fw_qs = sb("fw_qs", [128, 1])
        self.fw_ks = sb("fw_ks", [128, 1])
        self.col16 = sb("col16", [128, 16])
        self.col8a = sb("col8a", [128, 8])
        self.col8b = sb("col8b", [128, 8])
        self.col8c = sb("col8c", [128, 8])
        self.col8d = sb("col8d", [128, 8])
        self.konst = sb("konst", [128, 4])
        self.negb_t = sb("negb_t", [2, 512], BF16)
        self.bc_rows = sb("bc_rows", [2, 1024])
        self._t_bc = None
        self.PS = self._keep(nc.psum_tensor("PS", [128, 4096], F32))
        self.sems = {}

    def _keep(self, cm):
        v = cm.__enter__()
        self._ctx.append(cm)
        return v.ap() if hasattr(v, "ap") else v

    def pb(self, i, rows=128, cols=512):
        return self.PS[0:rows, 512 * i:512 * i + cols]

    def load_consts(self):
        p = self.p
        tl = []
        for dst, src in [(self.eye, "c_eye"), (self.swap, "c_swap"), (self.Es, "c_Es"),
                         (self.ones, "c_ones"), (self.ones_bf, "c_ones_bf"),
                         (self.orow_bf, "c_orow_bf"),
                         (self.hsel, "c_hsel"), (self.s2b, "c_s2b"),
                         (self.s2b_bf, "c_s2b_bf"), (self.rsel, "c_rsel")]:
            tl.append(p.dma(dst[:], self.dr[src][:]))
        self.t_consts = tuple(tl)
        p.dve.do(lambda e: e.memset(self.konst[:, 0:1], float(EPS)))
        p.dve.do(lambda e: e.memset(self.konst[:, 1:2], float(np.log(QK_SCALE))))
        p.dve.do(lambda e: e.memset(self.states_a[:], 0.0))
        self.t_k0 = p.dve.do(lambda e: e.memset(self.states_b[:], 0.0), inc=1)

    def row(self, tensor, slot, cols=slice(0, T), nrows=1):
        return tensor[32 * slot:32 * slot + nrows, cols]

    def emit_stats(self, src_hbm, mu_row, r_row, eps, n=1024.0, sig_row=None):
        p, A = self.p, self.AR
        A.push()
        xt = [A.alloc(512) for _ in range(2)]
        xb = [A.alloc_bf(512) for _ in range(2)]
        sq = [A.alloc_bf(512) for _ in range(2)]
        tr = A.alloc(1024)
        t_cons = {0: None, 1: None}
        prev_read = None
        for tc in range(4):
            cs = slice(512 * tc, 512 * (tc + 1))
            s0 = self.pb(6, 1); s1 = self.pb(7, 1)
            tok_mm = None
            for g in range(8):
                i = g % 2
                td = p.dma(xt[i][:], src_hbm[g * 128:(g + 1) * 128, cs], t_cons[i])
                tcast = p.pool.do(lambda e, i=i: e.tensor_copy(xb[i][:, 0:512], xt[i][:]), td, inc=1)
                tsq = p.act.do(lambda e, i=i: e.activation(sq[i][:, 0:512], xt[i][:], AF.Square),
                               td, inc=1)
                deps = [tcast, self.t_consts] + ([prev_read] if g == 0 else [])
                p.pe.do(lambda e, i=i, g=g: e.matmul(
                    s0, self.ones_bf[:, 0:1], xb[i][:, 0:512], start=(g == 0), stop=(g == 7)),
                    *deps)
                tok_mm = p.pe.do(lambda e, i=i, g=g: e.matmul(
                    s1, self.ones_bf[:, 0:1], sq[i][:, 0:512], start=(g == 0), stop=(g == 7)),
                    tsq, inc=1)
                t_cons[i] = tok_mm
            mu = mu_row[:, cs]; rr = r_row[:, cs]
            va = tr[0:1, 0:512]; vb = tr[0:1, 512:1024]
            p.dve.do(lambda e, mu=mu, s0=s0: e.tensor_scalar_mul(mu, s0, 1.0 / n), tok_mm)
            p.dve.do(lambda e, s1=s1: e.tensor_scalar_mul(va, s1, 1.0 / n))
            p.dve.do(lambda e, mu=mu: e.tensor_tensor(vb, mu, mu, op=ALU.mult))
            prev_read = p.dve.do(lambda e: e.tensor_sub(va, va, vb), inc=1)
            p.act.do(lambda e: e.activation(vb, va, AF.Ln, bias=self.konst[0:1, 0:1]),
                     prev_read, self.t_k0)
            ts_ = p.act.do(lambda e, rr=rr: e.activation(rr, vb, AF.Exp, scale=-0.5), inc=1)
            prev_read = (prev_read[0], prev_read[1])
            p.dve.do(lambda e: e.engine_nop(), ts_, inc=1)
        A.pop()
        return p.dve.tok()

    def bcast_row(self, bank, row_ap, deps=(), rows=128):
        p = self.p
        n = row_ap.shape[-1]
        np_ = row_ap.shape[0]
        hi = self.bc_rows[0:np_, 0:256].bitcast(BF16)[:, 0:n]
        lo = self.bc_rows[0:np_, 256:512].bitcast(BF16)[:, 0:n]
        tmp = self.bc_rows[0:np_, 512:512 + n]
        p.dve.do(lambda e: e.tensor_copy(tmp, row_ap),
                 *deps, *([self._t_bc] if self._t_bc else []))
        p.dve.do(lambda e: e.tensor_copy(hi, tmp))
        p.dve.do(lambda e: e.tensor_tensor(tmp, tmp, hi, op=ALU.subtract))
        t2 = p.dve.do(lambda e: e.tensor_copy(lo, tmp), inc=1)
        sel = self.orow_bf[0:1, 0:rows] if np_ == 1 else self.s2b_bf[0:2, 0:rows]
        p.pe.do(lambda e: e.matmul(self.pb(bank, rows, n), sel, hi,
                                   start=True, stop=False), t2, *deps, inc=0)
        tok = p.pe.do(lambda e: e.matmul(self.pb(bank, rows, n), sel, lo,
                                         start=False, stop=True), inc=1)
        self._t_bc = tok
        return tok

    def bank_write(self, i):
        return self._bank_r.get(i, [])

    def bank_written(self, i, tok):
        self._bank_w[i] = tok
        self._bank_r[i] = []

    def bank_read(self, i):
        return self._bank_w.get(i)

    def bank_did_read(self, i, tok):
        self._bank_r.setdefault(i, []).append(tok)

    def emit_attn_a(self):
        p, A = self.p, self.AR
        self._bank_r, self._bank_w = {}, {}
        A.push()
        self.aw_xp1 = A.alloc_bf(1280)[:, 0:1280]
        self.aw_xp2 = A.alloc_bf(256)[:, 0:256]
        self.aw_dt = A.alloc_bf(2048)[0:64, 0:2048]
        self.aw_dt_st = A.alloc(2048)
        self.aw_A = A.alloc(256)
        self.aw_At = A.alloc(256)
        xcf = [A.alloc_bf(518) for _ in range(NDI)]
        Bb = A.alloc_bf(16 * 512)
        Cb = A.alloc_bf(16 * 512)
        siluz = A.alloc_bf(16 * 512)
        gated = A.alloc_bf(16 * 512)
        NHR = 4
        hring = A.alloc(NHR * 512)
        dtqb = A.alloc_bf(512)
        brow = A.alloc_bf(512)
        crow = A.alloc_bf(512)
        xt2 = [A.alloc(512) for _ in range(2)]
        ot2 = [A.alloc(512) for _ in range(2)]
        SH = A.alloc(6144)
        xnq = SH[:, 0:2048].bitcast(BF16)
        wstage = SH[:, 2048:4096]
        wbf = [SH[:, 4096:5120].bitcast(BF16), SH[:, 5120:6144].bitcast(BF16)]
        delta = SH[:, 0:512]; du = SH[:, 512:1024]
        spA = SH[:, 1024:1536]; spB = SH[:, 1536:2048]; sp4 = SH[:, 2048:2560]
        g1b = SH[:, 2560:3072]
        dAb = [SH[:, 3072:3584], SH[:, 3584:4096], xt2[0], xt2[1]]
        xov = A.alloc(1024)
        du2 = A.alloc(1024)
        dX2 = [SH[:, 4096:5120], xov]
        ypb2 = [SH[:, 5120:5632].bitcast(BF16), SH[:, 5632:6144].bitcast(BF16)]
        wst2 = [wstage, A.alloc(2048)]
        cc0, cc1 = ot2[0], ot2[1]

        def xc(d):
            return xcf[d][:, 0:518]

        mu1 = self.rows_a[0:1, :]; r1 = self.rows_a[32:33, :]
        t_stats = self.emit_stats(self.dr["xT"], mu1, r1, EPS)
        self._bank_r[6] = [t_stats]
        self._bank_r[7] = [t_stats]

        t_xt2c = {0: None, 1: None}
        t_yp2c = {0: None, 1: None}
        t_dAc = {0: None, 1: None, 2: None, 3: None}
        t_dX2c = {0: None, 1: None}
        t_hr = {i: None for i in range(NHR)}
        t_hrp = {i: None for i in range(NHR)}
        t_wbfc = {0: None, 1: None}
        t_ot2c = {0: None, 1: None}
        t_sZ = {}
        t_gat = {}
        t_conv_done = {}

        for dir_ in ("f", "b"):
            fwd = dir_ == "f"
            sbase = 0 if fwd else 256
            tw = {}
            for dst, key in [(self.aw_dt_st[0:64, :], f"a{dir_}_dtW"), (self.aw_conv, f"a{dir_}_convW"),
                             (self.aw_cvB, f"a{dir_}_convB"), (self.aw_dtB, f"a{dir_}_dtB"),
                             (self.aw_D, f"a{dir_}_D"), (self.aw_At, f"a{dir_}_Alog")]:
                tw[key.split("_", 1)[1]] = p.dma(dst[:], self.dr[key][:])
            t1 = p.dma(wstage[:, 0:1280], self.dr[f"a{dir_}_xpW1"][:])
            t2 = p.dma(wstage[:, 1280:1536], self.dr[f"a{dir_}_xpW2"][:])
            tw["xpW1"] = p.pool.do(lambda e: e.tensor_copy(self.aw_xp1[:], wstage[:, 0:1280]),
                                   t1, inc=1)
            tw["xpW2"] = p.pool.do(lambda e: e.tensor_copy(self.aw_xp2[:], wstage[:, 1280:1536]),
                                   t2, inc=1)
            tw["dtW"] = p.pool.do(lambda e: e.tensor_copy(self.aw_dt[:], self.aw_dt_st[0:64, :]),
                                  tw["dtW"], inc=1)
            ta = p.act.do(lambda e: e.activation(self.aw_A[:], self.aw_At[:], AF.Exp),
                          tw["Alog"], inc=1)
            t_A = p.dve.do(lambda e: e.tensor_scalar_mul(self.aw_A[:], self.aw_A[:], -1.0),
                           ta, inc=1)
            tz = p.dve.do(lambda e: e.memset(self.tails[:], 0.0), inc=1)

            quarters = range(4) if fwd else range(3, -1, -1)
            for qi, q in enumerate(quarters):
                if (0 if fwd else 4) + qi >= AA_LIM:
                    break
                cs = slice(512 * q, 512 * (q + 1))
                tmu = self.bcast_row(6, mu1[:, cs], (self.t_consts, *self.bank_write(6)))
                self.bank_written(6, tmu)
                trr = self.bcast_row(7, r1[:, cs], tuple(self.bank_write(7)))
                self.bank_written(7, trr)
                t_xn = None
                for g in range(8):
                    i = g % 2
                    td = p.dma(xt2[i][:], self.dr["xT"][g * 128:(g + 1) * 128, cs], t_xt2c[i])
                    p.dve.do(lambda e, i=i: e.tensor_tensor(
                        xt2[i][:], xt2[i][:], self.pb(6), op=ALU.subtract),
                        td, tmu)
                    t_xn = p.dve.do(lambda e, g=g, i=i: e.tensor_tensor(
                        xnq[:, g * 512:(g + 1) * 512], xt2[i][:], self.pb(7), op=ALU.mult),
                        trr, inc=1)
                    t_xt2c[i] = t_xn
                self.bank_did_read(6, t_xn)
                self.bank_did_read(7, t_xn)

                if AA_PH < 1:
                    continue
                wst_war = {0: [tw["xpW1"], tw["xpW2"]], 1: None}
                tdw_pair = {}
                d_i = 0 if fwd else 1
                for og in range(32):
                    i = og % 2
                    bank = i
                    keya = (dir_, og)
                    if keya not in self.t_apre:
                        if og % 2 == 0:
                            pg = og // 2
                            sb_ = pg % 2
                            tdw_pair[sb_] = p.dma(
                                wst2[sb_][:, 0:2048].rearrange("p (a x) -> p a x", a=2),
                                self.dr[f"a{dir_}_inW"][og:og + 2]
                                    .transpose([1, 0, 2, 3])
                                    .rearrange("p a b c -> p a (b c)"),
                                wst_war[sb_])
                        sb_ = (og // 2) % 2
                        tcast = p.act.do(lambda e, i=i, sb_=sb_, half=og % 2: e.copy(
                            wbf[i][:, 0:1024],
                            wst2[sb_][:, half * 1024:half * 1024 + 1024]),
                            tdw_pair[sb_], t_wbfc[i],
                            getattr(self, "_t_app%d" % i, None), inc=1)
                        wst_war[sb_] = tcast
                        r0w = (d_i * 32 + og) * 128
                        tst_ = p.dma(self.wprea[r0w:r0w + 128, :],
                                     wbf[i][:, 0:1024], tcast)
                        self.t_apre[keya] = tst_
                        setattr(self, "_t_app%d" % i, tst_)
                    else:
                        r0w = (d_i * 32 + og) * 128
                        tcast = p.dma(wbf[i][:, 0:1024], self.wprea[r0w:r0w + 128, :],
                                      self.t_apre[keya], t_wbfc[i],
                                      getattr(self, "_t_app%d" % i, None))
                    tm = None
                    for kc in range(8):
                        def fmm(e, kc=kc, i=i, bank=bank):
                            return e.matmul(self.pb(bank), wbf[i][:, kc * 128:(kc + 1) * 128],
                                            xnq[:, kc * 512:(kc + 1) * 512],
                                            start=(kc == 0), stop=(kc == 7))
                        deps = [tcast, t_xn] + self.bank_write(bank) if kc == 0 else []
                        tm = p.pe.do(fmm, *deps, inc=(1 if kc == 7 else 0))
                    t_wbfc[i] = tm
                    self.bank_written(bank, tm)
                    if og < 16:
                        d = og
                        te = p.act.do(lambda e, d=d, bank=bank: e.copy(
                            xc(d)[:, 3:515], self.pb(bank)),
                            tm, t_conv_done.get(d), inc=1)
                        self.bank_did_read(bank, te)
                        if fwd:
                            tt1 = p.dve.do(lambda e, d=d: e.tensor_copy(
                                xc(d)[:, 0:3], self.tails[:, 3 * d:3 * d + 3].bitcast(BF16)[:, 0:3]),
                                te, tz, inc=1)
                            tt2 = p.dve.do(lambda e, d=d: e.tensor_copy(
                                self.tails[:, 3 * d:3 * d + 3].bitcast(BF16)[:, 0:3],
                                xc(d)[:, 512:515]), inc=1)
                        else:
                            tt1 = p.dve.do(lambda e, d=d: e.tensor_copy(
                                xc(d)[:, 515:518], self.tails[:, 3 * d:3 * d + 3].bitcast(BF16)[:, 0:3]),
                                te, tz, inc=1)
                            tt2 = p.dve.do(lambda e, d=d: e.tensor_copy(
                                self.tails[:, 3 * d:3 * d + 3].bitcast(BF16)[:, 0:3],
                                xc(d)[:, 3:6]), inc=1)
                        base = 0 if fwd else 3
                        tp = p.dve.do(lambda e, d=d, base=base: e.tensor_scalar_mul(
                            cc0[:], xc(d)[:, base:base + 512],
                            self.aw_conv[:, 4 * d:4 * d + 1]),
                            tt2, tw["convW"], t_ot2c[0], inc=1)
                        tp = p.dve.do(lambda e, d=d, base=base: e.scalar_tensor_tensor(
                            cc1[:], xc(d)[:, base + 1:base + 513],
                            self.aw_conv[:, 4 * d + 1:4 * d + 2], cc0[:],
                            op0=ALU.mult, op1=ALU.add), t_ot2c[1], inc=1)
                        tp = p.dve.do(lambda e, d=d, base=base: e.scalar_tensor_tensor(
                            cc0[:], xc(d)[:, base + 2:base + 514],
                            self.aw_conv[:, 4 * d + 2:4 * d + 3], cc1[:],
                            op0=ALU.mult, op1=ALU.add), inc=1)
                        tp = p.dve.do(lambda e, d=d, base=base: e.scalar_tensor_tensor(
                            cc1[:], xc(d)[:, base + 3:base + 515],
                            self.aw_conv[:, 4 * d + 3:4 * d + 4], cc0[:],
                            op0=ALU.mult, op1=ALU.add), inc=1)
                        t_ot2c[0] = tp
                        t_ot2c[1] = tp
                        tcv = p.act.do(lambda e, d=d: e.activation(
                            xc(d)[:, 3:515], cc1[:], AF.Silu,
                            bias=self.aw_cvB[:, d:d + 1]), tp, tw["convB"], inc=1)
                        t_conv_done[d] = tcv
                    else:
                        d = og - 16
                        te = p.act.do(lambda e, d=d, bank=bank: e.activation(
                            siluz[:, d * 512:(d + 1) * 512], self.pb(bank), AF.Silu),
                            tm, t_sZ.get(d), inc=1)
                        self.bank_did_read(bank, te)
                        t_sZ[d] = te
                t_inproj_w_done = [t_wbfc[0], t_wbfc[1], wst_war[0], wst_war[1],
                                   getattr(self, "_t_app0", None),
                                   getattr(self, "_t_app1", None)]

                if AA_PH < 2:
                    continue
                tx1 = tx2 = None
                for d in range(NDI):
                    deps1 = [t_conv_done[d], tw["xpW1"]] + (self.bank_write(2) if d == 0 else [])
                    tx1 = p.pe.do(lambda e, d=d: e.matmul(
                        self.pb(2, 80), self.aw_xp1[:, d * 80:(d + 1) * 80],
                        xc(d)[:, 3:515], start=(d == 0), stop=(d == NDI - 1)),
                        *deps1, inc=(1 if d == NDI - 1 else 0))
                    deps2 = [tw["xpW2"]] + (self.bank_write(3) if d == 0 else [])
                    tx2 = p.pe.do(lambda e, d=d: e.matmul(
                        self.pb(3, 16), self.aw_xp2[:, d * 16:(d + 1) * 16],
                        xc(d)[:, 3:515], start=(d == 0), stop=(d == NDI - 1)),
                        *deps2, inc=(1 if d == NDI - 1 else 0))
                self.bank_written(2, tx1)
                self.bank_written(3, tx2)
                tdq = p.act.do(lambda e: e.copy(dtqb[0:64, 0:512], self.pb(2, 80)[0:64, :]),
                               tx1, inc=1)
                tbr = p.act.do(lambda e: e.copy(brow[0:16, 0:512], self.pb(2, 80)[64:80, :]),
                               inc=1)
                self.bank_did_read(2, tbr)
                tcr = p.act.do(lambda e: e.copy(crow[0:16, 0:512], self.pb(3, 16)),
                               tx2, inc=1)
                self.bank_did_read(3, tcr)

                if AA_PH < 3:
                    continue
                t_bc = {}
                for which, rowt, tdep, dst in (("B", brow, tbr, Bb), ("C", crow, tcr, Cb)):
                    for s in range(16):
                        bank = 2 + (s % 2)
                        tb = p.pe.do(lambda e, s=s, bank=bank, rowt=rowt: e.matmul(
                            self.pb(bank), self.Es[:, s * 128:(s + 1) * 128],
                            rowt[0:16, 0:512], start=True, stop=True),
                            tdep, self.t_consts, *self.bank_write(bank), inc=1)
                        self.bank_written(bank, tb)
                        tb2 = p.act.do(lambda e, s=s, bank=bank, dst=dst: e.copy(
                            dst[:, s * 512:(s + 1) * 512], self.pb(bank)),
                            tb, t_bc.get((which, s)), inc=1)
                        self.bank_did_read(bank, tb2)
                        t_bc[(which, s)] = tb2
                t_Ball = p.act.tok()

                if AA_PH < 4:
                    continue
                t_dAc[2], t_dAc[3] = t_xt2c[0], t_xt2c[1]
                tglast = None
                for d in range(NDI):
                    if d >= AA_D:
                        continue
                    dd = ([tdq, tw["dtW"]] if d == 0 else []) + self.bank_write(4)
                    tdt = tr4 = trA = trC = tdl = tdu = None
                    if AA_S & 1:
                        tdt = p.pe.do(lambda e, d=d: e.matmul(
                            self.pb(4), self.aw_dt[:, d * 128:(d + 1) * 128], dtqb[0:64, 0:512],
                            start=True, stop=True), *dd, inc=1)
                        self.bank_written(4, tdt)
                    dcol = self.aw_dtB[:, d:d + 1]
                    aliased = t_inproj_w_done if d == 0 else []
                    if AA_S & 2:
                        tr4 = p.dve.do(lambda e, dcol=dcol: e.tensor_scalar(
                            sp4[:], self.pb(4), dcol, 0.0, op0=ALU.add, op1=ALU.max),
                            tdt, tw["dtB"], *aliased, inc=1)
                        self.bank_did_read(4, tr4)
                    if AA_S & 4:
                        trA = p.act.do(lambda e, dcol=dcol: e.activation(
                            spA[:], self.pb(4), AF.Abs, bias=dcol),
                            tdt, tr4, *aliased, inc=1)
                        self.bank_did_read(4, trA)
                        p.act.do(lambda e: e.activation(spB[:], spA[:], AF.Exp, scale=-1.0))
                        trC = p.act.do(lambda e: e.activation(spA[:], spB[:], AF.Ln, bias=1.0), inc=1)
                    if AA_S & 8:
                        tdl = p.dve.do(lambda e: e.tensor_add(delta[:], sp4[:], spA[:]),
                                       trC, tr4, inc=1)
                    if AA_S & 16:
                        p.pool.do(lambda e, d=d: e.tensor_tensor(
                            du2[:, 0:512], delta[:], xc(d)[:, 3:515], op=ALU.mult),
                            tdl, *aliased)
                        tdu = p.pool.do(lambda e, d=d: e.tensor_tensor(
                            du2[:, 512:1024], delta[:], xc(d)[:, 3:515], op=ALU.mult),
                            inc=1)
                    if AA_G < 1:
                        continue
                    for s in range(0, 16, 2):
                        i0, i1 = s % 4, (s + 1) % 4
                        ip = (s // 2) % 2
                        hi = s % NHR
                        tdx = p.pool.do(lambda e, s=s, ip=ip: e.tensor_tensor(
                            dX2[ip][:], du2[:], Bb[:, s * 512:(s + 2) * 512], op=ALU.mult),
                            tdu, t_Ball, t_dX2c[ip], inc=1)
                        acol0 = self.aw_A[:, d * 16 + s:d * 16 + s + 1]
                        acol1 = self.aw_A[:, d * 16 + s + 1:d * 16 + s + 2]
                        tda0 = p.act.do(lambda e, i0=i0, acol0=acol0: e.activation(
                            dAb[i0][:], delta[:], AF.Exp, scale=acol0),
                            tdl, t_A, t_dAc[i0], inc=1)
                        tda1 = p.act.do(lambda e, i1=i1, acol1=acol1: e.activation(
                            dAb[i1][:], delta[:], AF.Exp, scale=acol1),
                            t_dAc[i1], inc=1)
                        scol0 = self.states_a[:, sbase + d * 16 + s:sbase + d * 16 + s + 1]
                        scol1 = self.states_a[:, sbase + d * 16 + s + 1:sbase + d * 16 + s + 2]
                        if AA_G < 2:
                            continue
                        if fwd:
                            tsc0 = p.dve.do(lambda e, i0=i0, hi=hi, ip=ip, scol0=scol0:
                                            e.tensor_tensor_scan(
                                hring[:, hi * 512:(hi + 1) * 512], dAb[i0][:],
                                dX2[ip][:, 0:512], scol0, op0=ALU.mult, op1=ALU.add),
                                tda0, tdx, t_hr[hi], t_hrp[hi], inc=1)
                            tsc1 = p.dve.do(lambda e, i1=i1, hi=hi, ip=ip, scol1=scol1:
                                            e.tensor_tensor_scan(
                                hring[:, (hi + 1) * 512:(hi + 2) * 512], dAb[i1][:],
                                dX2[ip][:, 512:1024], scol1, op0=ALU.mult, op1=ALU.add),
                                tda1, t_hr[hi + 1], t_hrp[hi + 1], inc=1)
                        else:
                            tsc0 = p.dve.do(lambda e, i0=i0, hi=hi, ip=ip, scol0=scol0:
                                            e.tensor_tensor_scan(
                                hring[:, hi * 512:(hi + 1) * 512][:, ::-1],
                                dAb[i0][:, ::-1], dX2[ip][:, 0:512][:, ::-1],
                                scol0, op0=ALU.mult, op1=ALU.add),
                                tda0, tdx, t_hr[hi], t_hrp[hi], inc=1)
                            tsc1 = p.dve.do(lambda e, i1=i1, hi=hi, ip=ip, scol1=scol1:
                                            e.tensor_tensor_scan(
                                hring[:, (hi + 1) * 512:(hi + 2) * 512][:, ::-1],
                                dAb[i1][:, ::-1], dX2[ip][:, 512:1024][:, ::-1],
                                scol1, op0=ALU.mult, op1=ALU.add),
                                tda1, t_hr[hi + 1], t_hrp[hi + 1], inc=1)
                        t_dAc[i0] = tsc0
                        t_dAc[i1] = tsc1
                        t_dX2c[ip] = tsc1
                        if AA_G < 3:
                            continue
                        ypeng = p.pool if (s // 2) % 2 == 0 else p.dve
                        typ = ypeng.do(lambda e, s=s, ip=ip, hi=hi: e.tensor_tensor(
                            ypb2[ip][:], hring[:, hi * 512:(hi + 2) * 512],
                            Cb[:, s * 512:(s + 2) * 512], op=ALU.mult),
                            tsc0, tsc1, t_yp2c[ip], inc=1)
                        t_hrp[hi] = typ
                        t_hrp[hi + 1] = typ
                        p.pe.do(lambda e, ip=ip, s=s: e.matmul(
                            self.pb(5), self.eye[:], ypb2[ip][:, 0:512],
                            start=(s == 0), stop=False),
                            typ, *(self.bank_write(5) + [self.t_consts] if s == 0 else []))
                        tya = p.pe.do(lambda e, ip=ip, s=s: e.matmul(
                            self.pb(5), self.eye[:], ypb2[ip][:, 512:1024],
                            start=False, stop=(s == 14)), inc=1)
                        t_yp2c[ip] = tya
                        if AA_G < 4:
                            continue
                        s1 = s + 1
                        hi2 = hi + 1
                        if hi2 == NHR - 1 or s1 == 15:
                            n = hi2 + 1
                            off = 511 if fwd else 0
                            s0_ = s1 - hi2
                            def fsv(e, d=d, s0_=s0_, n=n, off=off):
                                return e.tensor_copy(
                                    self.states_a[:, sbase + d * 16 + s0_:sbase + d * 16 + s0_ + n],
                                    hring[:, off::512][:, 0:n])
                            tsv = p.dve.do(fsv, inc=1)
                            for hh in range(n):
                                t_hr[hh] = tsv
                    if AA_G < 5:
                        continue
                    self.bank_written(5, tya)
                    tg1 = p.dve.do(lambda e, d=d: e.scalar_tensor_tensor(
                        g1b[:], xc(d)[:, 3:515], self.aw_D[:, d:d + 1], self.pb(5),
                        op0=ALU.mult, op1=ALU.add),
                        tya, tw["D"], t_gat.get(d), inc=1)
                    self.bank_did_read(5, tg1)
                    tgt = p.pool.do(lambda e, d=d: e.tensor_tensor(
                        gated[:, d * 512:(d + 1) * 512], g1b[:], siluz[:, d * 512:(d + 1) * 512],
                        op=ALU.mult),
                        tg1, t_sZ.get(d), t_gat.get(d), inc=1)
                    t_gat[d] = tgt
                    tglast = tgt
                t_scan_end = [t for t in [p.dve.tok(), t_gat.get(15), p.act.tok(),
                                          t_yp2c.get(0), t_yp2c.get(1)] if t]

                if AA_PH < 5:
                    continue
                ost_war = {0: t_scan_end, 1: t_scan_end}
                d_i = 0 if fwd else 1
                for og in range(8):
                    i = og % 2
                    bank = i
                    keyo = (dir_, og)
                    r0w = (d_i * 8 + og) * 128
                    if keyo not in self.t_opre:
                        sb_ = og % 2
                        dd = ost_war[sb_]
                        if not isinstance(dd, list):
                            dd = [dd]
                        tdw = p.dma(wst2[sb_][:, 0:2048],
                                    self.dr[f"a{dir_}_outW"][og]
                                        .rearrange("p a b -> p (a b)"),
                                    *[x for x in dd if x])
                        tcast = p.act.do(lambda e, i=i, sb_=sb_: e.copy(
                            wbf[i][:, 0:2048], wst2[sb_][:, 0:2048]), tdw, t_wbfc[i],
                            getattr(self, "_t_opp%d" % i, None),
                            getattr(self, "_t_app%d" % i, None), inc=1)
                        ost_war[sb_] = tcast
                        tso = p.dma(self.wpreo[r0w:r0w + 128, :],
                                    wbf[i][:, 0:2048], tcast)
                        self.t_opre[keyo] = tso
                        setattr(self, "_t_opp%d" % i, tso)
                    else:
                        tcast = p.dma(wbf[i][:, 0:2048], self.wpreo[r0w:r0w + 128, :],
                                      self.t_opre[keyo], t_wbfc[i],
                                      getattr(self, "_t_opp%d" % i, None),
                                      getattr(self, "_t_app%d" % i, None),
                                      *([x for x in (t_scan_end if isinstance(
                                          t_scan_end, list) else [t_scan_end]) if x]
                                        if og < 2 else []))
                    tm = None
                    for kc in range(16):
                        def fmm(e, kc=kc, i=i, bank=bank):
                            return e.matmul(self.pb(bank), wbf[i][:, kc * 128:(kc + 1) * 128],
                                            gated[:, kc * 512:(kc + 1) * 512],
                                            start=(kc == 0), stop=(kc == 15))
                        deps = [tcast, t_gat.get(15)] + self.bank_write(bank) if kc == 0 else []
                        tm = p.pe.do(fmm, *deps, inc=(1 if kc == 15 else 0))
                    t_wbfc[i] = tm
                    self.bank_written(bank, tm)
                    rows = slice(og * 128, (og + 1) * 128)
                    if fwd:
                        tev = p.act.do(lambda e, i=i, bank=bank: e.copy(ot2[i][:], self.pb(bank)),
                                       tm, t_ot2c[i], inc=1)
                        self.bank_did_read(bank, tev)
                        tst = p.dma(self.hbm_f[rows, cs], ot2[i][:], tev)
                        t_ot2c[i] = tst
                    else:
                        tx = p.dma(xt2[i][:], self.dr["xT"][rows, cs], t_xt2c[i])
                        tf = p.dma(ot2[i][:], self.hbm_f[rows, cs], t_ot2c[i])
                        p.dve.do(lambda e, i=i, bank=bank: e.tensor_tensor(
                            xt2[i][:], xt2[i][:], self.pb(bank), op=ALU.add),
                            tx, tm)
                        ta2 = p.dve.do(lambda e, i=i: e.tensor_tensor(
                            xt2[i][:], xt2[i][:], ot2[i][:], op=ALU.add), tf, inc=1)
                        self.bank_did_read(bank, ta2)
                        tst = p.dma(self.hbm_s[rows, cs], xt2[i][:], ta2)
                        t_xt2c[i] = tst
                        t_ot2c[i] = ta2
                p.barrier()
        A.pop()

    def emit_hbm_copy(self, dst, src):
        p, A = self.p, self.AR
        A.push()
        buf = [A.alloc(2048) for _ in range(2)]
        tc_ = {0: None, 1: None}
        for g in range(8):
            i = g % 2
            rows = slice(g * 128, (g + 1) * 128)
            td = p.dma(buf[i][:], src[rows, :], tc_[i])
            ts_ = p.dma(dst[rows, :], buf[i][:], td)
            tc_[i] = ts_
        A.pop()

    def emit_dump(self, slot, src_tiles):
        p = self.p
        p.barrier()
        for g in range(8):
            p.dma(self.dbgT[slot * D + g * 128:slot * D + (g + 1) * 128, :],
                  src_tiles(g))
        p.barrier()

    def finish(self):
        self.p.barrier()


def build_nc(shared, upto):
    nc = bass.Bass("TRN2", target_bir_lowering=False,
                   detect_race_conditions=bool(int(os.environ.get("RACE", "0"))))
    b = Builder(nc, shared, upto)
    sems = {}
    cms = []
    for k in ("pe", "act", "dve", "pool", "sp",
              "d0", "d1", "d2", "d3", "d4", "d5", "d6", "d7"):
        cm = nc.semaphore("S_" + k)
        sems[k] = cm.__enter__()
        cms.append(cm)
    blk = nc.Block()
    block = blk.__enter__()
    b.load_consts()
    b.emit_attn_a()
    if FF_DBG & 8 and upto != "attn_a":
        b.p.barrier()
        b.emit_hbm_copy(b.dbgT[6 * D:7 * D, :], b.hbm_s)
        b.p.barrier()
    if upto == "attn_a":
        b.emit_hbm_copy(b.out, b.hbm_s)
    elif upto == "ffn1":
        b.emit_ffn(1, b.hbm_s, None)
    elif upto == "attn_b":
        b.emit_ffn(1, b.hbm_s, b.hbm_f)
        b.emit_attn_b(b.hbm_f, b.out)
    else:
        b.emit_ffn(1, b.hbm_s, b.hbm_f)
        b.emit_attn_b(b.hbm_f, b.hbm_s)
        b.emit_ffn(2, b.hbm_s, None)
    b.finish()
    b.p.emit(block, sems)
    blk.__exit__(None, None, None)
    for cm in cms:
        cm.__exit__(None, None, None)
    return b, nc


def kernel(**inputs):
    key = "full"
    shared = stage_shared(inputs)
    if key not in _built:
        b, nc = build_nc(shared, key)
        _built[key] = nc
    nc = _built[key]
    xs = np.asarray(inputs["L0_attn_a_x"], np.float32)
    in_maps = []
    for c in range(B):
        m = dict(shared)
        m["xT"] = np.ascontiguousarray(xs[c].T)
        in_maps.append(m)
    res = run_bass_kernel_spmd(nc, in_maps, core_ids=list(range(B)))
    out = np.stack([np.asarray(r["outT"]).T for r in res.results], axis=0)
    return out


def emit_ffn(self, app, src_hbm, dst_hbm):
    p, A = self.p, self.AR
    dst = dst_hbm if dst_hbm is not None else self.out
    self._bank_r, self._bank_w = {}, {}
    self._t_bc = None
    self._t_ftc = {0: None, 1: None}
    A.push()
    ft = A.alloc(16384)
    ftv = lambda g: ft[:, g * 2048:(g + 1) * 2048]
    tr1 = A.alloc(512); tr2 = A.alloc(512); tr3 = A.alloc(512)
    trb = A.alloc_bf(1024)
    rnt = A.alloc(512)
    rnt_bf = rnt.bitcast(BF16)
    mu2 = self.rows_a[0:1, :]
    r2_ = self.rows_a[32:33, :]
    t_stats = self.emit_stats(src_hbm, mu2, r2_, EPS)
    U = A.alloc(23824)
    Ub = U.bitcast(BF16)
    ftbf = lambda k: Ub[:, k * 1026:(k + 1) * 1026]
    qh = lambda g: Ub[:, 8208 + g * 2048:8208 + (g + 1) * 2048]
    vt = lambda t: Ub[:, 24592 + t * 512:24592 + (t + 1) * 512]
    vw = lambda k: Ub[:, 32784 + k * 512:32784 + (k + 1) * 512]
    rot = Ub[:, 36880:40976]
    wstqk = U[:, 20488:21512]
    wbfqk = [Ub[:, 43024:44048], Ub[:, 44048:45072]]
    OT = lambda g: Ub[:, g * 2048:(g + 1) * 2048]
    msk = lambda j: Ub[:, 32784 + j * 512:32784 + (j + 1) * 512]
    tsb = [U[:, 18440:18952], U[:, 18952:19464]]
    ptb = [Ub[:, 38928:39440], Ub[:, 39440:39952]]
    rbsb = U[0:64, 19976:20488]
    hs2n = ftbf
    w2st = U[:, 4104:6920]
    w2bf = [Ub[:, 13840:16656], Ub[:, 39184:42000]]
    h2 = lambda k: Ub[:, 16656 + k * 1024:16656 + (k + 1) * 1024]
    w1st = U[:, 19592:21640]
    w1bf = [Ub[:, 43280:45328], Ub[:, 45328:47376]]
    qn_slot = [(self.rows_a, 64), (self.rows_b, 0), (self.rows_b, 32), (self.rows_b, 64)]
    kn_slot = [(self.rows_c, 0), (self.rows_c, 32), (self.rows_c, 64), (self.rows_a, 0)]
    row = lambda sl, cs, n=1: sl[0][sl[1]:sl[1] + n, cs]

    self._bank_r[6] = [t_stats]
    self._bank_r[7] = [t_stats]
    for tc in range(4):
        cs = slice(512 * tc, 512 * (tc + 1))
        tmu = self.bcast_row(6, mu2[:, cs], tuple(self.bank_write(6)) + (t_stats,))
        self.bank_written(6, tmu)
        trr = self.bcast_row(7, r2_[:, cs], tuple(self.bank_write(7)))
        self.bank_written(7, trr)
        tl = None
        if not hasattr(self, "_t_ftc"):
            self._t_ftc = {0: None, 1: None}
        tc_c = self._t_ftc
        for g in range(8):
            i = g % 2
            src = tr1 if i == 0 else tr2
            td = p.dma(src[:], src_hbm[g * 128:(g + 1) * 128, cs], tc_c[i])
            p.dve.do(lambda e, src=src: e.tensor_tensor(src[:], src[:], self.pb(6),
                                                        op=ALU.subtract), td, tmu)
            tl = p.dve.do(lambda e, g=g, cs=cs, src=src: e.tensor_tensor(
                ftv(g)[:, cs], src[:], self.pb(7), op=ALU.mult), trr, inc=1)
            tc_c[i] = tl
        self.bank_did_read(6, tl)
        self.bank_did_read(7, tl)
    p.barrier()
    if FF_DBG & 1:
        self.emit_dump(0, ftv)
        p.dma(self.dbgT[7 * D:7 * D + 128, :], self.rows_a[0:128, :])
        p.barrier()

    for l in range(FF_L):
        pf = f"f{l}_"
        tqs = p.dma(self.fw_qs[:], self.dr[pf + "qs"][:])
        tks = p.dma(self.fw_ks[:], self.dr[pf + "ks"][:])
        tlg = p.dma(self.col16[:, 0:8], self.dr[pf + "lng"][:])
        tlb = p.dma(self.col16[:, 8:16], self.dr[pf + "lnb"][:])
        tb1 = (p.dma(self.aw_dtB[:, 0:16], self.dr[pf + "b1a"][:, 0:16]),
               p.dma(self.aw_D[:, 0:6], self.dr[pf + "b1a"][:, 16:22]),
               p.dma(self.aw_cvB[:, 0:16], self.dr[pf + "b1g"][:, 0:16]),
               p.dma(self.aw_conv[:, 0:6], self.dr[pf + "b1g"][:, 16:22]))
        tb2 = p.dma(self.col8b[:], self.dr[pf + "b2"][:])
        lnb_bf = self.col8c[:, 0:4].bitcast(BF16)[:, 0:8]
        tlnb = p.pool.do(lambda e: e.tensor_copy(lnb_bf, self.col16[:, 8:16]), tlb, inc=1)
        t_vw = None
        for kc in range(8):
            tdv = p.dma(w2st[:, 0:512], self.dr[pf + "vW"][kc], t_vw)
            t_vw = p.pool.do(lambda e, kc=kc: e.tensor_copy(vw(kc), w2st[:, 0:512]),
                             tdv, inc=1)
        t_vw_all = t_vw

        for h in range(2):
            hc = slice(1024 * h, 1024 * (h + 1))
            trot = tuple(
                p.dma(rot[:, tb * 1024:(tb + 1) * 1024],
                      self.dr["c_rot"][:, tb * 2048 + 1024 * h:
                                       tb * 2048 + 1024 * h + 1024])
                for tb in range(4))
            for kc in range(8):
                p.pool.do(lambda e, kc=kc, hc=hc: e.tensor_copy(
                    ftbf(kc)[:, 1:1025], ftv(kc)[:, hc]))
                if h == 0:
                    p.pool.do(lambda e, kc=kc: e.memset(ftbf(kc)[:, 0:1], 0.0),
                              inc=(1 if kc == 7 else 0))
                else:
                    p.pool.do(lambda e, kc=kc: e.tensor_copy(
                        ftbf(kc)[:, 0:1], ftv(kc)[:, 1023:1024]),
                        inc=(1 if kc == 7 else 0))
            t_r1_all = p.pool.tok()
            p.barrier()

            for og in range(8):
                i = og % 2
                keyq = (l, og)
                if keyq not in self.t_qkpre:
                    tdw = p.dma(wstqk[:],
                                self.dr[pf + "qkW"][og].rearrange("p a b -> p (a b)"),
                                getattr(self, "_t_wstqk", None))
                    tcw = p.pool.do(lambda e, i=i: e.tensor_copy(
                        wbfqk[i][:, 0:1024], wstqk[:]),
                        tdw, getattr(self, "_t_wbq%d" % i, None),
                        getattr(self, "_t_qkpp%d" % i, None), inc=1)
                    self._t_wstqk = tcw
                    r0w = (l * 8 + og) * 128
                    tsq_ = p.dma(self.wpreq[r0w:r0w + 128, :], wbfqk[i][:, 0:1024], tcw)
                    self.t_qkpre[keyq] = tsq_
                    setattr(self, "_t_qkpp%d" % i, tsq_)
                else:
                    r0w = (l * 8 + og) * 128
                    tcw = p.dma(wbfqk[i][:, 0:1024], self.wpreq[r0w:r0w + 128, :],
                                self.t_qkpre[keyq],
                                getattr(self, "_t_wbq%d" % i, None),
                                getattr(self, "_t_qkpp%d" % i, None))
                g = og % 4
                isq = og < 4
                for tcl in range(2):
                    tc = 2 * h + tcl
                    bank = tcl % 2
                    tm = None
                    for kc in range(8):
                        def fmm(e, kc=kc, i=i, bank=bank, tcl=tcl):
                            sh = 1 if kc < 4 else 0
                            return e.matmul(self.pb(bank),
                                            wbfqk[i][:, kc * 128:(kc + 1) * 128],
                                            ftbf(kc)[:, tcl * 512 + sh:tcl * 512 + sh + 512],
                                            start=(kc == 0), stop=(kc == 7))
                        deps = ([tcw, t_r1_all, trot] + self.bank_write(bank)) if kc == 0 else []
                        tm = p.pe.do(fmm, *deps, inc=(1 if kc == 7 else 0))
                    setattr(self, "_t_wbq%d" % i, tm)
                    self.bank_written(bank, tm)
                    tsq = p.act.do(lambda e, bank=bank: e.activation(
                        trb[:, 0:512], self.pb(bank), AF.Square), tm,
                        getattr(self, "_t_trb0", None), inc=1)
                    self.bank_did_read(bank, tsq)
                    tn = p.pe.do(lambda e: e.matmul(self.pb(2, 2), self.hsel[:],
                                                    trb[:, 0:512], start=True, stop=True),
                                 tsq, self.t_consts, *self.bank_write(2), inc=1)
                    self.bank_written(2, tn)
                    tmx = p.dve.do(lambda e: e.tensor_scalar(
                        rnt[0:2, 0:512], self.pb(2, 2), 1e-24, None, op0=ALU.max),
                        tn, getattr(self, "_t_rnt", None), inc=1)
                    self.bank_did_read(2, tmx)
                    p.act.do(lambda e: e.activation(rnt[0:2, 0:512], rnt[0:2, 0:512], AF.Ln),
                             tmx, self.t_k0)
                    if isq:
                        tre = p.act.do(lambda e: e.activation(
                            rnt[0:2, 0:512], rnt[0:2, 0:512], AF.Exp, scale=-0.5,
                            bias=self.konst[0:2, 1:2]), inc=1)
                    else:
                        tre = p.act.do(lambda e: e.activation(
                            rnt[0:2, 0:512], rnt[0:2, 0:512], AF.Exp, scale=-0.5), inc=1)
                    trn = self.bcast_row(3, rnt[0:2, 0:512],
                                         tuple(self.bank_write(3)) + (tre,))
                    self.bank_written(3, trn)
                    self._t_rnt = trn
                    tev = p.act.do(lambda e: e.copy(tr1[:], self.pb(3)), trn,
                                   getattr(self, "_t_tr1f", None), inc=1)
                    self.bank_did_read(3, tev)
                    qs_col = self.fw_qs[:, 0:1] if isq else self.fw_ks[:, 0:1]
                    tqh = p.dve.do(lambda e, bank=bank, qs_col=qs_col: e.scalar_tensor_tensor(
                        trb[:, 512:1024], self.pb(bank), qs_col, tr1[:],
                        op0=ALU.mult, op1=ALU.mult),
                        tev, tqs if isq else tks, tsq,
                        getattr(self, "_t_trb1", None), inc=1)
                    self.bank_did_read(bank, tqh)
                    self._t_tr1f = tqh
                    self._t_trb0 = tqh
                    tsw = p.pe.do(lambda e: e.matmul(self.pb(4), self.swap[:],
                                                     trb[:, 512:1024], start=True, stop=True),
                                  tqh, *self.bank_write(4), inc=1)
                    self.bank_written(4, tsw)
                    self._t_trb1 = tsw
                    tboff = 0 if isq else 2048
                    cs_t = rot[:, tboff + tcl * 512:tboff + tcl * 512 + 512]
                    ss_t = rot[:, tboff + 1024 + tcl * 512:tboff + 1024 + tcl * 512 + 512]
                    tt1 = p.pool.do(lambda e, cs_t=cs_t: e.tensor_tensor(
                        tr2[:], trb[:, 512:1024], cs_t, op=ALU.mult),
                        tqh, trot, getattr(self, "_t_tr2f", None), inc=1)
                    tt2 = p.dve.do(lambda e, ss_t=ss_t: e.tensor_tensor(
                        tr3[:], self.pb(4), ss_t, op=ALU.mult),
                        tsw, getattr(self, "_t_tr3f", None), inc=1)
                    self.bank_did_read(4, tt2)
                    gq = g if isq else 4 + g
                    tqp = p.dve.do(lambda e, gq=gq, tc=tc: e.tensor_tensor(
                        qh(gq)[:, tc * 512:(tc + 1) * 512], tr2[:], tr3[:], op=ALU.add),
                        tt1, getattr(self, "_t_qh%d_%d" % (gq, tc), None), inc=1)
                    setattr(self, "_t_qh%d_%d" % (gq, tc), tqp)
                    self._t_tr2f = tqp
                    self._t_tr3f = tqp
                    tsq2 = p.act.do(lambda e: e.activation(
                        trb[:, 0:512], trb[:, 512:1024], AF.Square),
                        tqh, inc=1)
                    self._t_trb0 = tsq2
                    self._t_trb1 = (tsw, tsq2)
                    tn2 = p.pe.do(lambda e: e.matmul(self.pb(2, 2), self.hsel[:],
                                                     trb[:, 0:512], start=True, stop=True),
                                  tsq2, *self.bank_write(2), inc=1)
                    self.bank_written(2, tn2)
                    slot = (qn_slot if isq else kn_slot)[g]
                    dstrow = row(slot, slice(tc * 512, (tc + 1) * 512), 2)
                    p.act.do(lambda e, dstrow=dstrow: e.activation(
                        dstrow, self.pb(2, 2), AF.Ln), tn2)
                    tnr = p.act.do(lambda e, dstrow=dstrow: e.activation(
                        dstrow, dstrow, AF.Exp, scale=0.5), inc=1)
                    self.bank_did_read(2, tnr)
            for tt in range(8):
                gt = 8 * h + tt
                tm = None
                for kc in range(8):
                    def fvm(e, kc=kc, tt=tt):
                        sh = 1 if kc < 4 else 0
                        return e.matmul(self.pb(5),
                                        ftbf(kc)[:, tt * 128 + sh:tt * 128 + sh + 128],
                                        vw(kc), start=(kc == 0), stop=(kc == 7))
                    deps = ([t_vw_all, t_r1_all] + self.bank_write(5)) if kc == 0 else []
                    tm = p.pe.do(fvm, *deps, inc=(1 if kc == 7 else 0))
                self.bank_written(5, tm)
                tev = p.act.do(lambda e, gt=gt: e.copy(vt(gt), self.pb(5)), tm, inc=1)
                self.bank_did_read(5, tev)
            p.barrier()

        if FF_PH < 2:
            continue
        self._bank_r, self._bank_w = {}, {}
        tmk = tuple(p.dma(msk(j), self.dr["c_masks"][:, j * 512:(j + 1) * 512])
                    for j in range(8))
        p.barrier()
        for g in range(4):
            for w in range(4):
                q0 = 512 * w
                j0 = 4 if w == 0 else 0
                kwin = slice(max(512 * (w - 1), 0), q0 + 512)
                knr2 = row(kn_slot[g], kwin, 2)
                qnr2 = row(qn_slot[g], slice(q0, q0 + 512), 2)
                negb2 = self.negb_t[0:2, :]
                tmx = p.dve.do(lambda e, knr2=knr2: e.tensor_reduce(
                    rnt[0:2, 0:1], knr2, axis=mybir.AxisListType.X, op=ALU.max),
                    getattr(self, "_t_negb", None), inc=1)
                p.dve.do(lambda e, qnr2=qnr2: e.tensor_copy(tr3[0:2, 0:512], qnr2))
                p.dve.do(lambda e: e.tensor_scalar(
                    tr3[0:2, 0:512], tr3[0:2, 0:512], rnt[0:2, 0:1], -1.0,
                    op0=ALU.mult, op1=ALU.mult))
                tnb = p.dve.do(lambda e, negb2=negb2: e.tensor_copy(
                    negb2, tr3[0:2, 0:512]), inc=1)
                for sub in range(2):
                    hd = 2 * g + sub
                    r0 = 64 * sub
                    tav = tdn = None
                    js = list(range(j0, 8))
                    QB = (0, 1, 5, 6)
                    tqk_l = {}
                    for idx in range(len(js) + 2):
                        if idx < len(js):
                            j = js[idx]
                            bank = QB[idx % 4]
                            p.pe.do(lambda e, j=j, bank=bank, g=g, r0=r0, q0=q0,
                                    w=w: e.matmul(
                                self.pb(bank),
                                qh(4 + g)[r0:r0 + 64,
                                          512 * (w - 1) + j * 128:
                                          512 * (w - 1) + j * 128 + 128],
                                qh(g)[r0:r0 + 64, q0:q0 + 512],
                                start=True, stop=False),
                                *self.bank_write(bank), inc=0)
                            tqk = p.pe.do(lambda e, bank=bank, sub=sub, negb2=negb2:
                                          e.matmul(
                                self.pb(bank), self.rsel[:, sub * 128:(sub + 1) * 128],
                                negb2, start=False, stop=True), tnb, inc=1)
                            self.bank_written(bank, tqk)
                            tqk_l[idx] = tqk
                        if idx >= 2:
                            i2 = idx - 2
                            j = js[i2]
                            bank = QB[i2 % 4]
                            kt = 4 * (w - 1) + j
                            ii = i2 % 2
                            tst = p.dve.do(lambda e, bank=bank, j=j, ii=ii:
                                           e.scalar_tensor_tensor(
                                tsb[ii][:], self.pb(bank), -84.0, msk(j),
                                op0=ALU.max, op1=ALU.add),
                                tqk_l[i2], tmk,
                                getattr(self, "_t_tsb%d" % ii, None), inc=1)
                            self.bank_did_read(bank, tst)
                            tex = p.act.do(lambda e, ii=ii: e.activation(
                                ptb[ii][:], tsb[ii][:], AF.Exp),
                                tst, getattr(self, "_t_pt%d" % ii, None), inc=1)
                            setattr(self, "_t_tsb%d" % ii, tex)
                            tav = p.pe.do(lambda e, kt=kt, hd=hd, ii=ii, i2=i2,
                                          lj=len(js): e.matmul(
                                self.pb(2, 64), vt(kt)[:, hd * 64:(hd + 1) * 64],
                                ptb[ii][:],
                                start=(i2 == 0), stop=(i2 == lj - 1)),
                                tex, *(self.bank_write(2) if i2 == 0 else []), inc=1)
                            tdn = p.pe.do(lambda e, ii=ii, i2=i2, lj=len(js): e.matmul(
                                self.pb(3, 1), self.ones_bf[:, 0:1], ptb[ii][:],
                                start=(i2 == 0), stop=(i2 == lj - 1)),
                                *(self.bank_write(3) if i2 == 0 else []), inc=1)
                            setattr(self, "_t_pt%d" % ii, tdn)
                    self.bank_written(2, tav)
                    self.bank_written(3, tdn)
                    trc = p.dve.do(lambda e: e.reciprocal(rnt[0:1, 0:512], self.pb(3, 1)),
                                   tdn, tmx, inc=1)
                    self.bank_did_read(3, trc)
                    trb_ = self.bcast_row(4, rnt[0:1, 0:512],
                                          tuple(self.bank_write(4)) + (trc,), rows=64)
                    self.bank_written(4, trb_)
                    tevr = p.act.do(lambda e: e.copy(rbsb[:, 0:512], self.pb(4, 64)), trb_,
                                    getattr(self, "_t_rbsb", None), inc=1)
                    self.bank_did_read(4, tevr)
                    tot = p.dve.do(lambda e, g=g, r0=r0, q0=q0: e.tensor_tensor(
                        OT(g)[r0:r0 + 64, q0:q0 + 512], self.pb(2, 64), rbsb[:, 0:512],
                        op=ALU.mult),
                        tav, tevr, getattr(self, "_t_OT%d" % g, None), inc=1)
                    self.bank_did_read(2, tot)
                    self._t_rbsb = tot
                    setattr(self, "_t_OT%d" % g, tot)
                self._t_negb = p.pe.tok()
        p.barrier()

        if FF_PH < 3:
            continue
        self._bank_r, self._bank_w = {}, {}
        for og in range(8):
            i = og % 2
            tdw = p.dma(wstqk[:, 0:512], self.dr[pf + "woW"][og].rearrange("p a b -> p (a b)"),
                        getattr(self, "_t_wstqk", None))
            tcw = p.pool.do(lambda e, i=i: e.tensor_copy(wbfqk[i][:, 0:512], wstqk[:, 0:512]),
                            tdw, getattr(self, "_t_wbq%d" % i, None), inc=1)
            self._t_wstqk = tcw
            for tc in range(4):
                bank = tc % 2
                tm = None
                for kc in range(4):
                    def fwo(e, kc=kc, i=i, bank=bank, tc=tc):
                        return e.matmul(self.pb(bank), wbfqk[i][:, kc * 128:(kc + 1) * 128],
                                        OT(kc)[:, tc * 512:(tc + 1) * 512],
                                        start=(kc == 0), stop=(kc == 3))
                    deps = ([tcw] + self.bank_write(bank)) if kc == 0 else []
                    tm = p.pe.do(fwo, *deps, inc=(1 if kc == 3 else 0))
                setattr(self, "_t_wbq%d" % i, tm)
                self.bank_written(bank, tm)
                tup = p.dve.do(lambda e, og=og, tc=tc, bank=bank: e.tensor_tensor(
                    ftv(og)[:, tc * 512:(tc + 1) * 512],
                    ftv(og)[:, tc * 512:(tc + 1) * 512], self.pb(bank), op=ALU.add),
                    tm, inc=1)
                self.bank_did_read(bank, tup)
        p.barrier()

        if FF_DBG & 2:
            self.emit_dump(1 + 3 * l, ftv)
        if FF_PH < 4:
            continue
        self._bank_r, self._bank_w = {}, {}
        s_lo = self.rows_a[0:1, :]; s_hi = self.rows_a[32:33, :]
        q_lo = self.rows_b[0:1, :]; q_hi = self.rows_b[32:33, :]
        mus = self.rows_a[64:65, :]; rs_ = self.rows_b[64:65, :]
        for tc in range(4):
            cs = slice(512 * tc, 512 * (tc + 1))
            tml = None
            for og in range(8):
                sb_ = 0 if og < 4 else 1
                qb_ = 2 if og < 4 else 3
                tcp = p.pool.do(lambda e, og=og, cs=cs: e.tensor_copy(
                    trb[:, 0:512], ftv(og)[:, cs]), getattr(self, "_t_trbW", None), inc=1)
                tsq = p.act.do(lambda e, og=og, cs=cs: e.activation(
                    trb[:, 512:1024], ftv(og)[:, cs], AF.Square),
                    getattr(self, "_t_trbW2", None), inc=1)
                tm1 = p.pe.do(lambda e, sb_=sb_, og=og: e.matmul(
                    self.pb(sb_, 1), self.ones_bf[:, 0:1], trb[:, 0:512],
                    start=(og % 4 == 0), stop=(og % 4 == 3)),
                    tcp, *(self.bank_write(sb_) if og % 4 == 0 else []), inc=1)
                self._t_trbW = tm1
                tm2 = p.pe.do(lambda e, qb_=qb_, og=og: e.matmul(
                    self.pb(qb_, 1), self.ones_bf[:, 0:1], trb[:, 512:1024],
                    start=(og % 4 == 0), stop=(og % 4 == 3)),
                    tsq, *(self.bank_write(qb_) if og % 4 == 0 else []), inc=1)
                self._t_trbW2 = tm2
                if og % 4 == 3:
                    self.bank_written(sb_, tm1)
                    self.bank_written(qb_, tm2)
                tml = tm2
            for bnk, dstr in ((0, s_lo), (1, s_hi), (2, q_lo), (3, q_hi)):
                tev = p.dve.do(lambda e, bnk=bnk, dstr=dstr, cs=cs: e.tensor_copy(
                    dstr[:, cs], self.pb(bnk, 1)), tml, inc=1)
                self.bank_did_read(bnk, tev)
        p.barrier()
        tmpA = self.bc_rows[0:1, 512:1024]
        tmpB = rnt[0:1, 0:512]
        for tc in range(4):
            cs = slice(512 * tc, 512 * (tc + 1))
            if tc == 0:
                p.dve.do(lambda e: e.memset(tmpA[:, 0:1], 0.0))
                p.dve.do(lambda e: e.tensor_copy(tmpA[:, 1:512], s_hi[:, 0:511]))
            else:
                p.dve.do(lambda e, tc=tc: e.tensor_copy(
                    tmpA[:], s_hi[:, tc * 512 - 1:(tc + 1) * 512 - 1]))
            p.dve.do(lambda e, cs=cs: e.tensor_tensor(tmpA[:], tmpA[:], s_lo[:, cs],
                                                      op=ALU.add))
            p.dve.do(lambda e, cs=cs: e.tensor_scalar_mul(mus[:, cs], tmpA[:], 1.0 / 1024))
            if tc == 0:
                p.dve.do(lambda e: e.memset(tmpA[:, 0:1], 0.0))
                p.dve.do(lambda e: e.tensor_copy(tmpA[:, 1:512], q_hi[:, 0:511]))
            else:
                p.dve.do(lambda e, tc=tc: e.tensor_copy(
                    tmpA[:], q_hi[:, tc * 512 - 1:(tc + 1) * 512 - 1]))
            p.dve.do(lambda e, cs=cs: e.tensor_tensor(tmpA[:], tmpA[:], q_lo[:, cs],
                                                      op=ALU.add))
            p.dve.do(lambda e: e.tensor_scalar_mul(tmpA[:], tmpA[:], 1.0 / 1024))
            p.dve.do(lambda e, cs=cs: e.tensor_tensor(tmpB[:], mus[:, cs], mus[:, cs],
                                                      op=ALU.mult))
            tva = p.dve.do(lambda e: e.tensor_tensor(tmpA[:], tmpA[:], tmpB[:],
                                                     op=ALU.subtract), inc=1)
            p.act.do(lambda e: e.activation(tmpA[:], tmpA[:], AF.Ln,
                                            bias=self.konst[0:1, 0:1]), tva, self.t_k0)
            tse = p.act.do(lambda e, cs=cs: e.activation(rs_[:, cs], tmpA[:],
                                                         AF.Exp, scale=-0.5), inc=1)
            p.dve.do(lambda e: e.engine_nop(), tse, inc=1)
        p.barrier()

        if FF_PH < 5:
            continue
        for kc in range(4, 8):
            p.dve.do(lambda e, kc=kc: e.tensor_copy(
                self.col8d[:, kc - 4:kc - 3], ftv(kc)[:, 1023:1024]))
        for h in range(2):
            for tcl in range(2):
                tc = 2 * h + tcl
                cs = slice(512 * tc, 512 * (tc + 1))
                tmu = self.bcast_row(6, mus[:, cs], tuple(self.bank_write(6)))
                self.bank_written(6, tmu)
                trr = self.bcast_row(7, rs_[:, cs], tuple(self.bank_write(7)))
                self.bank_written(7, trr)
                tl = None
                for kc in range(8):
                    sh = 0 if kc < 4 else 1
                    a0 = 512 * tc - sh
                    gcol = self.col16[:, kc:kc + 1]
                    if h == 1 and tcl == 0 and kc >= 4:
                        def f1(e, kc=kc):
                            return e.tensor_tensor(tr1[:, 1:512], ftv(kc)[:, 1024:1535],
                                                   self.pb(6)[:, 1:512], op=ALU.subtract)
                        p.dve.do(f1, tmu, getattr(self, "_t_tr1W", None))
                        tl = p.dve.do(lambda e, kc=kc, tcl=tcl, gcol=gcol: e.scalar_tensor_tensor(
                            hs2n(kc)[:, tcl * 512 + 1:tcl * 512 + 512], tr1[:, 1:512], gcol,
                            self.pb(7)[:, 1:512], op0=ALU.mult, op1=ALU.mult),
                            trr, tlg, getattr(self, "_t_hs%d" % kc, None), inc=1)
                        p.dve.do(lambda e, kc=kc: e.tensor_tensor(
                            tr1[:, 0:1], self.col8d[:, kc - 4:kc - 3], self.pb(6)[:, 0:1],
                            op=ALU.subtract))
                        tl = p.dve.do(lambda e, kc=kc, tcl=tcl, gcol=gcol: e.scalar_tensor_tensor(
                            hs2n(kc)[:, tcl * 512:tcl * 512 + 1], tr1[:, 0:1], gcol,
                            self.pb(7)[:, 0:1], op0=ALU.mult, op1=ALU.mult), inc=1)
                        setattr(self, "_t_hs%d" % kc, tl)
                        self._t_tr1W = tl
                        continue
                    if a0 < 0:
                        def f1(e, kc=kc):
                            return e.tensor_tensor(tr1[:, 1:512], ftv(kc)[:, 0:511],
                                                   self.pb(6)[:, 1:512], op=ALU.subtract)
                        p.dve.do(f1, tmu, getattr(self, "_t_tr1W", None))
                        tl = p.dve.do(lambda e, kc=kc, tcl=tcl, gcol=gcol: e.scalar_tensor_tensor(
                            hs2n(kc)[:, tcl * 512 + 1:tcl * 512 + 512], tr1[:, 1:512], gcol,
                            self.pb(7)[:, 1:512], op0=ALU.mult, op1=ALU.mult),
                            trr, tlg, getattr(self, "_t_hs%d" % kc, None), inc=1)
                        p.dve.do(lambda e: e.tensor_copy(tr2[:, 0:1], self.pb(6)[:, 0:1]),
                                 tmu, getattr(self, "_t_tr2W", None))
                        ta2 = p.dve.do(lambda e: e.tensor_copy(tr2[:, 1:2], self.pb(7)[:, 0:1]),
                                       trr, inc=1)
                        p.dve.do(lambda e: e.tensor_tensor(tr2[:, 2:3], tr2[:, 0:1],
                                                           tr2[:, 1:2], op=ALU.mult), ta2)
                        p.dve.do(lambda e: e.tensor_scalar_mul(tr2[:, 3:4], tr2[:, 2:3], -1.0))
                        tl2 = p.dve.do(lambda e, kc=kc, tcl=tcl, gcol=gcol: e.tensor_scalar(
                            hs2n(kc)[:, tcl * 512:tcl * 512 + 1], tr2[:, 3:4], gcol, None,
                            op0=ALU.mult), tlg, inc=1)
                        self._t_tr2W = tl2
                        tl = tl2
                    else:
                        def f1(e, kc=kc, a0=a0):
                            return e.tensor_tensor(tr1[:], ftv(kc)[:, a0:a0 + 512],
                                                   self.pb(6), op=ALU.subtract)
                        p.dve.do(f1, tmu, getattr(self, "_t_tr1W", None))
                        tl = p.dve.do(lambda e, kc=kc, tcl=tcl, gcol=gcol: e.scalar_tensor_tensor(
                            hs2n(kc)[:, tcl * 512:(tcl + 1) * 512], tr1[:], gcol,
                            self.pb(7), op0=ALU.mult, op1=ALU.mult),
                            trr, tlg, getattr(self, "_t_hs%d" % kc, None), inc=1)
                    setattr(self, "_t_hs%d" % kc, tl)
                    self._t_tr1W = tl
                self.bank_did_read(6, tl)
                self.bank_did_read(7, tl)
            p.barrier()
            self._bank_r, self._bank_w = {}, {}
            for og in range(NOGH):
                i = og % 2
                key1 = (l, og)
                if key1 not in self.t_w1pre:
                    tda = p.dma(w1st[:, 0:1024],
                                self.dr[pf + "w1a"][og].rearrange("p a b -> p (a b)"),
                                getattr(self, "_t_w1st", None))
                    tdg = p.dma(w1st[:, 1024:2048],
                                self.dr[pf + "w1g"][og].rearrange("p a b -> p (a b)"))
                    tca = p.act.do(lambda e, i=i: e.copy(w1bf[i][:, 0:1024],
                                                         w1st[:, 0:1024]),
                                   tda, getattr(self, "_t_w1b%d" % i, None),
                                   getattr(self, "_t_w1pp%d" % i, None), inc=1)
                    tcg = p.act.do(lambda e, i=i: e.copy(w1bf[i][:, 1024:2048],
                                                         w1st[:, 1024:2048]), tdg, inc=1)
                    self._t_w1st = tcg
                    r0w = (l * NOGH + og) * 128
                    tsa = p.dma(self.wpre1[r0w:r0w + 128, :], w1bf[i][:], tca, tcg)
                    self.t_w1pre[key1] = tsa
                    setattr(self, "_t_w1pp%d" % i, tsa)
                else:
                    r0w = (l * NOGH + og) * 128
                    tld = p.dma(w1bf[i][:], self.wpre1[r0w:r0w + 128, :],
                                self.t_w1pre[key1],
                                getattr(self, "_t_w1b%d" % i, None),
                                getattr(self, "_t_w1pp%d" % i, None))
                    tca = tcg = tld
                tcbA = tcbG = None
                for kc in range(8):
                    da = ([tca, tlnb] + self.bank_write(4)) if kc == 0 else []
                    tcbA = p.pe.do(lambda e, kc=kc, i=i: e.matmul(
                        self.pb(4, 128, 1), w1bf[i][:, kc * 128:(kc + 1) * 128],
                        lnb_bf[:, kc:kc + 1], start=(kc == 0), stop=(kc == 7)),
                        *da, inc=(1 if kc == 7 else 0))
                    dg = ([tcg] + self.bank_write(5)) if kc == 0 else []
                    tcbG = p.pe.do(lambda e, kc=kc, i=i: e.matmul(
                        self.pb(5, 128, 1), w1bf[i][:, 1024 + kc * 128:1024 + (kc + 1) * 128],
                        lnb_bf[:, kc:kc + 1], start=(kc == 0), stop=(kc == 7)),
                        *dg, inc=(1 if kc == 7 else 0))
                self.bank_written(4, tcbA)
                self.bank_written(5, tcbG)
                b1a_col = self.aw_dtB[:, og:og + 1] if og < 16 else self.aw_D[:, og - 16:og - 15]
                b1g_col = self.aw_cvB[:, og:og + 1] if og < 16 else self.aw_conv[:, og - 16:og - 15]
                tca2 = p.dve.do(lambda e, b1a_col=b1a_col: e.tensor_scalar(
                    self.col8a[:, 0:1], self.pb(4, 128, 1), b1a_col, None, op0=ALU.add),
                    tcbA, tb1, getattr(self, "_t_cbA", None), inc=1)
                self.bank_did_read(4, tca2)
                tcg2 = p.dve.do(lambda e, b1g_col=b1g_col: e.tensor_scalar(
                    self.col8a[:, 1:2], self.pb(5, 128, 1), b1g_col, None, op0=ALU.add),
                    tcbG, getattr(self, "_t_cbG", None), inc=1)
                self.bank_did_read(5, tcg2)
                self._t_cbA = tca2
                self._t_cbG = tcg2
                for tcl in range(2):
                    tc = 2 * h + tcl
                    bA = tcl % 2
                    bG = 2 + tcl % 2
                    tmA = tmG = None
                    for kc in range(8):
                        dA_ = ([tca] + self.bank_write(bA)) if kc == 0 else []
                        tmA = p.pe.do(lambda e, kc=kc, i=i, bA=bA, tcl=tcl: e.matmul(
                            self.pb(bA), w1bf[i][:, kc * 128:(kc + 1) * 128],
                            hs2n(kc)[:, tcl * 512:(tcl + 1) * 512],
                            start=(kc == 0), stop=(kc == 7)),
                            *dA_, inc=(1 if kc == 7 else 0))
                        dG_ = ([tcg] + self.bank_write(bG)) if kc == 0 else []
                        tmG = p.pe.do(lambda e, kc=kc, i=i, bG=bG, tcl=tcl: e.matmul(
                            self.pb(bG), w1bf[i][:, 1024 + kc * 128:1024 + (kc + 1) * 128],
                            hs2n(kc)[:, tcl * 512:(tcl + 1) * 512],
                            start=(kc == 0), stop=(kc == 7)),
                            *dG_, inc=(1 if kc == 7 else 0))
                    self.bank_written(bA, tmA)
                    self.bank_written(bG, tmG)
                    tgg = p.act.do(lambda e, bG=bG: e.activation(
                        trb[:, 0:512], self.pb(bG), AF.Gelu, bias=self.col8a[:, 1:2]),
                        tmG, tcg2, getattr(self, "_t_ggW", None), inc=1)
                    self.bank_did_read(bG, tgg)
                    th2 = p.dve.do(lambda e, og=og, tcl=tcl, bA=bA: e.scalar_tensor_tensor(
                        h2(og)[:, tcl * 512:(tcl + 1) * 512], self.pb(bA),
                        self.col8a[:, 0:1], trb[:, 0:512], op0=ALU.add, op1=ALU.mult),
                        tmA, tca2, tgg, getattr(self, "_t_h2%d" % og, None), inc=1)
                    self.bank_did_read(bA, th2)
                    self._t_ggW = th2
                    setattr(self, "_t_h2%d" % og, th2)
                setattr(self, "_t_w1b%d" % i, (tmA, tmG, tcbA, tcbG))
            p.barrier()
            self._bank_r, self._bank_w = {}, {}
            for og in range(8):
                i = og % 2
                key2 = (l, og)
                if key2 not in self.t_w2pre:
                    tdw = p.dma(w2st[:],
                                self.dr[pf + "w2W"][og].rearrange("p a b -> p (a b)"),
                                getattr(self, "_t_w2st", None))
                    tcw = p.act.do(lambda e, i=i: e.copy(w2bf[i][:], w2st[:]),
                                   tdw, getattr(self, "_t_w2b%d" % i, None),
                                   getattr(self, "_t_w2pp%d" % i, None), inc=1)
                    self._t_w2st = tcw
                    r0w = (l * 8 + og) * 128
                    tsw = p.dma(self.wpre2[r0w:r0w + 128, :], w2bf[i][:], tcw)
                    self.t_w2pre[key2] = tsw
                    setattr(self, "_t_w2pp%d" % i, tsw)
                else:
                    r0w = (l * 8 + og) * 128
                    tcw = p.dma(w2bf[i][:], self.wpre2[r0w:r0w + 128, :],
                                self.t_w2pre[key2],
                                getattr(self, "_t_w2b%d" % i, None),
                                getattr(self, "_t_w2pp%d" % i, None))
                for tcl in range(2):
                    tc = 2 * h + tcl
                    bank = 4 + tcl % 2
                    tm = None
                    for kc in range(NOGH):
                        dd = ([tcw] + self.bank_write(bank)) if kc == 0 else []
                        tm = p.pe.do(lambda e, kc=kc, i=i, bank=bank, tcl=tcl: e.matmul(
                            self.pb(bank), w2bf[i][:, kc * 128:(kc + 1) * 128],
                            h2(kc)[:, tcl * 512:(tcl + 1) * 512],
                            start=(kc == 0), stop=(kc == NOGH - 1)),
                            *dd, inc=(1 if kc == NOGH - 1 else 0))
                    setattr(self, "_t_w2b%d" % i, tm)
                    self.bank_written(bank, tm)
                    b2col = self.col8b[:, og:og + 1]
                    tup = p.dve.do(lambda e, og=og, tc=tc, bank=bank, b2col=b2col:
                                   e.scalar_tensor_tensor(
                                       ftv(og)[:, tc * 512:(tc + 1) * 512], self.pb(bank),
                                       b2col, ftv(og)[:, tc * 512:(tc + 1) * 512],
                                       op0=ALU.add, op1=ALU.add),
                                   tm, tb2, inc=1)
                    self.bank_did_read(bank, tup)
            p.barrier()
        if FF_DBG & 4:
            self.emit_dump(2 + 3 * l, ftv)
    tcc = {0: None, 1: None}
    for g in range(8):
        for tc in range(4):
            i = tc % 2
            src = tr1 if i == 0 else tr2
            cs = slice(512 * tc, 512 * (tc + 1))
            td = p.dma(src[:], src_hbm[g * 128:(g + 1) * 128, cs], tcc[i])
            ta = p.dve.do(lambda e, g=g, cs=cs, src=src: e.tensor_tensor(
                src[:], src[:], ftv(g)[:, cs], op=ALU.add), td, inc=1)
            ts_ = p.dma(dst[g * 128:(g + 1) * 128, cs], src[:], ta)
            tcc[i] = ts_
    p.barrier()
    A.pop()


Builder.emit_ffn = emit_ffn


def emit_attn_b(self, src_hbm, dst_hbm):
    p, A = self.p, self.AR
    self._bank_r, self._bank_w = {}, {}
    self._t_bc = None
    self._t_xnc = {0: None, 1: None}
    A.push()
    xn3n = A.alloc_bf(8 * 2048)
    xb = [A.alloc(1026) for _ in range(8)]
    xab = A.alloc_bf(8 * 1024)
    xabv = lambda k: xab[:, k * 1024:(k + 1) * 1024]
    siluz = A.alloc_bf(8 * 1024)
    szv = lambda k: siluz[:, k * 1024:(k + 1) * 1024]
    gated = A.alloc_bf(8 * 1024)
    gtv = lambda k: gated[:, k * 1024:(k + 1) * 1024]
    t1b = A.alloc_bf(1024)
    t3B = A.alloc_bf(1024)
    t3C = A.alloc_bf(1024)
    wstg = A.alloc(1024)
    wbfg = [A.alloc_bf(1024) for _ in range(2)]
    xpw = A.alloc_bf(512)
    bgw = A.alloc_bf(128)
    cgw = A.alloc_bf(128)
    dtw = A.alloc_bf(1024)
    bpw = A.alloc_bf(1024)
    cpw = A.alloc_bf(1024)
    tra = A.alloc(512); trb2 = A.alloc(512); trc2 = A.alloc(512)
    trd = A.alloc(512); tre2 = A.alloc(512)
    stash = A.alloc(16)

    mu3 = self.rows_a[0:1, :]; r3 = self.rows_a[32:33, :]
    rc = self.rows_b[0:1, :]; mr3 = self.rows_b[32:33, :]
    t_stats = self.emit_stats(src_hbm, mu3, r3, EPS)
    tmpA = self.bc_rows[0:1, 512:1024]
    tmpB = self.rows_c[0:1, :]
    trows = None
    for tc in range(4):
        cs = slice(512 * tc, 512 * (tc + 1))
        p.dve.do(lambda e, cs=cs: e.tensor_tensor(tmpA[:], r3[:, cs], r3[:, cs],
                                                  op=ALU.mult), t_stats)
        p.dve.do(lambda e: e.reciprocal(tmpB[0:1, 0:512], tmpA[:]))
        p.dve.do(lambda e: e.tensor_scalar(tmpB[0:1, 0:512], tmpB[0:1, 0:512],
                                           -float(EPS), None, op0=ALU.add))
        p.dve.do(lambda e: e.tensor_tensor(tmpB[0:1, 0:512], tmpB[0:1, 0:512], tmpA[:],
                                           op=ALU.mult))
        tv = p.dve.do(lambda e: e.engine_nop(), inc=1)
        p.act.do(lambda e: e.activation(tmpB[0:1, 0:512], tmpB[0:1, 0:512], AF.Ln,
                                        bias=self.konst[0:1, 0:1]), tv, self.t_k0)
        ta = p.act.do(lambda e: e.activation(tmpB[0:1, 0:512], tmpB[0:1, 0:512],
                                             AF.Exp, scale=-0.5), inc=1)
        r3c = self.rows_c[0:1, 512:1024]
        p.dve.do(lambda e, cs=cs: e.tensor_copy(r3c[:], r3[:, cs]))
        p.dve.do(lambda e, cs=cs: e.tensor_tensor(rc[:, cs], r3c[:], tmpB[0:1, 0:512],
                                                  op=ALU.mult), ta)
        trows = p.dve.do(lambda e, cs=cs: e.tensor_tensor(mr3[:, cs], mu3[:, cs], r3c[:],
                                                          op=ALU.mult), inc=1)
    tg = p.dma(self.col16[:, 0:8], self.dr["b_gamma"][:])
    tbt = p.dma(self.col16[:, 8:16], self.dr["b_beta"][:])
    lnb_bf = self.col8c[:, 0:4].bitcast(BF16)[:, 0:8]
    tbt2 = p.pool.do(lambda e: e.tensor_copy(lnb_bf, self.col16[:, 8:16]), tbt, inc=1)
    tcv = p.dma(self.aw_conv[:, 0:24], self.dr["b_convW"][:])
    tcb = p.dma(self.aw_cvB[:, 0:8], self.dr["b_convB"][:])
    tdb = p.dma(self.aw_cvB[:, 8:16], self.dr["b_dtB"][:])
    tAc = p.dma(self.aw_dtB[:, 0:8], self.dr["b_A"][:])
    tDc = p.dma(self.aw_D[:, 0:8], self.dr["b_Dsk"][:])
    tw1 = p.dma(wstg[:, 0:512].rearrange("p (a b) -> p a b", a=8),
                self.dr["b_xpW"][:].transpose([1, 0, 2]))
    txp = p.pool.do(lambda e: e.tensor_copy(xpw[:], wstg[:, 0:512]), tw1, inc=1)
    tw2 = p.dma(wstg[:, 0:128].rearrange("p (a b) -> p a b", a=8),
                self.dr["b_BgW"][:].transpose([1, 0, 2]), txp)
    tbg = p.pool.do(lambda e: e.tensor_copy(bgw[:], wstg[:, 0:128]), tw2, inc=1)
    tw3 = p.dma(wstg[:, 0:128].rearrange("p (a b) -> p a b", a=8),
                self.dr["b_CgW"][:].transpose([1, 0, 2]), tbg)
    tcg = p.pool.do(lambda e: e.tensor_copy(cgw[:], wstg[:, 0:128]), tw3, inc=1)
    tw4 = p.dma(wstg[0:64, 0:1024], self.dr["b_dtW"][:], tcg)
    tdt = p.pool.do(lambda e: e.tensor_copy(dtw[0:64, :], wstg[0:64, 0:1024]), tw4, inc=1)
    tw5 = p.dma(wstg[0:16, 0:1024], self.dr["b_BpW"][:], tdt)
    tbp = p.pool.do(lambda e: e.tensor_copy(bpw[0:16, :], wstg[0:16, 0:1024]), tw5, inc=1)
    tw6 = p.dma(wstg[0:16, 0:1024], self.dr["b_CpW"][:], tbp)
    tcp = p.pool.do(lambda e: e.tensor_copy(cpw[0:16, :], wstg[0:16, 0:1024]), tw6, inc=1)

    for tc in range(4):
        cs = slice(512 * tc, 512 * (tc + 1))
        tmu = self.bcast_row(6, mu3[:, cs], tuple(self.bank_write(6)) + (t_stats,))
        self.bank_written(6, tmu)
        trr = self.bcast_row(7, rc[:, cs], tuple(self.bank_write(7)) + (trows,))
        self.bank_written(7, trr)
        tl = None
        if not hasattr(self, "_t_xnc"):
            self._t_xnc = {0: None, 1: None}
        tcc = self._t_xnc
        for g in range(8):
            i = g % 2
            src = tra if i == 0 else trb2
            td = p.dma(src[:], src_hbm[g * 128:(g + 1) * 128, cs], tcc[i])
            p.dve.do(lambda e, src=src: e.tensor_tensor(src[:], src[:], self.pb(6),
                                                        op=ALU.subtract), td, tmu)
            gcol = self.col16[:, g:g + 1]
            tl = p.dve.do(lambda e, g=g, cs=cs, src=src, gcol=gcol: e.scalar_tensor_tensor(
                xn3n[:, g * 2048:(g + 1) * 2048][:, cs], src[:], gcol, self.pb(7),
                op0=ALU.mult, op1=ALU.mult), trr, tg, inc=1)
            tcc[i] = tl
        self.bank_did_read(6, tl)
        self.bank_did_read(7, tl)
    p.barrier()

    tzero = p.dve.do(lambda e: e.memset(self.states_b[:], 0.0), inc=1)
    for h in range(2):
        self._bank_r, self._bank_w = {}, {}
        for og in range(16):
            i = og % 2
            tdw = p.dma(wstg[:], self.dr["b_inW"][og].rearrange("p a b -> p (a b)"),
                        getattr(self, "_t_wstb", None))
            tcw = p.pool.do(lambda e, i=i: e.tensor_copy(wbfg[i][:], wstg[:]),
                            tdw, getattr(self, "_t_wbg%d" % i, None), inc=1)
            self._t_wstb = tcw
            tcbX = None
            for kc in range(8):
                dd = ([tcw, tbt2] + self.bank_write(2)) if kc == 0 else []
                tcbX = p.pe.do(lambda e, kc=kc, i=i: e.matmul(
                    self.pb(2, 128, 1), wbfg[i][:, kc * 128:(kc + 1) * 128],
                    lnb_bf[:, kc:kc + 1], start=(kc == 0), stop=(kc == 7)),
                    *dd, inc=(1 if kc == 7 else 0))
            self.bank_written(2, tcbX)
            tcb2 = p.act.do(lambda e: e.copy(self.col8b[:, 0:1], self.pb(2, 128, 1)),
                            tcbX, getattr(self, "_t_cbb", None), inc=1)
            self.bank_did_read(2, tcb2)
            self._t_cbb = tcb2
            for tcl in range(3):
                if tcl == 2:
                    if og >= 8:
                        continue
                    nc_ = 1024 * h + 1024
                    if nc_ >= T:
                        tmz = p.dve.do(lambda e, og=og: e.memset(xb[og][:, 1025:1026], 0.0),
                                       getattr(self, "_t_xb%d" % og, None), inc=1)
                        setattr(self, "_t_xb%d" % og, tmz)
                        continue
                    tm = None
                    for kc in range(8):
                        dd = ([tcw] + self.bank_write(3)) if kc == 0 else []
                        tm = p.pe.do(lambda e, kc=kc, i=i, nc_=nc_: e.matmul(
                            self.pb(3, 128, 1), wbfg[i][:, kc * 128:(kc + 1) * 128],
                            xn3n[:, kc * 2048 + nc_:kc * 2048 + nc_ + 1],
                            start=(kc == 0), stop=(kc == 7)), *dd, inc=(1 if kc == 7 else 0))
                    self.bank_written(3, tm)
                    tev = p.act.do(lambda e, og=og: e.activation(
                        xb[og][:, 1025:1026], self.pb(3, 128, 1), AF.Identity,
                        bias=self.col8b[:, 0:1]),
                        tm, tcb2, getattr(self, "_t_xb%d" % og, None), inc=1)
                    self.bank_did_read(3, tev)
                    setattr(self, "_t_xb%d" % og, tev)
                    continue
                tc = 2 * h + tcl
                bank = tcl % 2
                tm = None
                for kc in range(8):
                    dd = ([tcw] + self.bank_write(bank)) if kc == 0 else []
                    tm = p.pe.do(lambda e, kc=kc, i=i, bank=bank, tc=tc: e.matmul(
                        self.pb(bank), wbfg[i][:, kc * 128:(kc + 1) * 128],
                        xn3n[:, kc * 2048 + tc * 512:kc * 2048 + (tc + 1) * 512],
                        start=(kc == 0), stop=(kc == 7)), *dd, inc=(1 if kc == 7 else 0))
                self.bank_written(bank, tm)
                if og < 8:
                    if tcl == 0:
                        if h == 0:
                            tpd = p.dve.do(lambda e, og=og: e.memset(xb[og][:, 0:1], 0.0),
                                           getattr(self, "_t_xb%d" % og, None), inc=1)
                        else:
                            tpd = p.dve.do(lambda e, og=og: e.tensor_copy(
                                xb[og][:, 0:1], stash[:, og:og + 1]),
                                getattr(self, "_t_xb%d" % og, None),
                                getattr(self, "_t_st%d" % og, None), inc=1)
                        setattr(self, "_t_xb%d" % og, tpd)
                    tev = p.act.do(lambda e, og=og, tcl=tcl, bank=bank: e.activation(
                        xb[og][:, 1 + tcl * 512:1 + (tcl + 1) * 512], self.pb(bank),
                        AF.Identity, bias=self.col8b[:, 0:1]),
                        tm, tcb2, getattr(self, "_t_xb%d" % og, None), inc=1)
                    self.bank_did_read(bank, tev)
                    setattr(self, "_t_xb%d" % og, tev)
                    if tcl == 1 and h == 0:
                        tst_ = p.dve.do(lambda e, og=og: e.tensor_copy(
                            stash[:, og:og + 1], xb[og][:, 1024:1025]), tev, inc=1)
                        setattr(self, "_t_st%d" % og, tst_)
                else:
                    og2 = og - 8
                    tev = p.act.do(lambda e, og2=og2, tcl=tcl, bank=bank: e.activation(
                        szv(og2)[:, tcl * 512:(tcl + 1) * 512], self.pb(bank),
                        AF.Silu, bias=self.col8b[:, 0:1]),
                        tm, tcb2, getattr(self, "_t_sz%d" % og2, None), inc=1)
                    self.bank_did_read(bank, tev)
                    setattr(self, "_t_sz%d" % og2, tev)
            setattr(self, "_t_wbg%d" % i, tm)
        p.barrier()
        for og in range(8):
            for cc in range(2):
                a0 = cc * 512
                p.dve.do(lambda e, og=og, a0=a0: e.tensor_scalar_mul(
                    tra[:], xb[og][:, a0:a0 + 512], self.aw_conv[:, og * 3:og * 3 + 1]),
                    tcv, getattr(self, "_t_cnv", None))
                p.dve.do(lambda e, og=og, a0=a0: e.scalar_tensor_tensor(
                    trb2[:], xb[og][:, a0 + 1:a0 + 513],
                    self.aw_conv[:, og * 3 + 1:og * 3 + 2], tra[:],
                    op0=ALU.mult, op1=ALU.add))
                tcc2 = p.dve.do(lambda e, og=og, a0=a0: e.scalar_tensor_tensor(
                    tra[:], xb[og][:, a0 + 2:a0 + 514],
                    self.aw_conv[:, og * 3 + 2:og * 3 + 3], trb2[:],
                    op0=ALU.mult, op1=ALU.add), inc=1)
                tsl = p.act.do(lambda e, og=og, a0=a0: e.activation(
                    xabv(og)[:, a0:a0 + 512], tra[:], AF.Silu,
                    bias=self.aw_cvB[:, og:og + 1]), tcc2, tcb,
                    getattr(self, "_t_xa%d" % og, None), inc=1)
                setattr(self, "_t_xa%d" % og, tsl)
                self._t_cnv = tsl
        p.barrier()
        for tcl in range(2):
            tmx1 = tmx2 = tmx3 = None
            for kc in range(8):
                d1 = ([txp] + self.bank_write(0)) if kc == 0 else []
                tmx1 = p.pe.do(lambda e, kc=kc, tcl=tcl: e.matmul(
                    self.pb(0, 64), xpw[:, kc * 64:(kc + 1) * 64],
                    xabv(kc)[:, tcl * 512:(tcl + 1) * 512],
                    start=(kc == 0), stop=(kc == 7)), *d1, inc=(1 if kc == 7 else 0))
                d2 = ([tbg] + self.bank_write(1)) if kc == 0 else []
                tmx2 = p.pe.do(lambda e, kc=kc, tcl=tcl: e.matmul(
                    self.pb(1, 16), bgw[:, kc * 16:(kc + 1) * 16],
                    xabv(kc)[:, tcl * 512:(tcl + 1) * 512],
                    start=(kc == 0), stop=(kc == 7)), *d2, inc=(1 if kc == 7 else 0))
                d3 = ([tcg] + self.bank_write(2)) if kc == 0 else []
                tmx3 = p.pe.do(lambda e, kc=kc, tcl=tcl: e.matmul(
                    self.pb(2, 16), cgw[:, kc * 16:(kc + 1) * 16],
                    xabv(kc)[:, tcl * 512:(tcl + 1) * 512],
                    start=(kc == 0), stop=(kc == 7)), *d3, inc=(1 if kc == 7 else 0))
            self.bank_written(0, tmx1)
            self.bank_written(1, tmx2)
            self.bank_written(2, tmx3)
            te1 = p.act.do(lambda e, tcl=tcl: e.copy(
                t1b[0:64, tcl * 512:(tcl + 1) * 512], self.pb(0, 64)), tmx1,
                getattr(self, "_t_t1b", None), inc=1)
            self.bank_did_read(0, te1)
            self._t_t1b = te1
            te2 = p.act.do(lambda e, tcl=tcl: e.copy(
                t3B[0:16, tcl * 512:(tcl + 1) * 512], self.pb(1, 16)), tmx2,
                getattr(self, "_t_t3B", None), inc=1)
            self.bank_did_read(1, te2)
            self._t_t3B = te2
            te3 = p.act.do(lambda e, tcl=tcl: e.copy(
                t3C[0:16, tcl * 512:(tcl + 1) * 512], self.pb(2, 16)), tmx3,
                getattr(self, "_t_t3C", None), inc=1)
            self.bank_did_read(2, te3)
            self._t_t3C = te3
        p.barrier()
        for og in range(8):
            acol = self.aw_dtB[:, og:og + 1]
            dcol = self.aw_cvB[:, 8 + og:9 + og]
            Dcol = self.aw_D[:, og:og + 1]
            for tcl in range(2):
                tdm = p.pe.do(lambda e, og=og, tcl=tcl: e.matmul(
                    self.pb(3), dtw[0:64, og * 128:(og + 1) * 128],
                    t1b[0:64, tcl * 512:(tcl + 1) * 512], start=True, stop=True),
                    tdt, *self.bank_write(3), inc=1)
                self.bank_written(3, tdm)
                tr4 = p.dve.do(lambda e, dcol=dcol: e.tensor_scalar(
                    trd[:], self.pb(3), dcol, 0.0, op0=ALU.add, op1=ALU.max),
                    tdm, tdb, getattr(self, "_t_trd", None), inc=1)
                trA = p.act.do(lambda e, dcol=dcol: e.activation(
                    tra[:], self.pb(3), AF.Abs, bias=dcol), tdm, tr4,
                    getattr(self, "_t_spb", None), inc=1)
                self.bank_did_read(3, trA)
                self.bank_did_read(3, tr4)
                p.act.do(lambda e: e.activation(trb2[:], tra[:], AF.Exp, scale=-1.0))
                trC = p.act.do(lambda e: e.activation(tra[:], trb2[:], AF.Ln, bias=1.0),
                               inc=1)
                tdl = p.dve.do(lambda e: e.tensor_add(trd[:], trd[:], tra[:]), trC, inc=1)
                self._t_trd = tdl
                self._t_spb = tdl
                tBm = p.pe.do(lambda e, og=og, tcl=tcl: e.matmul(
                    self.pb(4), bpw[0:16, og * 128:(og + 1) * 128],
                    t3B[0:16, tcl * 512:(tcl + 1) * 512], start=True, stop=True),
                    tbp, *self.bank_write(4), inc=1)
                self.bank_written(4, tBm)
                tCm = p.pe.do(lambda e, og=og, tcl=tcl: e.matmul(
                    self.pb(5), cpw[0:16, og * 128:(og + 1) * 128],
                    t3C[0:16, tcl * 512:(tcl + 1) * 512], start=True, stop=True),
                    tcp, *self.bank_write(5), inc=1)
                self.bank_written(5, tCm)
                tda = p.act.do(lambda e, acol=acol: e.activation(
                    trb2[:], trd[:], AF.Exp, scale=acol), tdl, tAc,
                    getattr(self, "_t_da", None), inc=1)
                tbx = p.dve.do(lambda e: e.tensor_tensor(trc2[:], trd[:], self.pb(4),
                                                         op=ALU.mult),
                               tBm, tdl, getattr(self, "_t_bx", None), inc=1)
                self.bank_did_read(4, tbx)
                tbx2 = p.pool.do(lambda e, og=og, tcl=tcl: e.tensor_tensor(
                    tre2[:], trc2[:], xabv(og)[:, tcl * 512:(tcl + 1) * 512],
                    op=ALU.mult), tbx, getattr(self, "_t_bx2", None), inc=1)
                scol = self.states_b[:, og:og + 1]
                tsc = p.dve.do(lambda e, scol=scol: e.tensor_tensor_scan(
                    trc2[:], trb2[:], tre2[:], scol, op0=ALU.mult, op1=ALU.add),
                    tda, tbx2, tzero, inc=1)
                self._t_bx = tsc
                self._t_da = tsc
                tsv = p.dve.do(lambda e, scol=scol: e.tensor_copy(scol, trc2[:, 511:512]),
                               inc=1)
                ty = p.dve.do(lambda e: e.tensor_tensor(trc2[:], trc2[:], self.pb(5),
                                                        op=ALU.mult), tCm, inc=1)
                self.bank_did_read(5, ty)
                tg1 = p.dve.do(lambda e, og=og, tcl=tcl, Dcol=Dcol: e.scalar_tensor_tensor(
                    trd[:], xabv(og)[:, tcl * 512:(tcl + 1) * 512], Dcol, trc2[:],
                    op0=ALU.mult, op1=ALU.add), tDc, inc=1)
                self._t_bx2 = tg1
                tgt = p.pool.do(lambda e, og=og, tcl=tcl: e.tensor_tensor(
                    gtv(og)[:, tcl * 512:(tcl + 1) * 512], trd[:],
                    szv(og)[:, tcl * 512:(tcl + 1) * 512], op=ALU.mult),
                    tg1, getattr(self, "_t_gt%d" % og, None), inc=1)
                setattr(self, "_t_gt%d" % og, tgt)
                self._t_trd = tgt
        p.barrier()
        self._bank_r, self._bank_w = {}, {}
        for tcl in range(2):
            tc = 2 * h + tcl
            cs = slice(512 * tc, 512 * (tc + 1))
            tb0 = self.bcast_row(0, r3[:, cs], tuple(self.bank_write(0)))
            self.bank_written(0, tb0)
            tb1 = self.bcast_row(1, mr3[:, cs], tuple(self.bank_write(1)) + (trows,))
            self.bank_written(1, tb1)
            for og in range(8):
                i = og % 2
                bank = 6 + (og % 2)
                tdw = p.dma(wstg[:], self.dr["b_outW"][og].rearrange("p a b -> p (a b)"),
                            getattr(self, "_t_wstb", None))
                tcw = p.pool.do(lambda e, i=i: e.tensor_copy(wbfg[i][:], wstg[:]),
                                tdw, getattr(self, "_t_wbg%d" % i, None), inc=1)
                self._t_wstb = tcw
                tm = None
                for kc in range(8):
                    dd = ([tcw] + self.bank_write(bank)) if kc == 0 else []
                    tm = p.pe.do(lambda e, kc=kc, i=i, bank=bank, tcl=tcl: e.matmul(
                        self.pb(bank), wbfg[i][:, kc * 128:(kc + 1) * 128],
                        gtv(kc)[:, tcl * 512:(tcl + 1) * 512],
                        start=(kc == 0), stop=(kc == 7)), *dd, inc=(1 if kc == 7 else 0))
                setattr(self, "_t_wbg%d" % i, tm)
                self.bank_written(bank, tm)
                rows = slice(og * 128, (og + 1) * 128)
                td = p.dma(tra[:] if og % 2 == 0 else trb2[:], src_hbm[rows, cs],
                           getattr(self, "_t_sup%d" % (og % 2), None))
                src = tra if og % 2 == 0 else trb2
                p.dve.do(lambda e, src=src: e.tensor_tensor(trc2[:], src[:], self.pb(0),
                                                            op=ALU.mult),
                         td, tb0, getattr(self, "_t_trc2b", None))
                p.dve.do(lambda e, src=src: e.tensor_add(trc2[:], trc2[:], src[:]))
                p.dve.do(lambda e: e.tensor_tensor(trc2[:], trc2[:], self.pb(1),
                                                   op=ALU.subtract), tb1)
                tout = p.dve.do(lambda e, bank=bank: e.tensor_tensor(
                    trc2[:], trc2[:], self.pb(bank), op=ALU.add), tm, inc=1)
                self.bank_did_read(bank, tout)
                tst2 = p.dma(dst_hbm[rows, cs], trc2[:], tout)
                self._t_trc2b = tst2
                setattr(self, "_t_sup%d" % (og % 2), tst2)
            self.bank_did_read(0, tout)
            self.bank_did_read(1, tout)
        p.barrier()
    A.pop()


Builder.emit_attn_b = emit_attn_b


# version 12
# speedup vs baseline: 1.2902x; 1.0002x over previous
import os
import numpy as np
import ml_dtypes
import concourse.bass as bass
import concourse.mybir as mybir
from concourse.bass_utils import run_bass_kernel_spmd

F32 = mybir.dt.float32
BF16 = mybir.dt.bfloat16
AF = mybir.ActivationFunctionType
ALU = mybir.AluOpType
BF = ml_dtypes.bfloat16

B, T, D = 8, 2048, 1024
TQ = T // 4
NKD = D // 128
DIN, DST, DTR = 2048, 16, 64
NDI = DIN // 128
NH, DH, WIN = 8, 64, 512
QK_SCALE = 8.0
HID, HIDP = 2730, 2816
NOGH = HIDP // 128
EPS = 1e-5
NEG = -1073741824.0
AA_LIM = int(os.environ.get('AA_LIM', '99'))
AA_PH = int(os.environ.get('AA_PH', '99'))
AA_G = int(os.environ.get('AA_G', '99'))
AA_D = int(os.environ.get('AA_D', '99'))
AA_S = int(os.environ.get('AA_S', '31'))
FF_L = int(os.environ.get('FF_L', '2'))
FF_PH = int(os.environ.get('FF_PH', '99'))
FF_DBG = int(os.environ.get('FF_DBG', '0'))
_DBL = int(os.environ.get('DBL', '0'))
_DBH = int(os.environ.get('DBH', '0'))
ARENA = 43392

_built = {}



def _tile_lhsT(wT, ncols=128):
    K, M = wT.shape
    nkc, nog = K // 128, M // ncols
    a = wT.reshape(nkc, 128, nog, ncols).transpose(2, 0, 1, 3)
    return np.ascontiguousarray(a.transpose(0, 2, 1, 3))

def _cols(v):
    return np.ascontiguousarray(np.asarray(v, np.float32).reshape(-1, 128).T)

def _band_masks():
    m = np.zeros((8, 128, 512), np.float32)
    for j in range(8):
        ki = np.arange(128)[:, None] + 128 * j - 512
        q = np.arange(512)[None, :]
        ok = (q - ki >= 0) & (q - ki <= WIN)
        m[j] = np.where(ok, 0.0, NEG)
    return m

def _rotary_tables():
    half_idx = np.arange(0, DH, 2, dtype=np.float64)
    inv_freq = 1.0 / 10000.0 ** (half_idx / DH)
    t = np.arange(T, dtype=np.float64)
    freqs = np.concatenate([t[:, None] * inv_freq[None, :]] * 2, axis=-1)
    base = (half_idx + 0.4 * DH) / (1.4 * DH)
    power = (t - T // 2) / float(WIN // 2)
    scale = np.concatenate([base[None, :] ** power[:, None]] * 2, axis=-1)
    cos, sin = np.cos(freqs), np.sin(freqs)
    out = []
    for a in (cos * scale, sin * scale, cos / scale, sin / scale):
        a = a.T.astype(np.float32)
        out.append(np.concatenate([a, a], axis=0))
    return out

def _swap_sign():
    M = np.zeros((128, 128), np.float32)
    for h in range(2):
        b = 64 * h
        for p in range(32):
            M[b + p + 32, b + p] = -1.0
            M[b + p, b + p + 32] = 1.0
    return M


def stage_shared(inp):
    g = {}
    f32 = lambda a: np.ascontiguousarray(np.asarray(a), dtype=np.float32)
    for d in ("f", "b"):
        p = lambda k: f32(inp[f"L0_attn_a_{d}_{k}"])
        g[f"a{d}_inW"] = _tile_lhsT(p("in_proj_w").T)
        g[f"a{d}_outW"] = _tile_lhsT(p("out_proj_w").T)
        xpT = p("x_proj_w").T
        g[f"a{d}_xpW1"] = np.ascontiguousarray(
            xpT[:, 0:80].reshape(NDI, 128, 80).transpose(1, 0, 2).reshape(128, NDI * 80))
        g[f"a{d}_xpW2"] = np.ascontiguousarray(
            xpT[:, 80:96].reshape(NDI, 128, 16).transpose(1, 0, 2).reshape(128, NDI * 16))
        g[f"a{d}_dtW"] = f32(p("dt_proj_w").T)
        cw = p("conv_w")
        if d == "b":
            cw = cw[:, ::-1]
        g[f"a{d}_convW"] = np.ascontiguousarray(
            cw.reshape(NDI, 128, 4).transpose(1, 0, 2).reshape(128, NDI * 4))
        g[f"a{d}_convB"] = _cols(p("conv_b"))
        g[f"a{d}_dtB"] = _cols(p("dt_proj_b"))
        g[f"a{d}_D"] = _cols(p("D"))
        g[f"a{d}_Alog"] = np.ascontiguousarray(
            p("A_log").reshape(NDI, 128, DST).transpose(1, 0, 2).reshape(128, NDI * DST))
    for l in range(2):
        wqkv = f32(inp["L0_ffn_wqkv"][l])
        g[f"f{l}_qkW"] = _tile_lhsT(wqkv[:, 0:1024])
        g[f"f{l}_vW"] = np.ascontiguousarray(wqkv[:, 1024:1536].reshape(NKD, 128, 512))
        g[f"f{l}_qs"] = np.ascontiguousarray(np.tile(f32(inp["L0_ffn_q_scale"][l]), 2)[:, None])
        g[f"f{l}_ks"] = np.ascontiguousarray(np.tile(f32(inp["L0_ffn_k_scale"][l]), 2)[:, None])
        g[f"f{l}_woW"] = _tile_lhsT(f32(inp["L0_ffn_wo"][l]))
        g[f"f{l}_lng"] = _cols(inp["L0_ffn_ln_g"][l])
        g[f"f{l}_lnb"] = _cols(inp["L0_ffn_ln_b"][l])
        w1 = f32(inp["L0_ffn_w1"][l])
        w1a = np.zeros((D, HIDP), np.float32); w1a[:, :HID] = w1[:, :HID]
        w1g = np.zeros((D, HIDP), np.float32); w1g[:, :HID] = w1[:, HID:]
        g[f"f{l}_w1a"] = _tile_lhsT(w1a)
        g[f"f{l}_w1g"] = _tile_lhsT(w1g)
        b1 = f32(inp["L0_ffn_b1"][l])
        b1a = np.zeros(HIDP, np.float32); b1a[:HID] = b1[:HID]
        b1g = np.zeros(HIDP, np.float32); b1g[:HID] = b1[HID:]
        g[f"f{l}_b1a"] = _cols(b1a); g[f"f{l}_b1g"] = _cols(b1g)
        w2p = np.zeros((HIDP, D), np.float32); w2p[:HID] = f32(inp["L0_ffn_w2"][l])
        g[f"f{l}_w2W"] = _tile_lhsT(w2p)
        g[f"f{l}_b2"] = _cols(inp["L0_ffn_b2"][l])
    p = lambda k: f32(inp[f"L0_attn_b_{k}"])
    g["b_gamma"] = _cols(p("gamma")); g["b_beta"] = _cols(p("beta"))
    g["b_inW"] = _tile_lhsT(p("W_in"))
    cw3 = p("conv_w")[:, 0, :]
    g["b_convW"] = np.ascontiguousarray(
        cw3.T.reshape(NKD, 128, 3).transpose(1, 0, 2).reshape(128, NKD * 3))
    g["b_convB"] = _cols(p("conv_b"))
    g["b_xpW"] = np.ascontiguousarray(p("W_xp").reshape(NKD, 128, 64))
    g["b_BgW"] = np.ascontiguousarray(p("W_Bg").reshape(NKD, 128, 16))
    g["b_CgW"] = np.ascontiguousarray(p("W_Cg").reshape(NKD, 128, 16))
    g["b_dtW"] = f32(p("W_dt"))
    g["b_dtB"] = _cols(p("b_dt"))
    g["b_BpW"] = f32(p("W_Bp")); g["b_CpW"] = f32(p("W_Cp"))
    g["b_A"] = _cols(p("A")); g["b_Dsk"] = _cols(p("D_skip"))
    g["b_outW"] = _tile_lhsT(p("W_out"))
    g["c_eye"] = np.eye(128, dtype=np.float32).astype(BF)
    g["c_swap"] = _swap_sign().astype(BF)
    es = np.zeros((DST, DST, 128), np.float32)
    for s in range(DST):
        es[:, s][s] = 1.0
    es = np.zeros((DST, DST * 128), np.float32)
    for s in range(DST):
        es[s, s * 128:(s + 1) * 128] = 1.0
    g["c_Es"] = es.astype(BF)
    g["c_masks"] = np.ascontiguousarray(
        _band_masks().transpose(1, 0, 2).reshape(128, 8 * 512)).astype(BF)
    g["c_rot"] = np.ascontiguousarray(
        np.concatenate(_rotary_tables(), axis=1)).astype(BF)
    g["c_ones"] = np.ones((128, 2), np.float32)
    g["c_orow"] = np.ones((128, 128), np.float32)
    g["c_orow_bf"] = np.ones((128, 128), np.float32).astype(BF)
    g["c_ones_bf"] = np.ones((128, 2), np.float32).astype(BF)
    hsel = np.zeros((128, 2), np.float32)
    hsel[0:64, 0] = 1.0; hsel[64:128, 1] = 1.0
    g["c_hsel"] = hsel.astype(BF)
    s2b = np.zeros((2, 128), np.float32)
    s2b[0, 0:64] = 1.0; s2b[1, 64:128] = 1.0
    g["c_s2b"] = s2b
    g["c_s2b_bf"] = s2b.astype(BF)
    rsel = np.zeros((2, 256), np.float32)
    rsel[0, 0:128] = 1.0; rsel[1, 128:256] = 1.0
    g["c_rsel"] = rsel.astype(BF)
    return g



_DTSZ = {"dt.float32": 4, "dt.bfloat16": 2, "dt.float32r": 4, "dt.int32": 4,
         "dt.float16": 2, "dt.int8": 1, "dt.uint8": 1}


def _ap_rect(ap):
    h = ap.tensor
    kind = type(h).__name__
    if "DRam" in kind or "PSum" in kind or "Psum" in kind:
        return None
    sz = _DTSZ[str(ap.dtype)]
    C = h.shape[-1]
    p0 = ap.offset // C + getattr(h, "base_partition", 0)
    c0 = ap.offset % C
    pext = 0
    cext = 0
    for s, n in ap.ap:
        if n == 1:
            continue
        if s % C == 0:
            pext += (s // C) * (n - 1)
        else:
            cext += s * (n - 1)
    return (h.name, p0, p0 + pext + 1, c0 * sz, (c0 + cext + 1) * sz)


class _RecInst:
    def then_inc(self, *a, **k):
        return self


class _Rec:
    def __init__(self):
        self.wr = []
        self.rd = []
        self.reads_psum = False

    def __getattr__(self, mname):
        def call(*args, **kwargs):
            aps = [a for a in args if hasattr(a, "tensor")]
            aps += [v for v in kwargs.values() if hasattr(v, "tensor")]
            if mname == "dma_start":
                aps = [kwargs["out"], kwargs["in_"]]
            if aps:
                r = _ap_rect(aps[0])
                if r:
                    self.wr.append(r)
                for a in aps[1:]:
                    r = _ap_rect(a)
                    if r:
                        self.rd.append(r)
                    elif "Sum" in type(a.tensor).__name__ or \
                            "sum" in type(a.tensor).__name__:
                        self.reads_psum = True
            return _RecInst()
        return call


def _overlap(a, b):
    return (a[0] == b[0] and a[1] < b[2] and b[1] < a[2]
            and a[3] < b[4] and b[3] < a[4])


class EQ:
    HIST = 64

    def __init__(self, key, drain=False):
        self.key = key
        self.drain = drain
        self.items = []
        self.count = 0
        self.n_uncov = 0
        self.waited = {}
        self.hist = []
        self.self_waited = 0

    def wait(self, tok):
        if tok is None:
            return
        if not (isinstance(tok, tuple) and len(tok) == 2 and hasattr(tok[0], "key")):
            for t in tok:
                self.wait(t)
            return
        eq2, v = tok
        if eq2 is self:
            return
        if self.waited.get(eq2.key, 0) >= v:
            return
        self.waited[eq2.key] = v
        self.items.append(("w", eq2, v))

    def do_sem(self, fn, sem_eq, inc, *deps):
        for dp in deps:
            self.wait(dp)
        self.items.append(("i", fn, inc, sem_eq))

    def do(self, fn, *deps, inc=0):
        for dp in deps:
            self.wait(dp)
        if self.drain:
            if getattr(self, "blanket", False) or \
                    (self.key == "dve" and _DBL <= self.count < _DBH):
                if self.count:
                    self.items.append(("w", self, self.count))
                self.items.append(("i", fn, 1))
                self.count += 1
                self.n_uncov = 0
                return (self, self.count) if inc else None
            rec = _Rec()
            fn(rec)
            need = self.count if rec.reads_psum else 0
            for (cum, hw, hr) in self.hist:
                if cum <= self.self_waited:
                    continue
                hit = False
                for a in rec.rd:
                    for b2 in hw:
                        if _overlap(a, b2):
                            hit = True
                            break
                    if hit:
                        break
                if not hit:
                    for a in rec.wr:
                        for b2 in hw:
                            if _overlap(a, b2):
                                hit = True
                                break
                        if not hit:
                            for b2 in hr:
                                if _overlap(a, b2):
                                    hit = True
                                    break
                        if hit:
                            break
                if hit and cum > need:
                    need = cum
            if need > self.self_waited:
                self.items.append(("w", self, need))
                self.self_waited = need
            self.items.append(("i", fn, 1))
            self.count += 1
            self.n_uncov = 0
            self.hist.append((self.count, rec.wr, rec.rd))
            if len(self.hist) > self.HIST:
                self.hist.pop(0)
            return (self, self.count) if inc else None
        self.items.append(("i", fn, inc))
        if inc:
            self.count += inc
            self.n_uncov = 0
            return (self, self.count)
        self.n_uncov += 1
        return None

    def tok(self):
        assert self.n_uncov == 0, f"{self.key}: {self.n_uncov} uncovered instrs at fence"
        return (self, self.count)


class Lane:
    def __init__(self, key):
        self.key = key
        self.count = 0


class Prog:
    NLANES = 8

    def __init__(self, nc):
        self.nc = nc
        blanket = set(os.environ.get("DRAIN_BLANKET", "").split(","))
        self.pe, self.act, self.dve, self.pool, self.sp = (
            EQ("pe"), EQ("act", drain=True), EQ("dve", drain=True),
            EQ("pool", drain=True), EQ("sp"))
        for q in (self.act, self.dve, self.pool):
            if q.key in blanket:
                q.blanket = True
        self.all = [self.pe, self.act, self.dve, self.pool, self.sp]
        self.lanes = [Lane(f"d{i}") for i in range(self.NLANES)]
        self._rr = 0

    def dma(self, out, in_, *deps):
        ln = self.lanes[self._rr % self.NLANES]
        self._rr += 1
        if ln.count:
            self.sp.items.append(("w", ln, ln.count))
        for dp in deps:
            self.sp.wait(dp)
        self.sp.items.append(("i", lambda e: e.dma_start(out=out, in_=in_), 16, ln))
        ln.count += 16
        return (ln, ln.count)

    def barrier(self):
        toks = [q.tok() for q in self.all] + [(ln, ln.count) for ln in self.lanes if ln.count]
        for q in self.all:
            for t in toks:
                q.wait(t)

    def mm(self, out_ap, lhsT, rhs, start, stop, deps=(), inc=0):
        n = rhs.shape[-1]
        maxm = 512 if rhs.dtype != BF16 else 512
        if n <= maxm:
            return self.pe.do(
                lambda e: e.matmul(out_ap, lhsT, rhs, start=start, stop=stop),
                *deps, inc=inc)
        t = None
        nch = (n + maxm - 1) // maxm
        for i in range(nch):
            a, b2 = i * maxm, min((i + 1) * maxm, n)
            def f(e, a=a, b2=b2):
                return e.matmul(out_ap[:, a:b2], lhsT, rhs[:, a:b2], start=start, stop=stop)
            t = self.pe.do(f, *(deps if i == 0 else ()), inc=(inc if i == nch - 1 else 0))
        return t


    def emit(self, block, sems):
        def runner(q):
            def run(eng):
                for it in q.items:
                    if it[0] == "w":
                        eng.wait_ge(sems[it[1].key], it[2])
                    else:
                        inst = it[1](eng)
                        if it[2]:
                            key = it[3].key if len(it) > 3 else q.key
                            inst.then_inc(sems[key], it[2])
            return run
        block.tensor(runner(self.pe))
        block.scalar(runner(self.act))
        block.vector(runner(self.dve))
        block.gpsimd(runner(self.pool))
        block.sync(runner(self.sp))


class Arena:
    def __init__(self, ap):
        self.ap = ap
        self.ap_bf = ap.bitcast(BF16)
        self.frames = [{}]
        self.top = 0
        self.marks = []

    def push(self):
        self.marks.append(self.top)

    def pop(self):
        self.top = self.marks.pop()

    def alloc(self, cols):
        base = self.top
        self.top += cols
        assert self.top <= ARENA, f"arena overflow: {self.top} > {ARENA}"
        return self.ap[:, base:base + cols]

    def alloc_bf(self, bfcols):
        cols = (bfcols + 1) // 2
        f = self.alloc(cols)
        base = 0
        return f.bitcast(BF16)



class Builder:
    def __init__(self, nc, shared, upto="full"):
        self.nc = nc
        self.upto = upto
        self.p = Prog(nc)
        self.dr = {}
        for k, v in shared.items():
            dt = BF16 if v.dtype == BF else F32
            self.dr[k] = nc.dram_tensor(k, list(v.shape), dt, kind="ExternalInput").ap()
        self.dr["xT"] = nc.dram_tensor("xT", [D, T], F32, kind="ExternalInput").ap()
        self.out = nc.dram_tensor("outT", [D, T], F32, kind="ExternalOutput").ap()
        self.hbm_s = nc.dram_tensor("hbm_s", [D, T], F32, kind="Internal").ap()
        self.hbm_f = nc.dram_tensor("hbm_f", [D, T], F32, kind="Internal").ap()
        if FF_DBG:
            self.dbgT = nc.dram_tensor("dbgT", [8 * D, T], F32,
                                       kind="ExternalOutput").ap()
        self.wpre1 = nc.dram_tensor("wpre1", [2 * NOGH * 128, 2048], BF16,
                                    kind="Internal").ap()
        self.wpre2 = nc.dram_tensor("wpre2", [2 * 8 * 128, 2816], BF16,
                                    kind="Internal").ap()
        self.wpreq = nc.dram_tensor("wpreq", [2 * 8 * 128, 1024], BF16,
                                    kind="Internal").ap()
        self.wprea = nc.dram_tensor("wprea", [2 * 32 * 128, 1024], BF16,
                                    kind="Internal").ap()
        self.wpreo = nc.dram_tensor("wpreo", [2 * 8 * 128, 2048], BF16,
                                    kind="Internal").ap()
        self.wprew = nc.dram_tensor("wprew", [2 * 8 * 128, 512], BF16,
                                    kind="Internal").ap()
        self.t_w1pre = {}
        self.t_w2pre = {}
        self.t_qkpre = {}
        self.t_apre = {}
        self.t_opre = {}
        self.t_wopre = {}
        self._ctx = []
        sb = self._sb = lambda n, s, dt=F32: self._keep(nc.sbuf_tensor(n, s, dt))
        self.AR = Arena(sb("arena", [128, ARENA]))
        self.rows_a = sb("rows_a", [128, 2048])
        self.rows_b = sb("rows_b", [128, 2048])
        self.rows_c = sb("rows_c", [128, 2048])
        self.states_a = sb("states_a", [128, 512])
        self.states_b = sb("states_b", [128, 8])
        self.tails = sb("tails", [128, 48])
        self.eye = sb("eye", [128, 128], BF16)
        self.swap = sb("swap", [128, 128], BF16)
        self.Es = sb("Es", [16, 2048], BF16)
        self.ones = sb("ones", [128, 2])
        self.orow_bf = sb("orow_bf", [128, 128], BF16)
        self.ones_bf = sb("ones_bf", [128, 2], BF16)
        self.hsel = sb("hsel", [128, 2], BF16)
        self.s2b = sb("s2b", [2, 128])
        self.s2b_bf = sb("s2b_bf", [2, 128], BF16)
        self.rsel = sb("rsel", [2, 256], BF16)
        self.aw_conv = sb("aw_conv", [128, 64])
        self.aw_cvB = sb("aw_cvB", [128, 16])
        self.aw_dtB = sb("aw_dtB", [128, 16])
        self.aw_D = sb("aw_D", [128, 16])
        self.fw_qs = sb("fw_qs", [128, 1])
        self.fw_ks = sb("fw_ks", [128, 1])
        self.col16 = sb("col16", [128, 16])
        self.col8a = sb("col8a", [128, 8])
        self.col8b = sb("col8b", [128, 8])
        self.col8c = sb("col8c", [128, 8])
        self.col8d = sb("col8d", [128, 8])
        self.konst = sb("konst", [128, 4])
        self.negb_t = sb("negb_t", [2, 512], BF16)
        self.bc_rows = sb("bc_rows", [2, 1024])
        self._t_bc = None
        self.PS = self._keep(nc.psum_tensor("PS", [128, 4096], F32))
        self.sems = {}

    def _keep(self, cm):
        v = cm.__enter__()
        self._ctx.append(cm)
        return v.ap() if hasattr(v, "ap") else v

    def pb(self, i, rows=128, cols=512):
        return self.PS[0:rows, 512 * i:512 * i + cols]

    def load_consts(self):
        p = self.p
        tl = []
        for dst, src in [(self.eye, "c_eye"), (self.swap, "c_swap"), (self.Es, "c_Es"),
                         (self.ones, "c_ones"), (self.ones_bf, "c_ones_bf"),
                         (self.orow_bf, "c_orow_bf"),
                         (self.hsel, "c_hsel"), (self.s2b, "c_s2b"),
                         (self.s2b_bf, "c_s2b_bf"), (self.rsel, "c_rsel")]:
            tl.append(p.dma(dst[:], self.dr[src][:]))
        self.t_consts = tuple(tl)
        p.dve.do(lambda e: e.memset(self.konst[:, 0:1], float(EPS)))
        p.dve.do(lambda e: e.memset(self.konst[:, 1:2], float(np.log(QK_SCALE))))
        p.dve.do(lambda e: e.memset(self.states_a[:], 0.0))
        self.t_k0 = p.dve.do(lambda e: e.memset(self.states_b[:], 0.0), inc=1)

    def row(self, tensor, slot, cols=slice(0, T), nrows=1):
        return tensor[32 * slot:32 * slot + nrows, cols]

    def emit_stats(self, src_hbm, mu_row, r_row, eps, n=1024.0, sig_row=None):
        p, A = self.p, self.AR
        A.push()
        xt = [A.alloc(512) for _ in range(2)]
        xb = [A.alloc_bf(512) for _ in range(2)]
        sq = [A.alloc_bf(512) for _ in range(2)]
        tr = A.alloc(1024)
        t_cons = {0: None, 1: None}
        prev_read = None
        for tc in range(4):
            cs = slice(512 * tc, 512 * (tc + 1))
            s0 = self.pb(6, 1); s1 = self.pb(7, 1)
            tok_mm = None
            for g in range(8):
                i = g % 2
                td = p.dma(xt[i][:], src_hbm[g * 128:(g + 1) * 128, cs], t_cons[i])
                tcast = p.pool.do(lambda e, i=i: e.tensor_copy(xb[i][:, 0:512], xt[i][:]), td, inc=1)
                tsq = p.act.do(lambda e, i=i: e.activation(sq[i][:, 0:512], xt[i][:], AF.Square),
                               td, inc=1)
                deps = [tcast, self.t_consts] + ([prev_read] if g == 0 else [])
                p.pe.do(lambda e, i=i, g=g: e.matmul(
                    s0, self.ones_bf[:, 0:1], xb[i][:, 0:512], start=(g == 0), stop=(g == 7)),
                    *deps)
                tok_mm = p.pe.do(lambda e, i=i, g=g: e.matmul(
                    s1, self.ones_bf[:, 0:1], sq[i][:, 0:512], start=(g == 0), stop=(g == 7)),
                    tsq, inc=1)
                t_cons[i] = tok_mm
            mu = mu_row[:, cs]; rr = r_row[:, cs]
            va = tr[0:1, 0:512]; vb = tr[0:1, 512:1024]
            p.dve.do(lambda e, mu=mu, s0=s0: e.tensor_scalar_mul(mu, s0, 1.0 / n), tok_mm)
            p.dve.do(lambda e, s1=s1: e.tensor_scalar_mul(va, s1, 1.0 / n))
            p.dve.do(lambda e, mu=mu: e.tensor_tensor(vb, mu, mu, op=ALU.mult))
            prev_read = p.dve.do(lambda e: e.tensor_sub(va, va, vb), inc=1)
            p.act.do(lambda e: e.activation(vb, va, AF.Ln, bias=self.konst[0:1, 0:1]),
                     prev_read, self.t_k0)
            ts_ = p.act.do(lambda e, rr=rr: e.activation(rr, vb, AF.Exp, scale=-0.5), inc=1)
            prev_read = (prev_read[0], prev_read[1])
            p.dve.do(lambda e: e.engine_nop(), ts_, inc=1)
        A.pop()
        return p.dve.tok()

    def bcast_row(self, bank, row_ap, deps=(), rows=128):
        p = self.p
        n = row_ap.shape[-1]
        np_ = row_ap.shape[0]
        hi = self.bc_rows[0:np_, 0:256].bitcast(BF16)[:, 0:n]
        lo = self.bc_rows[0:np_, 256:512].bitcast(BF16)[:, 0:n]
        tmp = self.bc_rows[0:np_, 512:512 + n]
        p.dve.do(lambda e: e.tensor_copy(tmp, row_ap),
                 *deps, *([self._t_bc] if self._t_bc else []))
        p.dve.do(lambda e: e.tensor_copy(hi, tmp))
        p.dve.do(lambda e: e.tensor_tensor(tmp, tmp, hi, op=ALU.subtract))
        t2 = p.dve.do(lambda e: e.tensor_copy(lo, tmp), inc=1)
        sel = self.orow_bf[0:1, 0:rows] if np_ == 1 else self.s2b_bf[0:2, 0:rows]
        p.pe.do(lambda e: e.matmul(self.pb(bank, rows, n), sel, hi,
                                   start=True, stop=False), t2, *deps, inc=0)
        tok = p.pe.do(lambda e: e.matmul(self.pb(bank, rows, n), sel, lo,
                                         start=False, stop=True), inc=1)
        self._t_bc = tok
        return tok

    def bank_write(self, i):
        return self._bank_r.get(i, [])

    def bank_written(self, i, tok):
        self._bank_w[i] = tok
        self._bank_r[i] = []

    def bank_read(self, i):
        return self._bank_w.get(i)

    def bank_did_read(self, i, tok):
        self._bank_r.setdefault(i, []).append(tok)

    def emit_attn_a(self):
        p, A = self.p, self.AR
        self._bank_r, self._bank_w = {}, {}
        A.push()
        self.aw_xp1 = A.alloc_bf(1280)[:, 0:1280]
        self.aw_xp2 = A.alloc_bf(256)[:, 0:256]
        self.aw_dt = A.alloc_bf(2048)[0:64, 0:2048]
        self.aw_dt_st = A.alloc(2048)
        self.aw_A = A.alloc(256)
        self.aw_At = A.alloc(256)
        xcf = [A.alloc_bf(518) for _ in range(NDI)]
        Bb = A.alloc_bf(16 * 512)
        Cb = A.alloc_bf(16 * 512)
        siluz = A.alloc_bf(16 * 512)
        gated = A.alloc_bf(16 * 512)
        NHR = 4
        hring = A.alloc(NHR * 512)
        dtqb = A.alloc_bf(512)
        brow = A.alloc_bf(512)
        crow = A.alloc_bf(512)
        xt2 = [A.alloc(512) for _ in range(2)]
        ot2 = [A.alloc(512) for _ in range(2)]
        SH = A.alloc(6144)
        xnq = SH[:, 0:2048].bitcast(BF16)
        wstage = SH[:, 2048:4096]
        wbf = [SH[:, 4096:5120].bitcast(BF16), SH[:, 5120:6144].bitcast(BF16)]
        delta = SH[:, 0:512]; du = SH[:, 512:1024]
        spA = SH[:, 1024:1536]; spB = SH[:, 1536:2048]; sp4 = SH[:, 2048:2560]
        g1b = SH[:, 2560:3072]
        dAb = [SH[:, 3072:3584], SH[:, 3584:4096], xt2[0], xt2[1]]
        xov = A.alloc(1024)
        du2 = A.alloc(1024)
        dX2 = [SH[:, 4096:5120], xov]
        ypb2 = [SH[:, 5120:5632].bitcast(BF16), SH[:, 5632:6144].bitcast(BF16)]
        wst2 = [wstage, A.alloc(2048)]
        cc0, cc1 = ot2[0], ot2[1]

        def xc(d):
            return xcf[d][:, 0:518]

        mu1 = self.rows_a[0:1, :]; r1 = self.rows_a[32:33, :]
        t_stats = self.emit_stats(self.dr["xT"], mu1, r1, EPS)
        self._bank_r[6] = [t_stats]
        self._bank_r[7] = [t_stats]

        t_xt2c = {0: None, 1: None}
        t_yp2c = {0: None, 1: None}
        t_dAc = {0: None, 1: None, 2: None, 3: None}
        t_dX2c = {0: None, 1: None}
        t_hr = {i: None for i in range(NHR)}
        t_hrp = {i: None for i in range(NHR)}
        t_wbfc = {0: None, 1: None}
        t_ot2c = {0: None, 1: None}
        t_sZ = {}
        t_gat = {}
        t_conv_done = {}

        for dir_ in ("f", "b"):
            fwd = dir_ == "f"
            sbase = 0 if fwd else 256
            tw = {}
            for dst, key in [(self.aw_dt_st[0:64, :], f"a{dir_}_dtW"), (self.aw_conv, f"a{dir_}_convW"),
                             (self.aw_cvB, f"a{dir_}_convB"), (self.aw_dtB, f"a{dir_}_dtB"),
                             (self.aw_D, f"a{dir_}_D"), (self.aw_At, f"a{dir_}_Alog")]:
                tw[key.split("_", 1)[1]] = p.dma(dst[:], self.dr[key][:])
            t1 = p.dma(wstage[:, 0:1280], self.dr[f"a{dir_}_xpW1"][:])
            t2 = p.dma(wstage[:, 1280:1536], self.dr[f"a{dir_}_xpW2"][:])
            tw["xpW1"] = p.pool.do(lambda e: e.tensor_copy(self.aw_xp1[:], wstage[:, 0:1280]),
                                   t1, inc=1)
            tw["xpW2"] = p.pool.do(lambda e: e.tensor_copy(self.aw_xp2[:], wstage[:, 1280:1536]),
                                   t2, inc=1)
            tw["dtW"] = p.pool.do(lambda e: e.tensor_copy(self.aw_dt[:], self.aw_dt_st[0:64, :]),
                                  tw["dtW"], inc=1)
            ta = p.act.do(lambda e: e.activation(self.aw_A[:], self.aw_At[:], AF.Exp),
                          tw["Alog"], inc=1)
            t_A = p.dve.do(lambda e: e.tensor_scalar_mul(self.aw_A[:], self.aw_A[:], -1.0),
                           ta, inc=1)
            tz = p.dve.do(lambda e: e.memset(self.tails[:], 0.0), inc=1)

            quarters = range(4) if fwd else range(3, -1, -1)
            for qi, q in enumerate(quarters):
                if (0 if fwd else 4) + qi >= AA_LIM:
                    break
                cs = slice(512 * q, 512 * (q + 1))
                tmu = self.bcast_row(6, mu1[:, cs], (self.t_consts, *self.bank_write(6)))
                self.bank_written(6, tmu)
                trr = self.bcast_row(7, r1[:, cs], tuple(self.bank_write(7)))
                self.bank_written(7, trr)
                t_xn = None
                for g in range(8):
                    i = g % 2
                    td = p.dma(xt2[i][:], self.dr["xT"][g * 128:(g + 1) * 128, cs], t_xt2c[i])
                    p.dve.do(lambda e, i=i: e.tensor_tensor(
                        xt2[i][:], xt2[i][:], self.pb(6), op=ALU.subtract),
                        td, tmu)
                    t_xn = p.dve.do(lambda e, g=g, i=i: e.tensor_tensor(
                        xnq[:, g * 512:(g + 1) * 512], xt2[i][:], self.pb(7), op=ALU.mult),
                        trr, inc=1)
                    t_xt2c[i] = t_xn
                self.bank_did_read(6, t_xn)
                self.bank_did_read(7, t_xn)

                if AA_PH < 1:
                    continue
                wst_war = {0: [tw["xpW1"], tw["xpW2"]], 1: None}
                tdw_pair = {}
                d_i = 0 if fwd else 1
                for og in range(32):
                    i = og % 2
                    bank = i
                    keya = (dir_, og)
                    if keya not in self.t_apre:
                        if og % 2 == 0:
                            pg = og // 2
                            sb_ = pg % 2
                            tdw_pair[sb_] = p.dma(
                                wst2[sb_][:, 0:2048].rearrange("p (a x) -> p a x", a=2),
                                self.dr[f"a{dir_}_inW"][og:og + 2]
                                    .transpose([1, 0, 2, 3])
                                    .rearrange("p a b c -> p a (b c)"),
                                wst_war[sb_])
                        sb_ = (og // 2) % 2
                        tcast = p.act.do(lambda e, i=i, sb_=sb_, half=og % 2: e.copy(
                            wbf[i][:, 0:1024],
                            wst2[sb_][:, half * 1024:half * 1024 + 1024]),
                            tdw_pair[sb_], t_wbfc[i],
                            getattr(self, "_t_app%d" % i, None), inc=1)
                        wst_war[sb_] = tcast
                        r0w = (d_i * 32 + og) * 128
                        tst_ = p.dma(self.wprea[r0w:r0w + 128, :],
                                     wbf[i][:, 0:1024], tcast)
                        self.t_apre[keya] = tst_
                        setattr(self, "_t_app%d" % i, tst_)
                    else:
                        r0w = (d_i * 32 + og) * 128
                        tcast = p.dma(wbf[i][:, 0:1024], self.wprea[r0w:r0w + 128, :],
                                      self.t_apre[keya], t_wbfc[i],
                                      getattr(self, "_t_app%d" % i, None))
                    tm = None
                    for kc in range(8):
                        def fmm(e, kc=kc, i=i, bank=bank):
                            return e.matmul(self.pb(bank), wbf[i][:, kc * 128:(kc + 1) * 128],
                                            xnq[:, kc * 512:(kc + 1) * 512],
                                            start=(kc == 0), stop=(kc == 7))
                        deps = [tcast, t_xn] + self.bank_write(bank) if kc == 0 else []
                        tm = p.pe.do(fmm, *deps, inc=(1 if kc == 7 else 0))
                    t_wbfc[i] = tm
                    self.bank_written(bank, tm)
                    if og < 16:
                        d = og
                        te = p.act.do(lambda e, d=d, bank=bank: e.copy(
                            xc(d)[:, 3:515], self.pb(bank)),
                            tm, t_conv_done.get(d), inc=1)
                        self.bank_did_read(bank, te)
                        if fwd:
                            tt1 = p.dve.do(lambda e, d=d: e.tensor_copy(
                                xc(d)[:, 0:3], self.tails[:, 3 * d:3 * d + 3].bitcast(BF16)[:, 0:3]),
                                te, tz, inc=1)
                            tt2 = p.dve.do(lambda e, d=d: e.tensor_copy(
                                self.tails[:, 3 * d:3 * d + 3].bitcast(BF16)[:, 0:3],
                                xc(d)[:, 512:515]), inc=1)
                        else:
                            tt1 = p.dve.do(lambda e, d=d: e.tensor_copy(
                                xc(d)[:, 515:518], self.tails[:, 3 * d:3 * d + 3].bitcast(BF16)[:, 0:3]),
                                te, tz, inc=1)
                            tt2 = p.dve.do(lambda e, d=d: e.tensor_copy(
                                self.tails[:, 3 * d:3 * d + 3].bitcast(BF16)[:, 0:3],
                                xc(d)[:, 3:6]), inc=1)
                        base = 0 if fwd else 3
                        tp = p.dve.do(lambda e, d=d, base=base: e.tensor_scalar_mul(
                            cc0[:], xc(d)[:, base:base + 512],
                            self.aw_conv[:, 4 * d:4 * d + 1]),
                            tt2, tw["convW"], t_ot2c[0], inc=1)
                        tp = p.dve.do(lambda e, d=d, base=base: e.scalar_tensor_tensor(
                            cc1[:], xc(d)[:, base + 1:base + 513],
                            self.aw_conv[:, 4 * d + 1:4 * d + 2], cc0[:],
                            op0=ALU.mult, op1=ALU.add), t_ot2c[1], inc=1)
                        tp = p.dve.do(lambda e, d=d, base=base: e.scalar_tensor_tensor(
                            cc0[:], xc(d)[:, base + 2:base + 514],
                            self.aw_conv[:, 4 * d + 2:4 * d + 3], cc1[:],
                            op0=ALU.mult, op1=ALU.add), inc=1)
                        tp = p.dve.do(lambda e, d=d, base=base: e.scalar_tensor_tensor(
                            cc1[:], xc(d)[:, base + 3:base + 515],
                            self.aw_conv[:, 4 * d + 3:4 * d + 4], cc0[:],
                            op0=ALU.mult, op1=ALU.add), inc=1)
                        t_ot2c[0] = tp
                        t_ot2c[1] = tp
                        tcv = p.act.do(lambda e, d=d: e.activation(
                            xc(d)[:, 3:515], cc1[:], AF.Silu,
                            bias=self.aw_cvB[:, d:d + 1]), tp, tw["convB"], inc=1)
                        t_conv_done[d] = tcv
                    else:
                        d = og - 16
                        te = p.act.do(lambda e, d=d, bank=bank: e.activation(
                            siluz[:, d * 512:(d + 1) * 512], self.pb(bank), AF.Silu),
                            tm, t_sZ.get(d), inc=1)
                        self.bank_did_read(bank, te)
                        t_sZ[d] = te
                t_inproj_w_done = [t_wbfc[0], t_wbfc[1], wst_war[0], wst_war[1],
                                   getattr(self, "_t_app0", None),
                                   getattr(self, "_t_app1", None)]

                if AA_PH < 2:
                    continue
                tx1 = tx2 = None
                for d in range(NDI):
                    deps1 = [t_conv_done[d], tw["xpW1"]] + (self.bank_write(2) if d == 0 else [])
                    tx1 = p.pe.do(lambda e, d=d: e.matmul(
                        self.pb(2, 80), self.aw_xp1[:, d * 80:(d + 1) * 80],
                        xc(d)[:, 3:515], start=(d == 0), stop=(d == NDI - 1)),
                        *deps1, inc=(1 if d == NDI - 1 else 0))
                    deps2 = [tw["xpW2"]] + (self.bank_write(3) if d == 0 else [])
                    tx2 = p.pe.do(lambda e, d=d: e.matmul(
                        self.pb(3, 16), self.aw_xp2[:, d * 16:(d + 1) * 16],
                        xc(d)[:, 3:515], start=(d == 0), stop=(d == NDI - 1)),
                        *deps2, inc=(1 if d == NDI - 1 else 0))
                self.bank_written(2, tx1)
                self.bank_written(3, tx2)
                tdq = p.act.do(lambda e: e.copy(dtqb[0:64, 0:512], self.pb(2, 80)[0:64, :]),
                               tx1, inc=1)
                tbr = p.act.do(lambda e: e.copy(brow[0:16, 0:512], self.pb(2, 80)[64:80, :]),
                               inc=1)
                self.bank_did_read(2, tbr)
                tcr = p.act.do(lambda e: e.copy(crow[0:16, 0:512], self.pb(3, 16)),
                               tx2, inc=1)
                self.bank_did_read(3, tcr)

                if AA_PH < 3:
                    continue
                t_bc = {}
                for which, rowt, tdep, dst in (("B", brow, tbr, Bb), ("C", crow, tcr, Cb)):
                    for s in range(16):
                        bank = 2 + (s % 2)
                        tb = p.pe.do(lambda e, s=s, bank=bank, rowt=rowt: e.matmul(
                            self.pb(bank), self.Es[:, s * 128:(s + 1) * 128],
                            rowt[0:16, 0:512], start=True, stop=True),
                            tdep, self.t_consts, *self.bank_write(bank), inc=1)
                        self.bank_written(bank, tb)
                        tb2 = p.act.do(lambda e, s=s, bank=bank, dst=dst: e.copy(
                            dst[:, s * 512:(s + 1) * 512], self.pb(bank)),
                            tb, t_bc.get((which, s)), inc=1)
                        self.bank_did_read(bank, tb2)
                        t_bc[(which, s)] = tb2
                t_Ball = p.act.tok()

                if AA_PH < 4:
                    continue
                t_dAc[2], t_dAc[3] = t_xt2c[0], t_xt2c[1]
                tglast = None
                for d in range(NDI):
                    if d >= AA_D:
                        continue
                    dd = ([tdq, tw["dtW"]] if d == 0 else []) + self.bank_write(4)
                    tdt = tr4 = trA = trC = tdl = tdu = None
                    if AA_S & 1:
                        tdt = p.pe.do(lambda e, d=d: e.matmul(
                            self.pb(4), self.aw_dt[:, d * 128:(d + 1) * 128], dtqb[0:64, 0:512],
                            start=True, stop=True), *dd, inc=1)
                        self.bank_written(4, tdt)
                    dcol = self.aw_dtB[:, d:d + 1]
                    aliased = t_inproj_w_done if d == 0 else []
                    if AA_S & 2:
                        tr4 = p.dve.do(lambda e, dcol=dcol: e.tensor_scalar(
                            sp4[:], self.pb(4), dcol, 0.0, op0=ALU.add, op1=ALU.max),
                            tdt, tw["dtB"], *aliased, inc=1)
                        self.bank_did_read(4, tr4)
                    if AA_S & 4:
                        trA = p.act.do(lambda e, dcol=dcol: e.activation(
                            spA[:], self.pb(4), AF.Abs, bias=dcol),
                            tdt, tr4, *aliased, inc=1)
                        self.bank_did_read(4, trA)
                        p.act.do(lambda e: e.activation(spB[:], spA[:], AF.Exp, scale=-1.0))
                        trC = p.act.do(lambda e: e.activation(spA[:], spB[:], AF.Ln, bias=1.0), inc=1)
                    if AA_S & 8:
                        tdl = p.dve.do(lambda e: e.tensor_add(delta[:], sp4[:], spA[:]),
                                       trC, tr4, inc=1)
                    if AA_S & 16:
                        p.pool.do(lambda e, d=d: e.tensor_tensor(
                            du2[:, 0:512], delta[:], xc(d)[:, 3:515], op=ALU.mult),
                            tdl, *aliased)
                        tdu = p.pool.do(lambda e, d=d: e.tensor_tensor(
                            du2[:, 512:1024], delta[:], xc(d)[:, 3:515], op=ALU.mult),
                            inc=1)
                    if AA_G < 1:
                        continue
                    for s in range(0, 16, 2):
                        i0, i1 = s % 4, (s + 1) % 4
                        ip = (s // 2) % 2
                        hi = s % NHR
                        tdx = p.pool.do(lambda e, s=s, ip=ip: e.tensor_tensor(
                            dX2[ip][:], du2[:], Bb[:, s * 512:(s + 2) * 512], op=ALU.mult),
                            tdu, t_Ball, t_dX2c[ip], inc=1)
                        acol0 = self.aw_A[:, d * 16 + s:d * 16 + s + 1]
                        acol1 = self.aw_A[:, d * 16 + s + 1:d * 16 + s + 2]
                        tda0 = p.act.do(lambda e, i0=i0, acol0=acol0: e.activation(
                            dAb[i0][:], delta[:], AF.Exp, scale=acol0),
                            tdl, t_A, t_dAc[i0], inc=1)
                        tda1 = p.act.do(lambda e, i1=i1, acol1=acol1: e.activation(
                            dAb[i1][:], delta[:], AF.Exp, scale=acol1),
                            t_dAc[i1], inc=1)
                        scol0 = self.states_a[:, sbase + d * 16 + s:sbase + d * 16 + s + 1]
                        scol1 = self.states_a[:, sbase + d * 16 + s + 1:sbase + d * 16 + s + 2]
                        if AA_G < 2:
                            continue
                        if fwd:
                            tsc0 = p.dve.do(lambda e, i0=i0, hi=hi, ip=ip, scol0=scol0:
                                            e.tensor_tensor_scan(
                                hring[:, hi * 512:(hi + 1) * 512], dAb[i0][:],
                                dX2[ip][:, 0:512], scol0, op0=ALU.mult, op1=ALU.add),
                                tda0, tdx, t_hr[hi], t_hrp[hi], inc=1)
                            tsc1 = p.dve.do(lambda e, i1=i1, hi=hi, ip=ip, scol1=scol1:
                                            e.tensor_tensor_scan(
                                hring[:, (hi + 1) * 512:(hi + 2) * 512], dAb[i1][:],
                                dX2[ip][:, 512:1024], scol1, op0=ALU.mult, op1=ALU.add),
                                tda1, t_hr[hi + 1], t_hrp[hi + 1], inc=1)
                        else:
                            tsc0 = p.dve.do(lambda e, i0=i0, hi=hi, ip=ip, scol0=scol0:
                                            e.tensor_tensor_scan(
                                hring[:, hi * 512:(hi + 1) * 512][:, ::-1],
                                dAb[i0][:, ::-1], dX2[ip][:, 0:512][:, ::-1],
                                scol0, op0=ALU.mult, op1=ALU.add),
                                tda0, tdx, t_hr[hi], t_hrp[hi], inc=1)
                            tsc1 = p.dve.do(lambda e, i1=i1, hi=hi, ip=ip, scol1=scol1:
                                            e.tensor_tensor_scan(
                                hring[:, (hi + 1) * 512:(hi + 2) * 512][:, ::-1],
                                dAb[i1][:, ::-1], dX2[ip][:, 512:1024][:, ::-1],
                                scol1, op0=ALU.mult, op1=ALU.add),
                                tda1, t_hr[hi + 1], t_hrp[hi + 1], inc=1)
                        t_dAc[i0] = tsc0
                        t_dAc[i1] = tsc1
                        t_dX2c[ip] = tsc1
                        if AA_G < 3:
                            continue
                        ypeng = p.pool if (s // 2) % 2 == 0 else p.dve
                        typ = ypeng.do(lambda e, s=s, ip=ip, hi=hi: e.tensor_tensor(
                            ypb2[ip][:], hring[:, hi * 512:(hi + 2) * 512],
                            Cb[:, s * 512:(s + 2) * 512], op=ALU.mult),
                            tsc0, tsc1, t_yp2c[ip], inc=1)
                        t_hrp[hi] = typ
                        t_hrp[hi + 1] = typ
                        p.pe.do(lambda e, ip=ip, s=s: e.matmul(
                            self.pb(5), self.eye[:], ypb2[ip][:, 0:512],
                            start=(s == 0), stop=False),
                            typ, *(self.bank_write(5) + [self.t_consts] if s == 0 else []))
                        tya = p.pe.do(lambda e, ip=ip, s=s: e.matmul(
                            self.pb(5), self.eye[:], ypb2[ip][:, 512:1024],
                            start=False, stop=(s == 14)), inc=1)
                        t_yp2c[ip] = tya
                        if AA_G < 4:
                            continue
                        s1 = s + 1
                        hi2 = hi + 1
                        if hi2 == NHR - 1 or s1 == 15:
                            n = hi2 + 1
                            off = 511 if fwd else 0
                            s0_ = s1 - hi2
                            def fsv(e, d=d, s0_=s0_, n=n, off=off):
                                return e.tensor_copy(
                                    self.states_a[:, sbase + d * 16 + s0_:sbase + d * 16 + s0_ + n],
                                    hring[:, off::512][:, 0:n])
                            tsv = p.dve.do(fsv, inc=1)
                            for hh in range(n):
                                t_hr[hh] = tsv
                    if AA_G < 5:
                        continue
                    self.bank_written(5, tya)
                    tg1 = p.dve.do(lambda e, d=d: e.scalar_tensor_tensor(
                        g1b[:], xc(d)[:, 3:515], self.aw_D[:, d:d + 1], self.pb(5),
                        op0=ALU.mult, op1=ALU.add),
                        tya, tw["D"], t_gat.get(d), inc=1)
                    self.bank_did_read(5, tg1)
                    tgt = p.pool.do(lambda e, d=d: e.tensor_tensor(
                        gated[:, d * 512:(d + 1) * 512], g1b[:], siluz[:, d * 512:(d + 1) * 512],
                        op=ALU.mult),
                        tg1, t_sZ.get(d), t_gat.get(d), inc=1)
                    t_gat[d] = tgt
                    tglast = tgt
                t_scan_end = [t for t in [p.dve.tok(), t_gat.get(15), p.act.tok(),
                                          t_yp2c.get(0), t_yp2c.get(1)] if t]

                if AA_PH < 5:
                    continue
                ost_war = {0: t_scan_end, 1: t_scan_end}
                d_i = 0 if fwd else 1
                for og in range(8):
                    i = og % 2
                    bank = i
                    keyo = (dir_, og)
                    r0w = (d_i * 8 + og) * 128
                    if keyo not in self.t_opre:
                        sb_ = og % 2
                        dd = ost_war[sb_]
                        if not isinstance(dd, list):
                            dd = [dd]
                        tdw = p.dma(wst2[sb_][:, 0:2048],
                                    self.dr[f"a{dir_}_outW"][og]
                                        .rearrange("p a b -> p (a b)"),
                                    *[x for x in dd if x])
                        tcast = p.act.do(lambda e, i=i, sb_=sb_: e.copy(
                            wbf[i][:, 0:2048], wst2[sb_][:, 0:2048]), tdw, t_wbfc[i],
                            getattr(self, "_t_opp%d" % i, None),
                            getattr(self, "_t_app%d" % i, None), inc=1)
                        ost_war[sb_] = tcast
                        tso = p.dma(self.wpreo[r0w:r0w + 128, :],
                                    wbf[i][:, 0:2048], tcast)
                        self.t_opre[keyo] = tso
                        setattr(self, "_t_opp%d" % i, tso)
                    else:
                        tcast = p.dma(wbf[i][:, 0:2048], self.wpreo[r0w:r0w + 128, :],
                                      self.t_opre[keyo], t_wbfc[i],
                                      getattr(self, "_t_opp%d" % i, None),
                                      getattr(self, "_t_app%d" % i, None),
                                      *([x for x in (t_scan_end if isinstance(
                                          t_scan_end, list) else [t_scan_end]) if x]
                                        if og < 2 else []))
                    tm = None
                    for kc in range(16):
                        def fmm(e, kc=kc, i=i, bank=bank):
                            return e.matmul(self.pb(bank), wbf[i][:, kc * 128:(kc + 1) * 128],
                                            gated[:, kc * 512:(kc + 1) * 512],
                                            start=(kc == 0), stop=(kc == 15))
                        deps = [tcast, t_gat.get(15)] + self.bank_write(bank) if kc == 0 else []
                        tm = p.pe.do(fmm, *deps, inc=(1 if kc == 15 else 0))
                    t_wbfc[i] = tm
                    self.bank_written(bank, tm)
                    rows = slice(og * 128, (og + 1) * 128)
                    if fwd:
                        tev = p.act.do(lambda e, i=i, bank=bank: e.copy(ot2[i][:], self.pb(bank)),
                                       tm, t_ot2c[i], inc=1)
                        self.bank_did_read(bank, tev)
                        tst = p.dma(self.hbm_f[rows, cs], ot2[i][:], tev)
                        t_ot2c[i] = tst
                    else:
                        tx = p.dma(xt2[i][:], self.dr["xT"][rows, cs], t_xt2c[i])
                        tf = p.dma(ot2[i][:], self.hbm_f[rows, cs], t_ot2c[i])
                        p.dve.do(lambda e, i=i, bank=bank: e.tensor_tensor(
                            xt2[i][:], xt2[i][:], self.pb(bank), op=ALU.add),
                            tx, tm)
                        ta2 = p.dve.do(lambda e, i=i: e.tensor_tensor(
                            xt2[i][:], xt2[i][:], ot2[i][:], op=ALU.add), tf, inc=1)
                        self.bank_did_read(bank, ta2)
                        tst = p.dma(self.hbm_s[rows, cs], xt2[i][:], ta2)
                        t_xt2c[i] = tst
                        t_ot2c[i] = ta2
                p.barrier()
        A.pop()

    def emit_hbm_copy(self, dst, src):
        p, A = self.p, self.AR
        A.push()
        buf = [A.alloc(2048) for _ in range(2)]
        tc_ = {0: None, 1: None}
        for g in range(8):
            i = g % 2
            rows = slice(g * 128, (g + 1) * 128)
            td = p.dma(buf[i][:], src[rows, :], tc_[i])
            ts_ = p.dma(dst[rows, :], buf[i][:], td)
            tc_[i] = ts_
        A.pop()

    def emit_dump(self, slot, src_tiles):
        p = self.p
        p.barrier()
        for g in range(8):
            p.dma(self.dbgT[slot * D + g * 128:slot * D + (g + 1) * 128, :],
                  src_tiles(g))
        p.barrier()

    def finish(self):
        self.p.barrier()


def build_nc(shared, upto):
    nc = bass.Bass("TRN2", target_bir_lowering=False,
                   detect_race_conditions=bool(int(os.environ.get("RACE", "0"))))
    b = Builder(nc, shared, upto)
    sems = {}
    cms = []
    for k in ("pe", "act", "dve", "pool", "sp",
              "d0", "d1", "d2", "d3", "d4", "d5", "d6", "d7"):
        cm = nc.semaphore("S_" + k)
        sems[k] = cm.__enter__()
        cms.append(cm)
    blk = nc.Block()
    block = blk.__enter__()
    b.load_consts()
    b.emit_attn_a()
    if FF_DBG & 8 and upto != "attn_a":
        b.p.barrier()
        b.emit_hbm_copy(b.dbgT[6 * D:7 * D, :], b.hbm_s)
        b.p.barrier()
    if upto == "attn_a":
        b.emit_hbm_copy(b.out, b.hbm_s)
    elif upto == "ffn1":
        b.emit_ffn(1, b.hbm_s, None)
    elif upto == "attn_b":
        b.emit_ffn(1, b.hbm_s, b.hbm_f)
        b.emit_attn_b(b.hbm_f, b.out)
    else:
        b.emit_ffn(1, b.hbm_s, b.hbm_f)
        b.emit_attn_b(b.hbm_f, b.hbm_s)
        b.emit_ffn(2, b.hbm_s, None)
    b.finish()
    b.p.emit(block, sems)
    blk.__exit__(None, None, None)
    for cm in cms:
        cm.__exit__(None, None, None)
    return b, nc


def kernel(**inputs):
    key = "full"
    shared = stage_shared(inputs)
    if key not in _built:
        b, nc = build_nc(shared, key)
        _built[key] = nc
    nc = _built[key]
    xs = np.asarray(inputs["L0_attn_a_x"], np.float32)
    in_maps = []
    for c in range(B):
        m = dict(shared)
        m["xT"] = np.ascontiguousarray(xs[c].T)
        in_maps.append(m)
    res = run_bass_kernel_spmd(nc, in_maps, core_ids=list(range(B)))
    out = np.stack([np.asarray(r["outT"]).T for r in res.results], axis=0)
    return out


def emit_ffn(self, app, src_hbm, dst_hbm):
    p, A = self.p, self.AR
    dst = dst_hbm if dst_hbm is not None else self.out
    self._bank_r, self._bank_w = {}, {}
    self._t_bc = None
    self._t_ftc = {0: None, 1: None}
    A.push()
    ft = A.alloc(16384)
    ftv = lambda g: ft[:, g * 2048:(g + 1) * 2048]
    tr1 = A.alloc(512); tr2 = A.alloc(512); tr3 = A.alloc(512)
    trb = A.alloc_bf(1024)
    rnt = A.alloc(512)
    rnt_bf = rnt.bitcast(BF16)
    mu2 = self.rows_a[0:1, :]
    r2_ = self.rows_a[32:33, :]
    t_stats = self.emit_stats(src_hbm, mu2, r2_, EPS)
    U = A.alloc(23824)
    Ub = U.bitcast(BF16)
    ftbf = lambda k: Ub[:, k * 1026:(k + 1) * 1026]
    qh = lambda g: Ub[:, 8208 + g * 2048:8208 + (g + 1) * 2048]
    vt = lambda t: Ub[:, 24592 + t * 512:24592 + (t + 1) * 512]
    vw = lambda k: Ub[:, 32784 + k * 512:32784 + (k + 1) * 512]
    rot = Ub[:, 36880:40976]
    wstqk = U[:, 20488:21512]
    wbfqk = [Ub[:, 43024:44048], Ub[:, 44048:45072]]
    OT = lambda g: Ub[:, g * 2048:(g + 1) * 2048]
    msk = lambda j: Ub[:, 32784 + j * 512:32784 + (j + 1) * 512]
    tsb = [U[:, 18440:18952], U[:, 18952:19464]]
    ptb = [Ub[:, 38928:39440], Ub[:, 39440:39952]]
    rbsb = U[0:64, 19976:20488]
    hs2n = ftbf
    w2st = U[:, 4104:6920]
    w2bf = [Ub[:, 13840:16656], Ub[:, 39184:42000]]
    h2 = lambda k: Ub[:, 16656 + k * 1024:16656 + (k + 1) * 1024]
    w1st = U[:, 19592:21640]
    w1bf = [Ub[:, 43280:45328], Ub[:, 45328:47376]]
    qn_slot = [(self.rows_a, 64), (self.rows_b, 0), (self.rows_b, 32), (self.rows_b, 64)]
    kn_slot = [(self.rows_c, 0), (self.rows_c, 32), (self.rows_c, 64), (self.rows_a, 0)]
    row = lambda sl, cs, n=1: sl[0][sl[1]:sl[1] + n, cs]

    self._bank_r[6] = [t_stats]
    self._bank_r[7] = [t_stats]
    for tc in range(4):
        cs = slice(512 * tc, 512 * (tc + 1))
        tmu = self.bcast_row(6, mu2[:, cs], tuple(self.bank_write(6)) + (t_stats,))
        self.bank_written(6, tmu)
        trr = self.bcast_row(7, r2_[:, cs], tuple(self.bank_write(7)))
        self.bank_written(7, trr)
        tl = None
        if not hasattr(self, "_t_ftc"):
            self._t_ftc = {0: None, 1: None}
        tc_c = self._t_ftc
        for g in range(8):
            i = g % 2
            src = tr1 if i == 0 else tr2
            td = p.dma(src[:], src_hbm[g * 128:(g + 1) * 128, cs], tc_c[i])
            p.dve.do(lambda e, src=src: e.tensor_tensor(src[:], src[:], self.pb(6),
                                                        op=ALU.subtract), td, tmu)
            tl = p.dve.do(lambda e, g=g, cs=cs, src=src: e.tensor_tensor(
                ftv(g)[:, cs], src[:], self.pb(7), op=ALU.mult), trr, inc=1)
            tc_c[i] = tl
        self.bank_did_read(6, tl)
        self.bank_did_read(7, tl)
    p.barrier()
    if FF_DBG & 1:
        self.emit_dump(0, ftv)
        p.dma(self.dbgT[7 * D:7 * D + 128, :], self.rows_a[0:128, :])
        p.barrier()

    for l in range(FF_L):
        pf = f"f{l}_"
        tqs = p.dma(self.fw_qs[:], self.dr[pf + "qs"][:])
        tks = p.dma(self.fw_ks[:], self.dr[pf + "ks"][:])
        tlg = p.dma(self.col16[:, 0:8], self.dr[pf + "lng"][:])
        tlb = p.dma(self.col16[:, 8:16], self.dr[pf + "lnb"][:])
        tb1 = (p.dma(self.aw_dtB[:, 0:16], self.dr[pf + "b1a"][:, 0:16]),
               p.dma(self.aw_D[:, 0:6], self.dr[pf + "b1a"][:, 16:22]),
               p.dma(self.aw_cvB[:, 0:16], self.dr[pf + "b1g"][:, 0:16]),
               p.dma(self.aw_conv[:, 0:6], self.dr[pf + "b1g"][:, 16:22]))
        tb2 = p.dma(self.col8b[:], self.dr[pf + "b2"][:])
        lnb_bf = self.col8c[:, 0:4].bitcast(BF16)[:, 0:8]
        tlnb = p.pool.do(lambda e: e.tensor_copy(lnb_bf, self.col16[:, 8:16]), tlb, inc=1)
        t_vw = None
        for kc in range(8):
            tdv = p.dma(w2st[:, 0:512], self.dr[pf + "vW"][kc], t_vw)
            t_vw = p.pool.do(lambda e, kc=kc: e.tensor_copy(vw(kc), w2st[:, 0:512]),
                             tdv, inc=1)
        t_vw_all = t_vw

        for h in range(2):
            hc = slice(1024 * h, 1024 * (h + 1))
            trot = tuple(
                p.dma(rot[:, tb * 1024:(tb + 1) * 1024],
                      self.dr["c_rot"][:, tb * 2048 + 1024 * h:
                                       tb * 2048 + 1024 * h + 1024])
                for tb in range(4))
            for kc in range(8):
                p.pool.do(lambda e, kc=kc, hc=hc: e.tensor_copy(
                    ftbf(kc)[:, 1:1025], ftv(kc)[:, hc]))
                if h == 0:
                    p.pool.do(lambda e, kc=kc: e.memset(ftbf(kc)[:, 0:1], 0.0),
                              inc=(1 if kc == 7 else 0))
                else:
                    p.pool.do(lambda e, kc=kc: e.tensor_copy(
                        ftbf(kc)[:, 0:1], ftv(kc)[:, 1023:1024]),
                        inc=(1 if kc == 7 else 0))
            t_r1_all = p.pool.tok()
            p.barrier()

            for og in range(8):
                i = og % 2
                keyq = (l, og)
                if keyq not in self.t_qkpre:
                    tdw = p.dma(wstqk[:],
                                self.dr[pf + "qkW"][og].rearrange("p a b -> p (a b)"),
                                getattr(self, "_t_wstqk", None))
                    tcw = p.pool.do(lambda e, i=i: e.tensor_copy(
                        wbfqk[i][:, 0:1024], wstqk[:]),
                        tdw, getattr(self, "_t_wbq%d" % i, None),
                        getattr(self, "_t_qkpp%d" % i, None),
                        getattr(self, "_t_wopp%d" % i, None), inc=1)
                    self._t_wstqk = tcw
                    r0w = (l * 8 + og) * 128
                    tsq_ = p.dma(self.wpreq[r0w:r0w + 128, :], wbfqk[i][:, 0:1024], tcw)
                    self.t_qkpre[keyq] = tsq_
                    setattr(self, "_t_qkpp%d" % i, tsq_)
                else:
                    r0w = (l * 8 + og) * 128
                    tcw = p.dma(wbfqk[i][:, 0:1024], self.wpreq[r0w:r0w + 128, :],
                                self.t_qkpre[keyq],
                                getattr(self, "_t_wbq%d" % i, None),
                                getattr(self, "_t_qkpp%d" % i, None),
                                getattr(self, "_t_wopp%d" % i, None))
                g = og % 4
                isq = og < 4
                for tcl in range(2):
                    tc = 2 * h + tcl
                    bank = tcl % 2
                    tm = None
                    for kc in range(8):
                        def fmm(e, kc=kc, i=i, bank=bank, tcl=tcl):
                            sh = 1 if kc < 4 else 0
                            return e.matmul(self.pb(bank),
                                            wbfqk[i][:, kc * 128:(kc + 1) * 128],
                                            ftbf(kc)[:, tcl * 512 + sh:tcl * 512 + sh + 512],
                                            start=(kc == 0), stop=(kc == 7))
                        deps = ([tcw, t_r1_all, trot] + self.bank_write(bank)) if kc == 0 else []
                        tm = p.pe.do(fmm, *deps, inc=(1 if kc == 7 else 0))
                    setattr(self, "_t_wbq%d" % i, tm)
                    self.bank_written(bank, tm)
                    tsq = p.act.do(lambda e, bank=bank: e.activation(
                        trb[:, 0:512], self.pb(bank), AF.Square), tm,
                        getattr(self, "_t_trb0", None), inc=1)
                    self.bank_did_read(bank, tsq)
                    tn = p.pe.do(lambda e: e.matmul(self.pb(2, 2), self.hsel[:],
                                                    trb[:, 0:512], start=True, stop=True),
                                 tsq, self.t_consts, *self.bank_write(2), inc=1)
                    self.bank_written(2, tn)
                    tmx = p.dve.do(lambda e: e.tensor_scalar(
                        rnt[0:2, 0:512], self.pb(2, 2), 1e-24, None, op0=ALU.max),
                        tn, getattr(self, "_t_rnt", None), inc=1)
                    self.bank_did_read(2, tmx)
                    p.act.do(lambda e: e.activation(rnt[0:2, 0:512], rnt[0:2, 0:512], AF.Ln),
                             tmx, self.t_k0)
                    if isq:
                        tre = p.act.do(lambda e: e.activation(
                            rnt[0:2, 0:512], rnt[0:2, 0:512], AF.Exp, scale=-0.5,
                            bias=self.konst[0:2, 1:2]), inc=1)
                    else:
                        tre = p.act.do(lambda e: e.activation(
                            rnt[0:2, 0:512], rnt[0:2, 0:512], AF.Exp, scale=-0.5), inc=1)
                    trn = self.bcast_row(3, rnt[0:2, 0:512],
                                         tuple(self.bank_write(3)) + (tre,))
                    self.bank_written(3, trn)
                    self._t_rnt = trn
                    tev = p.act.do(lambda e: e.copy(tr1[:], self.pb(3)), trn,
                                   getattr(self, "_t_tr1f", None), inc=1)
                    self.bank_did_read(3, tev)
                    qs_col = self.fw_qs[:, 0:1] if isq else self.fw_ks[:, 0:1]
                    tqh = p.dve.do(lambda e, bank=bank, qs_col=qs_col: e.scalar_tensor_tensor(
                        trb[:, 512:1024], self.pb(bank), qs_col, tr1[:],
                        op0=ALU.mult, op1=ALU.mult),
                        tev, tqs if isq else tks, tsq,
                        getattr(self, "_t_trb1", None), inc=1)
                    self.bank_did_read(bank, tqh)
                    self._t_tr1f = tqh
                    self._t_trb0 = tqh
                    tsw = p.pe.do(lambda e: e.matmul(self.pb(4), self.swap[:],
                                                     trb[:, 512:1024], start=True, stop=True),
                                  tqh, *self.bank_write(4), inc=1)
                    self.bank_written(4, tsw)
                    self._t_trb1 = tsw
                    tboff = 0 if isq else 2048
                    cs_t = rot[:, tboff + tcl * 512:tboff + tcl * 512 + 512]
                    ss_t = rot[:, tboff + 1024 + tcl * 512:tboff + 1024 + tcl * 512 + 512]
                    tt1 = p.pool.do(lambda e, cs_t=cs_t: e.tensor_tensor(
                        tr2[:], trb[:, 512:1024], cs_t, op=ALU.mult),
                        tqh, trot, getattr(self, "_t_tr2f", None), inc=1)
                    tt2 = p.dve.do(lambda e, ss_t=ss_t: e.tensor_tensor(
                        tr3[:], self.pb(4), ss_t, op=ALU.mult),
                        tsw, getattr(self, "_t_tr3f", None), inc=1)
                    self.bank_did_read(4, tt2)
                    gq = g if isq else 4 + g
                    tqp = p.dve.do(lambda e, gq=gq, tc=tc: e.tensor_tensor(
                        qh(gq)[:, tc * 512:(tc + 1) * 512], tr2[:], tr3[:], op=ALU.add),
                        tt1, getattr(self, "_t_qh%d_%d" % (gq, tc), None), inc=1)
                    setattr(self, "_t_qh%d_%d" % (gq, tc), tqp)
                    self._t_tr2f = tqp
                    self._t_tr3f = tqp
                    tsq2 = p.act.do(lambda e: e.activation(
                        trb[:, 0:512], trb[:, 512:1024], AF.Square),
                        tqh, inc=1)
                    self._t_trb0 = tsq2
                    self._t_trb1 = (tsw, tsq2)
                    tn2 = p.pe.do(lambda e: e.matmul(self.pb(2, 2), self.hsel[:],
                                                     trb[:, 0:512], start=True, stop=True),
                                  tsq2, *self.bank_write(2), inc=1)
                    self.bank_written(2, tn2)
                    slot = (qn_slot if isq else kn_slot)[g]
                    dstrow = row(slot, slice(tc * 512, (tc + 1) * 512), 2)
                    p.act.do(lambda e, dstrow=dstrow: e.activation(
                        dstrow, self.pb(2, 2), AF.Ln), tn2)
                    tnr = p.act.do(lambda e, dstrow=dstrow: e.activation(
                        dstrow, dstrow, AF.Exp, scale=0.5), inc=1)
                    self.bank_did_read(2, tnr)
            for tt in range(8):
                gt = 8 * h + tt
                tm = None
                for kc in range(8):
                    def fvm(e, kc=kc, tt=tt):
                        sh = 1 if kc < 4 else 0
                        return e.matmul(self.pb(5),
                                        ftbf(kc)[:, tt * 128 + sh:tt * 128 + sh + 128],
                                        vw(kc), start=(kc == 0), stop=(kc == 7))
                    deps = ([t_vw_all, t_r1_all] + self.bank_write(5)) if kc == 0 else []
                    tm = p.pe.do(fvm, *deps, inc=(1 if kc == 7 else 0))
                self.bank_written(5, tm)
                tev = p.act.do(lambda e, gt=gt: e.copy(vt(gt), self.pb(5)), tm, inc=1)
                self.bank_did_read(5, tev)
            p.barrier()

        if FF_PH < 2:
            continue
        self._bank_r, self._bank_w = {}, {}
        tmk = tuple(p.dma(msk(j), self.dr["c_masks"][:, j * 512:(j + 1) * 512])
                    for j in range(8))
        p.barrier()
        for g in range(4):
            for w in range(4):
                q0 = 512 * w
                j0 = 4 if w == 0 else 0
                kwin = slice(max(512 * (w - 1), 0), q0 + 512)
                knr2 = row(kn_slot[g], kwin, 2)
                qnr2 = row(qn_slot[g], slice(q0, q0 + 512), 2)
                negb2 = self.negb_t[0:2, :]
                tmx = p.dve.do(lambda e, knr2=knr2: e.tensor_reduce(
                    rnt[0:2, 0:1], knr2, axis=mybir.AxisListType.X, op=ALU.max),
                    getattr(self, "_t_negb", None), inc=1)
                p.dve.do(lambda e, qnr2=qnr2: e.tensor_copy(tr3[0:2, 0:512], qnr2))
                p.dve.do(lambda e: e.tensor_scalar(
                    tr3[0:2, 0:512], tr3[0:2, 0:512], rnt[0:2, 0:1], -1.0,
                    op0=ALU.mult, op1=ALU.mult))
                tnb = p.dve.do(lambda e, negb2=negb2: e.tensor_copy(
                    negb2, tr3[0:2, 0:512]), inc=1)
                for sub in range(2):
                    hd = 2 * g + sub
                    r0 = 64 * sub
                    tav = tdn = None
                    js = list(range(j0, 8))
                    QB = (0, 1, 5, 6)
                    tqk_l = {}
                    for idx in range(len(js) + 2):
                        if idx < len(js):
                            j = js[idx]
                            bank = QB[idx % 4]
                            p.pe.do(lambda e, j=j, bank=bank, g=g, r0=r0, q0=q0,
                                    w=w: e.matmul(
                                self.pb(bank),
                                qh(4 + g)[r0:r0 + 64,
                                          512 * (w - 1) + j * 128:
                                          512 * (w - 1) + j * 128 + 128],
                                qh(g)[r0:r0 + 64, q0:q0 + 512],
                                start=True, stop=False),
                                *self.bank_write(bank), inc=0)
                            tqk = p.pe.do(lambda e, bank=bank, sub=sub, negb2=negb2:
                                          e.matmul(
                                self.pb(bank), self.rsel[:, sub * 128:(sub + 1) * 128],
                                negb2, start=False, stop=True), tnb, inc=1)
                            self.bank_written(bank, tqk)
                            tqk_l[idx] = tqk
                        if idx >= 2:
                            i2 = idx - 2
                            j = js[i2]
                            bank = QB[i2 % 4]
                            kt = 4 * (w - 1) + j
                            ii = i2 % 2
                            tst = p.dve.do(lambda e, bank=bank, j=j, ii=ii:
                                           e.scalar_tensor_tensor(
                                tsb[ii][:], self.pb(bank), -84.0, msk(j),
                                op0=ALU.max, op1=ALU.add),
                                tqk_l[i2], tmk,
                                getattr(self, "_t_tsb%d" % ii, None), inc=1)
                            self.bank_did_read(bank, tst)
                            tex = p.act.do(lambda e, ii=ii: e.activation(
                                ptb[ii][:], tsb[ii][:], AF.Exp),
                                tst, getattr(self, "_t_pt%d" % ii, None), inc=1)
                            setattr(self, "_t_tsb%d" % ii, tex)
                            tav = p.pe.do(lambda e, kt=kt, hd=hd, ii=ii, i2=i2,
                                          lj=len(js): e.matmul(
                                self.pb(2, 64), vt(kt)[:, hd * 64:(hd + 1) * 64],
                                ptb[ii][:],
                                start=(i2 == 0), stop=(i2 == lj - 1)),
                                tex, *(self.bank_write(2) if i2 == 0 else []), inc=1)
                            tdn = p.pe.do(lambda e, ii=ii, i2=i2, lj=len(js): e.matmul(
                                self.pb(3, 1), self.ones_bf[:, 0:1], ptb[ii][:],
                                start=(i2 == 0), stop=(i2 == lj - 1)),
                                *(self.bank_write(3) if i2 == 0 else []), inc=1)
                            setattr(self, "_t_pt%d" % ii, tdn)
                    self.bank_written(2, tav)
                    self.bank_written(3, tdn)
                    trc = p.dve.do(lambda e: e.reciprocal(rnt[0:1, 0:512], self.pb(3, 1)),
                                   tdn, tmx, inc=1)
                    self.bank_did_read(3, trc)
                    trb_ = self.bcast_row(4, rnt[0:1, 0:512],
                                          tuple(self.bank_write(4)) + (trc,), rows=64)
                    self.bank_written(4, trb_)
                    tevr = p.act.do(lambda e: e.copy(rbsb[:, 0:512], self.pb(4, 64)), trb_,
                                    getattr(self, "_t_rbsb", None), inc=1)
                    self.bank_did_read(4, tevr)
                    tot = p.dve.do(lambda e, g=g, r0=r0, q0=q0: e.tensor_tensor(
                        OT(g)[r0:r0 + 64, q0:q0 + 512], self.pb(2, 64), rbsb[:, 0:512],
                        op=ALU.mult),
                        tav, tevr, getattr(self, "_t_OT%d" % g, None), inc=1)
                    self.bank_did_read(2, tot)
                    self._t_rbsb = tot
                    setattr(self, "_t_OT%d" % g, tot)
                self._t_negb = p.pe.tok()
        p.barrier()

        if FF_PH < 3:
            continue
        self._bank_r, self._bank_w = {}, {}
        for og in range(8):
            i = og % 2
            keyw = (l, og)
            r0w = (l * 8 + og) * 128
            if keyw not in self.t_wopre:
                tdw = p.dma(wstqk[:, 0:512],
                            self.dr[pf + "woW"][og].rearrange("p a b -> p (a b)"),
                            getattr(self, "_t_wstqk", None))
                tcw = p.pool.do(lambda e, i=i: e.tensor_copy(
                    wbfqk[i][:, 0:512], wstqk[:, 0:512]),
                    tdw, getattr(self, "_t_wbq%d" % i, None),
                    getattr(self, "_t_wopp%d" % i, None),
                    getattr(self, "_t_qkpp%d" % i, None), inc=1)
                self._t_wstqk = tcw
                tsw = p.dma(self.wprew[r0w:r0w + 128, :], wbfqk[i][:, 0:512], tcw)
                self.t_wopre[keyw] = tsw
                setattr(self, "_t_wopp%d" % i, tsw)
            else:
                tcw = p.dma(wbfqk[i][:, 0:512], self.wprew[r0w:r0w + 128, :],
                            self.t_wopre[keyw],
                            getattr(self, "_t_wbq%d" % i, None),
                            getattr(self, "_t_wopp%d" % i, None),
                            getattr(self, "_t_qkpp%d" % i, None))
            for tc in range(4):
                bank = tc % 2
                tm = None
                for kc in range(4):
                    def fwo(e, kc=kc, i=i, bank=bank, tc=tc):
                        return e.matmul(self.pb(bank), wbfqk[i][:, kc * 128:(kc + 1) * 128],
                                        OT(kc)[:, tc * 512:(tc + 1) * 512],
                                        start=(kc == 0), stop=(kc == 3))
                    deps = ([tcw] + self.bank_write(bank)) if kc == 0 else []
                    tm = p.pe.do(fwo, *deps, inc=(1 if kc == 3 else 0))
                setattr(self, "_t_wbq%d" % i, tm)
                self.bank_written(bank, tm)
                tup = p.dve.do(lambda e, og=og, tc=tc, bank=bank: e.tensor_tensor(
                    ftv(og)[:, tc * 512:(tc + 1) * 512],
                    ftv(og)[:, tc * 512:(tc + 1) * 512], self.pb(bank), op=ALU.add),
                    tm, inc=1)
                self.bank_did_read(bank, tup)
        p.barrier()

        if FF_DBG & 2:
            self.emit_dump(1 + 3 * l, ftv)
        if FF_PH < 4:
            continue
        self._bank_r, self._bank_w = {}, {}
        s_lo = self.rows_a[0:1, :]; s_hi = self.rows_a[32:33, :]
        q_lo = self.rows_b[0:1, :]; q_hi = self.rows_b[32:33, :]
        mus = self.rows_a[64:65, :]; rs_ = self.rows_b[64:65, :]
        for tc in range(4):
            cs = slice(512 * tc, 512 * (tc + 1))
            tml = None
            for og in range(8):
                sb_ = 0 if og < 4 else 1
                qb_ = 2 if og < 4 else 3
                tcp = p.pool.do(lambda e, og=og, cs=cs: e.tensor_copy(
                    trb[:, 0:512], ftv(og)[:, cs]), getattr(self, "_t_trbW", None), inc=1)
                tsq = p.act.do(lambda e, og=og, cs=cs: e.activation(
                    trb[:, 512:1024], ftv(og)[:, cs], AF.Square),
                    getattr(self, "_t_trbW2", None), inc=1)
                tm1 = p.pe.do(lambda e, sb_=sb_, og=og: e.matmul(
                    self.pb(sb_, 1), self.ones_bf[:, 0:1], trb[:, 0:512],
                    start=(og % 4 == 0), stop=(og % 4 == 3)),
                    tcp, *(self.bank_write(sb_) if og % 4 == 0 else []), inc=1)
                self._t_trbW = tm1
                tm2 = p.pe.do(lambda e, qb_=qb_, og=og: e.matmul(
                    self.pb(qb_, 1), self.ones_bf[:, 0:1], trb[:, 512:1024],
                    start=(og % 4 == 0), stop=(og % 4 == 3)),
                    tsq, *(self.bank_write(qb_) if og % 4 == 0 else []), inc=1)
                self._t_trbW2 = tm2
                if og % 4 == 3:
                    self.bank_written(sb_, tm1)
                    self.bank_written(qb_, tm2)
                tml = tm2
            for bnk, dstr in ((0, s_lo), (1, s_hi), (2, q_lo), (3, q_hi)):
                tev = p.dve.do(lambda e, bnk=bnk, dstr=dstr, cs=cs: e.tensor_copy(
                    dstr[:, cs], self.pb(bnk, 1)), tml, inc=1)
                self.bank_did_read(bnk, tev)
        p.barrier()
        tmpA = self.bc_rows[0:1, 512:1024]
        tmpB = rnt[0:1, 0:512]
        for tc in range(4):
            cs = slice(512 * tc, 512 * (tc + 1))
            if tc == 0:
                p.dve.do(lambda e: e.memset(tmpA[:, 0:1], 0.0))
                p.dve.do(lambda e: e.tensor_copy(tmpA[:, 1:512], s_hi[:, 0:511]))
            else:
                p.dve.do(lambda e, tc=tc: e.tensor_copy(
                    tmpA[:], s_hi[:, tc * 512 - 1:(tc + 1) * 512 - 1]))
            p.dve.do(lambda e, cs=cs: e.tensor_tensor(tmpA[:], tmpA[:], s_lo[:, cs],
                                                      op=ALU.add))
            p.dve.do(lambda e, cs=cs: e.tensor_scalar_mul(mus[:, cs], tmpA[:], 1.0 / 1024))
            if tc == 0:
                p.dve.do(lambda e: e.memset(tmpA[:, 0:1], 0.0))
                p.dve.do(lambda e: e.tensor_copy(tmpA[:, 1:512], q_hi[:, 0:511]))
            else:
                p.dve.do(lambda e, tc=tc: e.tensor_copy(
                    tmpA[:], q_hi[:, tc * 512 - 1:(tc + 1) * 512 - 1]))
            p.dve.do(lambda e, cs=cs: e.tensor_tensor(tmpA[:], tmpA[:], q_lo[:, cs],
                                                      op=ALU.add))
            p.dve.do(lambda e: e.tensor_scalar_mul(tmpA[:], tmpA[:], 1.0 / 1024))
            p.dve.do(lambda e, cs=cs: e.tensor_tensor(tmpB[:], mus[:, cs], mus[:, cs],
                                                      op=ALU.mult))
            tva = p.dve.do(lambda e: e.tensor_tensor(tmpA[:], tmpA[:], tmpB[:],
                                                     op=ALU.subtract), inc=1)
            p.act.do(lambda e: e.activation(tmpA[:], tmpA[:], AF.Ln,
                                            bias=self.konst[0:1, 0:1]), tva, self.t_k0)
            tse = p.act.do(lambda e, cs=cs: e.activation(rs_[:, cs], tmpA[:],
                                                         AF.Exp, scale=-0.5), inc=1)
            p.dve.do(lambda e: e.engine_nop(), tse, inc=1)
        p.barrier()

        if FF_PH < 5:
            continue
        for kc in range(4, 8):
            p.dve.do(lambda e, kc=kc: e.tensor_copy(
                self.col8d[:, kc - 4:kc - 3], ftv(kc)[:, 1023:1024]))
        for h in range(2):
            for tcl in range(2):
                tc = 2 * h + tcl
                cs = slice(512 * tc, 512 * (tc + 1))
                tmu = self.bcast_row(6, mus[:, cs], tuple(self.bank_write(6)))
                self.bank_written(6, tmu)
                trr = self.bcast_row(7, rs_[:, cs], tuple(self.bank_write(7)))
                self.bank_written(7, trr)
                tl = None
                for kc in range(8):
                    sh = 0 if kc < 4 else 1
                    a0 = 512 * tc - sh
                    gcol = self.col16[:, kc:kc + 1]
                    if h == 1 and tcl == 0 and kc >= 4:
                        def f1(e, kc=kc):
                            return e.tensor_tensor(tr1[:, 1:512], ftv(kc)[:, 1024:1535],
                                                   self.pb(6)[:, 1:512], op=ALU.subtract)
                        p.dve.do(f1, tmu, getattr(self, "_t_tr1W", None))
                        tl = p.dve.do(lambda e, kc=kc, tcl=tcl, gcol=gcol: e.scalar_tensor_tensor(
                            hs2n(kc)[:, tcl * 512 + 1:tcl * 512 + 512], tr1[:, 1:512], gcol,
                            self.pb(7)[:, 1:512], op0=ALU.mult, op1=ALU.mult),
                            trr, tlg, getattr(self, "_t_hs%d" % kc, None), inc=1)
                        p.dve.do(lambda e, kc=kc: e.tensor_tensor(
                            tr1[:, 0:1], self.col8d[:, kc - 4:kc - 3], self.pb(6)[:, 0:1],
                            op=ALU.subtract))
                        tl = p.dve.do(lambda e, kc=kc, tcl=tcl, gcol=gcol: e.scalar_tensor_tensor(
                            hs2n(kc)[:, tcl * 512:tcl * 512 + 1], tr1[:, 0:1], gcol,
                            self.pb(7)[:, 0:1], op0=ALU.mult, op1=ALU.mult), inc=1)
                        setattr(self, "_t_hs%d" % kc, tl)
                        self._t_tr1W = tl
                        continue
                    if a0 < 0:
                        def f1(e, kc=kc):
                            return e.tensor_tensor(tr1[:, 1:512], ftv(kc)[:, 0:511],
                                                   self.pb(6)[:, 1:512], op=ALU.subtract)
                        p.dve.do(f1, tmu, getattr(self, "_t_tr1W", None))
                        tl = p.dve.do(lambda e, kc=kc, tcl=tcl, gcol=gcol: e.scalar_tensor_tensor(
                            hs2n(kc)[:, tcl * 512 + 1:tcl * 512 + 512], tr1[:, 1:512], gcol,
                            self.pb(7)[:, 1:512], op0=ALU.mult, op1=ALU.mult),
                            trr, tlg, getattr(self, "_t_hs%d" % kc, None), inc=1)
                        p.dve.do(lambda e: e.tensor_copy(tr2[:, 0:1], self.pb(6)[:, 0:1]),
                                 tmu, getattr(self, "_t_tr2W", None))
                        ta2 = p.dve.do(lambda e: e.tensor_copy(tr2[:, 1:2], self.pb(7)[:, 0:1]),
                                       trr, inc=1)
                        p.dve.do(lambda e: e.tensor_tensor(tr2[:, 2:3], tr2[:, 0:1],
                                                           tr2[:, 1:2], op=ALU.mult), ta2)
                        p.dve.do(lambda e: e.tensor_scalar_mul(tr2[:, 3:4], tr2[:, 2:3], -1.0))
                        tl2 = p.dve.do(lambda e, kc=kc, tcl=tcl, gcol=gcol: e.tensor_scalar(
                            hs2n(kc)[:, tcl * 512:tcl * 512 + 1], tr2[:, 3:4], gcol, None,
                            op0=ALU.mult), tlg, inc=1)
                        self._t_tr2W = tl2
                        tl = tl2
                    else:
                        def f1(e, kc=kc, a0=a0):
                            return e.tensor_tensor(tr1[:], ftv(kc)[:, a0:a0 + 512],
                                                   self.pb(6), op=ALU.subtract)
                        p.dve.do(f1, tmu, getattr(self, "_t_tr1W", None))
                        tl = p.dve.do(lambda e, kc=kc, tcl=tcl, gcol=gcol: e.scalar_tensor_tensor(
                            hs2n(kc)[:, tcl * 512:(tcl + 1) * 512], tr1[:], gcol,
                            self.pb(7), op0=ALU.mult, op1=ALU.mult),
                            trr, tlg, getattr(self, "_t_hs%d" % kc, None), inc=1)
                    setattr(self, "_t_hs%d" % kc, tl)
                    self._t_tr1W = tl
                self.bank_did_read(6, tl)
                self.bank_did_read(7, tl)
            p.barrier()
            self._bank_r, self._bank_w = {}, {}
            for og in range(NOGH):
                i = og % 2
                key1 = (l, og)
                if key1 not in self.t_w1pre:
                    tda = p.dma(w1st[:, 0:1024],
                                self.dr[pf + "w1a"][og].rearrange("p a b -> p (a b)"),
                                getattr(self, "_t_w1st", None))
                    tdg = p.dma(w1st[:, 1024:2048],
                                self.dr[pf + "w1g"][og].rearrange("p a b -> p (a b)"))
                    tca = p.act.do(lambda e, i=i: e.copy(w1bf[i][:, 0:1024],
                                                         w1st[:, 0:1024]),
                                   tda, getattr(self, "_t_w1b%d" % i, None),
                                   getattr(self, "_t_w1pp%d" % i, None), inc=1)
                    tcg = p.act.do(lambda e, i=i: e.copy(w1bf[i][:, 1024:2048],
                                                         w1st[:, 1024:2048]), tdg, inc=1)
                    self._t_w1st = tcg
                    r0w = (l * NOGH + og) * 128
                    tsa = p.dma(self.wpre1[r0w:r0w + 128, :], w1bf[i][:], tca, tcg)
                    self.t_w1pre[key1] = tsa
                    setattr(self, "_t_w1pp%d" % i, tsa)
                else:
                    r0w = (l * NOGH + og) * 128
                    tld = p.dma(w1bf[i][:], self.wpre1[r0w:r0w + 128, :],
                                self.t_w1pre[key1],
                                getattr(self, "_t_w1b%d" % i, None),
                                getattr(self, "_t_w1pp%d" % i, None))
                    tca = tcg = tld
                tcbA = tcbG = None
                for kc in range(8):
                    da = ([tca, tlnb] + self.bank_write(4)) if kc == 0 else []
                    tcbA = p.pe.do(lambda e, kc=kc, i=i: e.matmul(
                        self.pb(4, 128, 1), w1bf[i][:, kc * 128:(kc + 1) * 128],
                        lnb_bf[:, kc:kc + 1], start=(kc == 0), stop=(kc == 7)),
                        *da, inc=(1 if kc == 7 else 0))
                    dg = ([tcg] + self.bank_write(5)) if kc == 0 else []
                    tcbG = p.pe.do(lambda e, kc=kc, i=i: e.matmul(
                        self.pb(5, 128, 1), w1bf[i][:, 1024 + kc * 128:1024 + (kc + 1) * 128],
                        lnb_bf[:, kc:kc + 1], start=(kc == 0), stop=(kc == 7)),
                        *dg, inc=(1 if kc == 7 else 0))
                self.bank_written(4, tcbA)
                self.bank_written(5, tcbG)
                b1a_col = self.aw_dtB[:, og:og + 1] if og < 16 else self.aw_D[:, og - 16:og - 15]
                b1g_col = self.aw_cvB[:, og:og + 1] if og < 16 else self.aw_conv[:, og - 16:og - 15]
                tca2 = p.dve.do(lambda e, b1a_col=b1a_col: e.tensor_scalar(
                    self.col8a[:, 0:1], self.pb(4, 128, 1), b1a_col, None, op0=ALU.add),
                    tcbA, tb1, getattr(self, "_t_cbA", None), inc=1)
                self.bank_did_read(4, tca2)
                tcg2 = p.dve.do(lambda e, b1g_col=b1g_col: e.tensor_scalar(
                    self.col8a[:, 1:2], self.pb(5, 128, 1), b1g_col, None, op0=ALU.add),
                    tcbG, getattr(self, "_t_cbG", None), inc=1)
                self.bank_did_read(5, tcg2)
                self._t_cbA = tca2
                self._t_cbG = tcg2
                for tcl in range(2):
                    tc = 2 * h + tcl
                    bA = tcl % 2
                    bG = 2 + tcl % 2
                    tmA = tmG = None
                    for kc in range(8):
                        dA_ = ([tca] + self.bank_write(bA)) if kc == 0 else []
                        tmA = p.pe.do(lambda e, kc=kc, i=i, bA=bA, tcl=tcl: e.matmul(
                            self.pb(bA), w1bf[i][:, kc * 128:(kc + 1) * 128],
                            hs2n(kc)[:, tcl * 512:(tcl + 1) * 512],
                            start=(kc == 0), stop=(kc == 7)),
                            *dA_, inc=(1 if kc == 7 else 0))
                        dG_ = ([tcg] + self.bank_write(bG)) if kc == 0 else []
                        tmG = p.pe.do(lambda e, kc=kc, i=i, bG=bG, tcl=tcl: e.matmul(
                            self.pb(bG), w1bf[i][:, 1024 + kc * 128:1024 + (kc + 1) * 128],
                            hs2n(kc)[:, tcl * 512:(tcl + 1) * 512],
                            start=(kc == 0), stop=(kc == 7)),
                            *dG_, inc=(1 if kc == 7 else 0))
                    self.bank_written(bA, tmA)
                    self.bank_written(bG, tmG)
                    tgg = p.act.do(lambda e, bG=bG: e.activation(
                        trb[:, 0:512], self.pb(bG), AF.Gelu, bias=self.col8a[:, 1:2]),
                        tmG, tcg2, getattr(self, "_t_ggW", None), inc=1)
                    self.bank_did_read(bG, tgg)
                    th2 = p.dve.do(lambda e, og=og, tcl=tcl, bA=bA: e.scalar_tensor_tensor(
                        h2(og)[:, tcl * 512:(tcl + 1) * 512], self.pb(bA),
                        self.col8a[:, 0:1], trb[:, 0:512], op0=ALU.add, op1=ALU.mult),
                        tmA, tca2, tgg, getattr(self, "_t_h2%d" % og, None), inc=1)
                    self.bank_did_read(bA, th2)
                    self._t_ggW = th2
                    setattr(self, "_t_h2%d" % og, th2)
                setattr(self, "_t_w1b%d" % i, (tmA, tmG, tcbA, tcbG))
            p.barrier()
            self._bank_r, self._bank_w = {}, {}
            for og in range(8):
                i = og % 2
                key2 = (l, og)
                if key2 not in self.t_w2pre:
                    tdw = p.dma(w2st[:],
                                self.dr[pf + "w2W"][og].rearrange("p a b -> p (a b)"),
                                getattr(self, "_t_w2st", None))
                    tcw = p.act.do(lambda e, i=i: e.copy(w2bf[i][:], w2st[:]),
                                   tdw, getattr(self, "_t_w2b%d" % i, None),
                                   getattr(self, "_t_w2pp%d" % i, None), inc=1)
                    self._t_w2st = tcw
                    r0w = (l * 8 + og) * 128
                    tsw = p.dma(self.wpre2[r0w:r0w + 128, :], w2bf[i][:], tcw)
                    self.t_w2pre[key2] = tsw
                    setattr(self, "_t_w2pp%d" % i, tsw)
                else:
                    r0w = (l * 8 + og) * 128
                    tcw = p.dma(w2bf[i][:], self.wpre2[r0w:r0w + 128, :],
                                self.t_w2pre[key2],
                                getattr(self, "_t_w2b%d" % i, None),
                                getattr(self, "_t_w2pp%d" % i, None))
                for tcl in range(2):
                    tc = 2 * h + tcl
                    bank = 4 + tcl % 2
                    tm = None
                    for kc in range(NOGH):
                        dd = ([tcw] + self.bank_write(bank)) if kc == 0 else []
                        tm = p.pe.do(lambda e, kc=kc, i=i, bank=bank, tcl=tcl: e.matmul(
                            self.pb(bank), w2bf[i][:, kc * 128:(kc + 1) * 128],
                            h2(kc)[:, tcl * 512:(tcl + 1) * 512],
                            start=(kc == 0), stop=(kc == NOGH - 1)),
                            *dd, inc=(1 if kc == NOGH - 1 else 0))
                    setattr(self, "_t_w2b%d" % i, tm)
                    self.bank_written(bank, tm)
                    b2col = self.col8b[:, og:og + 1]
                    tup = p.dve.do(lambda e, og=og, tc=tc, bank=bank, b2col=b2col:
                                   e.scalar_tensor_tensor(
                                       ftv(og)[:, tc * 512:(tc + 1) * 512], self.pb(bank),
                                       b2col, ftv(og)[:, tc * 512:(tc + 1) * 512],
                                       op0=ALU.add, op1=ALU.add),
                                   tm, tb2, inc=1)
                    self.bank_did_read(bank, tup)
            p.barrier()
        if FF_DBG & 4:
            self.emit_dump(2 + 3 * l, ftv)
    tcc = {0: None, 1: None}
    for g in range(8):
        for tc in range(4):
            i = tc % 2
            src = tr1 if i == 0 else tr2
            cs = slice(512 * tc, 512 * (tc + 1))
            td = p.dma(src[:], src_hbm[g * 128:(g + 1) * 128, cs], tcc[i])
            ta = p.dve.do(lambda e, g=g, cs=cs, src=src: e.tensor_tensor(
                src[:], src[:], ftv(g)[:, cs], op=ALU.add), td, inc=1)
            ts_ = p.dma(dst[g * 128:(g + 1) * 128, cs], src[:], ta)
            tcc[i] = ts_
    p.barrier()
    A.pop()


Builder.emit_ffn = emit_ffn


def emit_attn_b(self, src_hbm, dst_hbm):
    p, A = self.p, self.AR
    self._bank_r, self._bank_w = {}, {}
    self._t_bc = None
    self._t_xnc = {0: None, 1: None}
    A.push()
    xn3n = A.alloc_bf(8 * 2048)
    xb = [A.alloc(1026) for _ in range(8)]
    xab = A.alloc_bf(8 * 1024)
    xabv = lambda k: xab[:, k * 1024:(k + 1) * 1024]
    siluz = A.alloc_bf(8 * 1024)
    szv = lambda k: siluz[:, k * 1024:(k + 1) * 1024]
    gated = A.alloc_bf(8 * 1024)
    gtv = lambda k: gated[:, k * 1024:(k + 1) * 1024]
    t1b = A.alloc_bf(1024)
    t3B = A.alloc_bf(1024)
    t3C = A.alloc_bf(1024)
    wstg = A.alloc(1024)
    wbfg = [A.alloc_bf(1024) for _ in range(2)]
    xpw = A.alloc_bf(512)
    bgw = A.alloc_bf(128)
    cgw = A.alloc_bf(128)
    dtw = A.alloc_bf(1024)
    bpw = A.alloc_bf(1024)
    cpw = A.alloc_bf(1024)
    tra = A.alloc(512); trb2 = A.alloc(512); trc2 = A.alloc(512)
    trd = A.alloc(512); tre2 = A.alloc(512)
    stash = A.alloc(16)

    mu3 = self.rows_a[0:1, :]; r3 = self.rows_a[32:33, :]
    rc = self.rows_b[0:1, :]; mr3 = self.rows_b[32:33, :]
    t_stats = self.emit_stats(src_hbm, mu3, r3, EPS)
    tmpA = self.bc_rows[0:1, 512:1024]
    tmpB = self.rows_c[0:1, :]
    trows = None
    for tc in range(4):
        cs = slice(512 * tc, 512 * (tc + 1))
        p.dve.do(lambda e, cs=cs: e.tensor_tensor(tmpA[:], r3[:, cs], r3[:, cs],
                                                  op=ALU.mult), t_stats)
        p.dve.do(lambda e: e.reciprocal(tmpB[0:1, 0:512], tmpA[:]))
        p.dve.do(lambda e: e.tensor_scalar(tmpB[0:1, 0:512], tmpB[0:1, 0:512],
                                           -float(EPS), None, op0=ALU.add))
        p.dve.do(lambda e: e.tensor_tensor(tmpB[0:1, 0:512], tmpB[0:1, 0:512], tmpA[:],
                                           op=ALU.mult))
        tv = p.dve.do(lambda e: e.engine_nop(), inc=1)
        p.act.do(lambda e: e.activation(tmpB[0:1, 0:512], tmpB[0:1, 0:512], AF.Ln,
                                        bias=self.konst[0:1, 0:1]), tv, self.t_k0)
        ta = p.act.do(lambda e: e.activation(tmpB[0:1, 0:512], tmpB[0:1, 0:512],
                                             AF.Exp, scale=-0.5), inc=1)
        r3c = self.rows_c[0:1, 512:1024]
        p.dve.do(lambda e, cs=cs: e.tensor_copy(r3c[:], r3[:, cs]))
        p.dve.do(lambda e, cs=cs: e.tensor_tensor(rc[:, cs], r3c[:], tmpB[0:1, 0:512],
                                                  op=ALU.mult), ta)
        trows = p.dve.do(lambda e, cs=cs: e.tensor_tensor(mr3[:, cs], mu3[:, cs], r3c[:],
                                                          op=ALU.mult), inc=1)
    tg = p.dma(self.col16[:, 0:8], self.dr["b_gamma"][:])
    tbt = p.dma(self.col16[:, 8:16], self.dr["b_beta"][:])
    lnb_bf = self.col8c[:, 0:4].bitcast(BF16)[:, 0:8]
    tbt2 = p.pool.do(lambda e: e.tensor_copy(lnb_bf, self.col16[:, 8:16]), tbt, inc=1)
    tcv = p.dma(self.aw_conv[:, 0:24], self.dr["b_convW"][:])
    tcb = p.dma(self.aw_cvB[:, 0:8], self.dr["b_convB"][:])
    tdb = p.dma(self.aw_cvB[:, 8:16], self.dr["b_dtB"][:])
    tAc = p.dma(self.aw_dtB[:, 0:8], self.dr["b_A"][:])
    tDc = p.dma(self.aw_D[:, 0:8], self.dr["b_Dsk"][:])
    tw1 = p.dma(wstg[:, 0:512].rearrange("p (a b) -> p a b", a=8),
                self.dr["b_xpW"][:].transpose([1, 0, 2]))
    txp = p.pool.do(lambda e: e.tensor_copy(xpw[:], wstg[:, 0:512]), tw1, inc=1)
    tw2 = p.dma(wstg[:, 0:128].rearrange("p (a b) -> p a b", a=8),
                self.dr["b_BgW"][:].transpose([1, 0, 2]), txp)
    tbg = p.pool.do(lambda e: e.tensor_copy(bgw[:], wstg[:, 0:128]), tw2, inc=1)
    tw3 = p.dma(wstg[:, 0:128].rearrange("p (a b) -> p a b", a=8),
                self.dr["b_CgW"][:].transpose([1, 0, 2]), tbg)
    tcg = p.pool.do(lambda e: e.tensor_copy(cgw[:], wstg[:, 0:128]), tw3, inc=1)
    tw4 = p.dma(wstg[0:64, 0:1024], self.dr["b_dtW"][:], tcg)
    tdt = p.pool.do(lambda e: e.tensor_copy(dtw[0:64, :], wstg[0:64, 0:1024]), tw4, inc=1)
    tw5 = p.dma(wstg[0:16, 0:1024], self.dr["b_BpW"][:], tdt)
    tbp = p.pool.do(lambda e: e.tensor_copy(bpw[0:16, :], wstg[0:16, 0:1024]), tw5, inc=1)
    tw6 = p.dma(wstg[0:16, 0:1024], self.dr["b_CpW"][:], tbp)
    tcp = p.pool.do(lambda e: e.tensor_copy(cpw[0:16, :], wstg[0:16, 0:1024]), tw6, inc=1)

    for tc in range(4):
        cs = slice(512 * tc, 512 * (tc + 1))
        tmu = self.bcast_row(6, mu3[:, cs], tuple(self.bank_write(6)) + (t_stats,))
        self.bank_written(6, tmu)
        trr = self.bcast_row(7, rc[:, cs], tuple(self.bank_write(7)) + (trows,))
        self.bank_written(7, trr)
        tl = None
        if not hasattr(self, "_t_xnc"):
            self._t_xnc = {0: None, 1: None}
        tcc = self._t_xnc
        for g in range(8):
            i = g % 2
            src = tra if i == 0 else trb2
            td = p.dma(src[:], src_hbm[g * 128:(g + 1) * 128, cs], tcc[i])
            p.dve.do(lambda e, src=src: e.tensor_tensor(src[:], src[:], self.pb(6),
                                                        op=ALU.subtract), td, tmu)
            gcol = self.col16[:, g:g + 1]
            tl = p.dve.do(lambda e, g=g, cs=cs, src=src, gcol=gcol: e.scalar_tensor_tensor(
                xn3n[:, g * 2048:(g + 1) * 2048][:, cs], src[:], gcol, self.pb(7),
                op0=ALU.mult, op1=ALU.mult), trr, tg, inc=1)
            tcc[i] = tl
        self.bank_did_read(6, tl)
        self.bank_did_read(7, tl)
    p.barrier()

    tzero = p.dve.do(lambda e: e.memset(self.states_b[:], 0.0), inc=1)
    for h in range(2):
        self._bank_r, self._bank_w = {}, {}
        for og in range(16):
            i = og % 2
            tdw = p.dma(wstg[:], self.dr["b_inW"][og].rearrange("p a b -> p (a b)"),
                        getattr(self, "_t_wstb", None))
            tcw = p.pool.do(lambda e, i=i: e.tensor_copy(wbfg[i][:], wstg[:]),
                            tdw, getattr(self, "_t_wbg%d" % i, None), inc=1)
            self._t_wstb = tcw
            tcbX = None
            for kc in range(8):
                dd = ([tcw, tbt2] + self.bank_write(2)) if kc == 0 else []
                tcbX = p.pe.do(lambda e, kc=kc, i=i: e.matmul(
                    self.pb(2, 128, 1), wbfg[i][:, kc * 128:(kc + 1) * 128],
                    lnb_bf[:, kc:kc + 1], start=(kc == 0), stop=(kc == 7)),
                    *dd, inc=(1 if kc == 7 else 0))
            self.bank_written(2, tcbX)
            tcb2 = p.act.do(lambda e: e.copy(self.col8b[:, 0:1], self.pb(2, 128, 1)),
                            tcbX, getattr(self, "_t_cbb", None), inc=1)
            self.bank_did_read(2, tcb2)
            self._t_cbb = tcb2
            for tcl in range(3):
                if tcl == 2:
                    if og >= 8:
                        continue
                    nc_ = 1024 * h + 1024
                    if nc_ >= T:
                        tmz = p.dve.do(lambda e, og=og: e.memset(xb[og][:, 1025:1026], 0.0),
                                       getattr(self, "_t_xb%d" % og, None), inc=1)
                        setattr(self, "_t_xb%d" % og, tmz)
                        continue
                    tm = None
                    for kc in range(8):
                        dd = ([tcw] + self.bank_write(3)) if kc == 0 else []
                        tm = p.pe.do(lambda e, kc=kc, i=i, nc_=nc_: e.matmul(
                            self.pb(3, 128, 1), wbfg[i][:, kc * 128:(kc + 1) * 128],
                            xn3n[:, kc * 2048 + nc_:kc * 2048 + nc_ + 1],
                            start=(kc == 0), stop=(kc == 7)), *dd, inc=(1 if kc == 7 else 0))
                    self.bank_written(3, tm)
                    tev = p.act.do(lambda e, og=og: e.activation(
                        xb[og][:, 1025:1026], self.pb(3, 128, 1), AF.Identity,
                        bias=self.col8b[:, 0:1]),
                        tm, tcb2, getattr(self, "_t_xb%d" % og, None), inc=1)
                    self.bank_did_read(3, tev)
                    setattr(self, "_t_xb%d" % og, tev)
                    continue
                tc = 2 * h + tcl
                bank = tcl % 2
                tm = None
                for kc in range(8):
                    dd = ([tcw] + self.bank_write(bank)) if kc == 0 else []
                    tm = p.pe.do(lambda e, kc=kc, i=i, bank=bank, tc=tc: e.matmul(
                        self.pb(bank), wbfg[i][:, kc * 128:(kc + 1) * 128],
                        xn3n[:, kc * 2048 + tc * 512:kc * 2048 + (tc + 1) * 512],
                        start=(kc == 0), stop=(kc == 7)), *dd, inc=(1 if kc == 7 else 0))
                self.bank_written(bank, tm)
                if og < 8:
                    if tcl == 0:
                        if h == 0:
                            tpd = p.dve.do(lambda e, og=og: e.memset(xb[og][:, 0:1], 0.0),
                                           getattr(self, "_t_xb%d" % og, None), inc=1)
                        else:
                            tpd = p.dve.do(lambda e, og=og: e.tensor_copy(
                                xb[og][:, 0:1], stash[:, og:og + 1]),
                                getattr(self, "_t_xb%d" % og, None),
                                getattr(self, "_t_st%d" % og, None), inc=1)
                        setattr(self, "_t_xb%d" % og, tpd)
                    tev = p.act.do(lambda e, og=og, tcl=tcl, bank=bank: e.activation(
                        xb[og][:, 1 + tcl * 512:1 + (tcl + 1) * 512], self.pb(bank),
                        AF.Identity, bias=self.col8b[:, 0:1]),
                        tm, tcb2, getattr(self, "_t_xb%d" % og, None), inc=1)
                    self.bank_did_read(bank, tev)
                    setattr(self, "_t_xb%d" % og, tev)
                    if tcl == 1 and h == 0:
                        tst_ = p.dve.do(lambda e, og=og: e.tensor_copy(
                            stash[:, og:og + 1], xb[og][:, 1024:1025]), tev, inc=1)
                        setattr(self, "_t_st%d" % og, tst_)
                else:
                    og2 = og - 8
                    tev = p.act.do(lambda e, og2=og2, tcl=tcl, bank=bank: e.activation(
                        szv(og2)[:, tcl * 512:(tcl + 1) * 512], self.pb(bank),
                        AF.Silu, bias=self.col8b[:, 0:1]),
                        tm, tcb2, getattr(self, "_t_sz%d" % og2, None), inc=1)
                    self.bank_did_read(bank, tev)
                    setattr(self, "_t_sz%d" % og2, tev)
            setattr(self, "_t_wbg%d" % i, tm)
        p.barrier()
        for og in range(8):
            for cc in range(2):
                a0 = cc * 512
                p.dve.do(lambda e, og=og, a0=a0: e.tensor_scalar_mul(
                    tra[:], xb[og][:, a0:a0 + 512], self.aw_conv[:, og * 3:og * 3 + 1]),
                    tcv, getattr(self, "_t_cnv", None))
                p.dve.do(lambda e, og=og, a0=a0: e.scalar_tensor_tensor(
                    trb2[:], xb[og][:, a0 + 1:a0 + 513],
                    self.aw_conv[:, og * 3 + 1:og * 3 + 2], tra[:],
                    op0=ALU.mult, op1=ALU.add))
                tcc2 = p.dve.do(lambda e, og=og, a0=a0: e.scalar_tensor_tensor(
                    tra[:], xb[og][:, a0 + 2:a0 + 514],
                    self.aw_conv[:, og * 3 + 2:og * 3 + 3], trb2[:],
                    op0=ALU.mult, op1=ALU.add), inc=1)
                tsl = p.act.do(lambda e, og=og, a0=a0: e.activation(
                    xabv(og)[:, a0:a0 + 512], tra[:], AF.Silu,
                    bias=self.aw_cvB[:, og:og + 1]), tcc2, tcb,
                    getattr(self, "_t_xa%d" % og, None), inc=1)
                setattr(self, "_t_xa%d" % og, tsl)
                self._t_cnv = tsl
        p.barrier()
        for tcl in range(2):
            tmx1 = tmx2 = tmx3 = None
            for kc in range(8):
                d1 = ([txp] + self.bank_write(0)) if kc == 0 else []
                tmx1 = p.pe.do(lambda e, kc=kc, tcl=tcl: e.matmul(
                    self.pb(0, 64), xpw[:, kc * 64:(kc + 1) * 64],
                    xabv(kc)[:, tcl * 512:(tcl + 1) * 512],
                    start=(kc == 0), stop=(kc == 7)), *d1, inc=(1 if kc == 7 else 0))
                d2 = ([tbg] + self.bank_write(1)) if kc == 0 else []
                tmx2 = p.pe.do(lambda e, kc=kc, tcl=tcl: e.matmul(
                    self.pb(1, 16), bgw[:, kc * 16:(kc + 1) * 16],
                    xabv(kc)[:, tcl * 512:(tcl + 1) * 512],
                    start=(kc == 0), stop=(kc == 7)), *d2, inc=(1 if kc == 7 else 0))
                d3 = ([tcg] + self.bank_write(2)) if kc == 0 else []
                tmx3 = p.pe.do(lambda e, kc=kc, tcl=tcl: e.matmul(
                    self.pb(2, 16), cgw[:, kc * 16:(kc + 1) * 16],
                    xabv(kc)[:, tcl * 512:(tcl + 1) * 512],
                    start=(kc == 0), stop=(kc == 7)), *d3, inc=(1 if kc == 7 else 0))
            self.bank_written(0, tmx1)
            self.bank_written(1, tmx2)
            self.bank_written(2, tmx3)
            te1 = p.act.do(lambda e, tcl=tcl: e.copy(
                t1b[0:64, tcl * 512:(tcl + 1) * 512], self.pb(0, 64)), tmx1,
                getattr(self, "_t_t1b", None), inc=1)
            self.bank_did_read(0, te1)
            self._t_t1b = te1
            te2 = p.act.do(lambda e, tcl=tcl: e.copy(
                t3B[0:16, tcl * 512:(tcl + 1) * 512], self.pb(1, 16)), tmx2,
                getattr(self, "_t_t3B", None), inc=1)
            self.bank_did_read(1, te2)
            self._t_t3B = te2
            te3 = p.act.do(lambda e, tcl=tcl: e.copy(
                t3C[0:16, tcl * 512:(tcl + 1) * 512], self.pb(2, 16)), tmx3,
                getattr(self, "_t_t3C", None), inc=1)
            self.bank_did_read(2, te3)
            self._t_t3C = te3
        p.barrier()
        for og in range(8):
            acol = self.aw_dtB[:, og:og + 1]
            dcol = self.aw_cvB[:, 8 + og:9 + og]
            Dcol = self.aw_D[:, og:og + 1]
            for tcl in range(2):
                tdm = p.pe.do(lambda e, og=og, tcl=tcl: e.matmul(
                    self.pb(3), dtw[0:64, og * 128:(og + 1) * 128],
                    t1b[0:64, tcl * 512:(tcl + 1) * 512], start=True, stop=True),
                    tdt, *self.bank_write(3), inc=1)
                self.bank_written(3, tdm)
                tr4 = p.dve.do(lambda e, dcol=dcol: e.tensor_scalar(
                    trd[:], self.pb(3), dcol, 0.0, op0=ALU.add, op1=ALU.max),
                    tdm, tdb, getattr(self, "_t_trd", None), inc=1)
                trA = p.act.do(lambda e, dcol=dcol: e.activation(
                    tra[:], self.pb(3), AF.Abs, bias=dcol), tdm, tr4,
                    getattr(self, "_t_spb", None), inc=1)
                self.bank_did_read(3, trA)
                self.bank_did_read(3, tr4)
                p.act.do(lambda e: e.activation(trb2[:], tra[:], AF.Exp, scale=-1.0))
                trC = p.act.do(lambda e: e.activation(tra[:], trb2[:], AF.Ln, bias=1.0),
                               inc=1)
                tdl = p.dve.do(lambda e: e.tensor_add(trd[:], trd[:], tra[:]), trC, inc=1)
                self._t_trd = tdl
                self._t_spb = tdl
                tBm = p.pe.do(lambda e, og=og, tcl=tcl: e.matmul(
                    self.pb(4), bpw[0:16, og * 128:(og + 1) * 128],
                    t3B[0:16, tcl * 512:(tcl + 1) * 512], start=True, stop=True),
                    tbp, *self.bank_write(4), inc=1)
                self.bank_written(4, tBm)
                tCm = p.pe.do(lambda e, og=og, tcl=tcl: e.matmul(
                    self.pb(5), cpw[0:16, og * 128:(og + 1) * 128],
                    t3C[0:16, tcl * 512:(tcl + 1) * 512], start=True, stop=True),
                    tcp, *self.bank_write(5), inc=1)
                self.bank_written(5, tCm)
                tda = p.act.do(lambda e, acol=acol: e.activation(
                    trb2[:], trd[:], AF.Exp, scale=acol), tdl, tAc,
                    getattr(self, "_t_da", None), inc=1)
                tbx = p.dve.do(lambda e: e.tensor_tensor(trc2[:], trd[:], self.pb(4),
                                                         op=ALU.mult),
                               tBm, tdl, getattr(self, "_t_bx", None), inc=1)
                self.bank_did_read(4, tbx)
                tbx2 = p.pool.do(lambda e, og=og, tcl=tcl: e.tensor_tensor(
                    tre2[:], trc2[:], xabv(og)[:, tcl * 512:(tcl + 1) * 512],
                    op=ALU.mult), tbx, getattr(self, "_t_bx2", None), inc=1)
                scol = self.states_b[:, og:og + 1]
                tsc = p.dve.do(lambda e, scol=scol: e.tensor_tensor_scan(
                    trc2[:], trb2[:], tre2[:], scol, op0=ALU.mult, op1=ALU.add),
                    tda, tbx2, tzero, inc=1)
                self._t_bx = tsc
                self._t_da = tsc
                tsv = p.dve.do(lambda e, scol=scol: e.tensor_copy(scol, trc2[:, 511:512]),
                               inc=1)
                ty = p.dve.do(lambda e: e.tensor_tensor(trc2[:], trc2[:], self.pb(5),
                                                        op=ALU.mult), tCm, inc=1)
                self.bank_did_read(5, ty)
                tg1 = p.dve.do(lambda e, og=og, tcl=tcl, Dcol=Dcol: e.scalar_tensor_tensor(
                    trd[:], xabv(og)[:, tcl * 512:(tcl + 1) * 512], Dcol, trc2[:],
                    op0=ALU.mult, op1=ALU.add), tDc, inc=1)
                self._t_bx2 = tg1
                tgt = p.pool.do(lambda e, og=og, tcl=tcl: e.tensor_tensor(
                    gtv(og)[:, tcl * 512:(tcl + 1) * 512], trd[:],
                    szv(og)[:, tcl * 512:(tcl + 1) * 512], op=ALU.mult),
                    tg1, getattr(self, "_t_gt%d" % og, None), inc=1)
                setattr(self, "_t_gt%d" % og, tgt)
                self._t_trd = tgt
        p.barrier()
        self._bank_r, self._bank_w = {}, {}
        for tcl in range(2):
            tc = 2 * h + tcl
            cs = slice(512 * tc, 512 * (tc + 1))
            tb0 = self.bcast_row(0, r3[:, cs], tuple(self.bank_write(0)))
            self.bank_written(0, tb0)
            tb1 = self.bcast_row(1, mr3[:, cs], tuple(self.bank_write(1)) + (trows,))
            self.bank_written(1, tb1)
            for og in range(8):
                i = og % 2
                bank = 6 + (og % 2)
                tdw = p.dma(wstg[:], self.dr["b_outW"][og].rearrange("p a b -> p (a b)"),
                            getattr(self, "_t_wstb", None))
                tcw = p.pool.do(lambda e, i=i: e.tensor_copy(wbfg[i][:], wstg[:]),
                                tdw, getattr(self, "_t_wbg%d" % i, None), inc=1)
                self._t_wstb = tcw
                tm = None
                for kc in range(8):
                    dd = ([tcw] + self.bank_write(bank)) if kc == 0 else []
                    tm = p.pe.do(lambda e, kc=kc, i=i, bank=bank, tcl=tcl: e.matmul(
                        self.pb(bank), wbfg[i][:, kc * 128:(kc + 1) * 128],
                        gtv(kc)[:, tcl * 512:(tcl + 1) * 512],
                        start=(kc == 0), stop=(kc == 7)), *dd, inc=(1 if kc == 7 else 0))
                setattr(self, "_t_wbg%d" % i, tm)
                self.bank_written(bank, tm)
                rows = slice(og * 128, (og + 1) * 128)
                td = p.dma(tra[:] if og % 2 == 0 else trb2[:], src_hbm[rows, cs],
                           getattr(self, "_t_sup%d" % (og % 2), None))
                src = tra if og % 2 == 0 else trb2
                p.dve.do(lambda e, src=src: e.tensor_tensor(trc2[:], src[:], self.pb(0),
                                                            op=ALU.mult),
                         td, tb0, getattr(self, "_t_trc2b", None))
                p.dve.do(lambda e, src=src: e.tensor_add(trc2[:], trc2[:], src[:]))
                p.dve.do(lambda e: e.tensor_tensor(trc2[:], trc2[:], self.pb(1),
                                                   op=ALU.subtract), tb1)
                tout = p.dve.do(lambda e, bank=bank: e.tensor_tensor(
                    trc2[:], trc2[:], self.pb(bank), op=ALU.add), tm, inc=1)
                self.bank_did_read(bank, tout)
                tst2 = p.dma(dst_hbm[rows, cs], trc2[:], tout)
                self._t_trc2b = tst2
                setattr(self, "_t_sup%d" % (og % 2), tst2)
            self.bank_did_read(0, tout)
            self.bank_did_read(1, tout)
        p.barrier()
    A.pop()


Builder.emit_attn_b = emit_attn_b


# version 13 (final)
# speedup vs baseline: 1.2962x; 1.0046x over previous
import os
import numpy as np
import ml_dtypes
import concourse.bass as bass
import concourse.mybir as mybir
from concourse.bass_utils import run_bass_kernel_spmd

F32 = mybir.dt.float32
BF16 = mybir.dt.bfloat16
AF = mybir.ActivationFunctionType
ALU = mybir.AluOpType
BF = ml_dtypes.bfloat16

B, T, D = 8, 2048, 1024
TQ = T // 4
NKD = D // 128
DIN, DST, DTR = 2048, 16, 64
NDI = DIN // 128
NH, DH, WIN = 8, 64, 512
QK_SCALE = 8.0
HID, HIDP = 2730, 2816
NOGH = HIDP // 128
EPS = 1e-5
NEG = -1073741824.0
AA_LIM = int(os.environ.get('AA_LIM', '99'))
AA_PH = int(os.environ.get('AA_PH', '99'))
AA_G = int(os.environ.get('AA_G', '99'))
AA_D = int(os.environ.get('AA_D', '99'))
AA_S = int(os.environ.get('AA_S', '31'))
FF_L = int(os.environ.get('FF_L', '2'))
FF_PH = int(os.environ.get('FF_PH', '99'))
FF_DBG = int(os.environ.get('FF_DBG', '0'))
_DBL = int(os.environ.get('DBL', '0'))
_DBH = int(os.environ.get('DBH', '0'))
ARENA = 43392

_built = {}



def _tile_lhsT(wT, ncols=128):
    K, M = wT.shape
    nkc, nog = K // 128, M // ncols
    a = wT.reshape(nkc, 128, nog, ncols).transpose(2, 0, 1, 3)
    return np.ascontiguousarray(a.transpose(0, 2, 1, 3))

def _cols(v):
    return np.ascontiguousarray(np.asarray(v, np.float32).reshape(-1, 128).T)

def _band_masks():
    m = np.zeros((8, 128, 512), np.float32)
    for j in range(8):
        ki = np.arange(128)[:, None] + 128 * j - 512
        q = np.arange(512)[None, :]
        ok = (q - ki >= 0) & (q - ki <= WIN)
        m[j] = np.where(ok, 0.0, NEG)
    return m

def _rotary_tables():
    half_idx = np.arange(0, DH, 2, dtype=np.float64)
    inv_freq = 1.0 / 10000.0 ** (half_idx / DH)
    t = np.arange(T, dtype=np.float64)
    freqs = np.concatenate([t[:, None] * inv_freq[None, :]] * 2, axis=-1)
    base = (half_idx + 0.4 * DH) / (1.4 * DH)
    power = (t - T // 2) / float(WIN // 2)
    scale = np.concatenate([base[None, :] ** power[:, None]] * 2, axis=-1)
    cos, sin = np.cos(freqs), np.sin(freqs)
    out = []
    for a in (cos * scale, sin * scale, cos / scale, sin / scale):
        a = a.T.astype(np.float32)
        out.append(np.concatenate([a, a], axis=0))
    return out

def _swap_sign():
    M = np.zeros((128, 128), np.float32)
    for h in range(2):
        b = 64 * h
        for p in range(32):
            M[b + p + 32, b + p] = -1.0
            M[b + p, b + p + 32] = 1.0
    return M


def stage_shared(inp):
    g = {}
    f32 = lambda a: np.ascontiguousarray(np.asarray(a), dtype=np.float32)
    for d in ("f", "b"):
        p = lambda k: f32(inp[f"L0_attn_a_{d}_{k}"])
        g[f"a{d}_inW"] = _tile_lhsT(p("in_proj_w").T)
        g[f"a{d}_outW"] = _tile_lhsT(p("out_proj_w").T)
        xpT = p("x_proj_w").T
        g[f"a{d}_xpW1"] = np.ascontiguousarray(
            xpT[:, 0:80].reshape(NDI, 128, 80).transpose(1, 0, 2).reshape(128, NDI * 80))
        g[f"a{d}_xpW2"] = np.ascontiguousarray(
            xpT[:, 80:96].reshape(NDI, 128, 16).transpose(1, 0, 2).reshape(128, NDI * 16))
        g[f"a{d}_dtW"] = f32(p("dt_proj_w").T)
        cw = p("conv_w")
        if d == "b":
            cw = cw[:, ::-1]
        g[f"a{d}_convW"] = np.ascontiguousarray(
            cw.reshape(NDI, 128, 4).transpose(1, 0, 2).reshape(128, NDI * 4))
        g[f"a{d}_convB"] = _cols(p("conv_b"))
        g[f"a{d}_dtB"] = _cols(p("dt_proj_b"))
        g[f"a{d}_D"] = _cols(p("D"))
        g[f"a{d}_Alog"] = np.ascontiguousarray(
            p("A_log").reshape(NDI, 128, DST).transpose(1, 0, 2).reshape(128, NDI * DST))
    for l in range(2):
        wqkv = f32(inp["L0_ffn_wqkv"][l])
        g[f"f{l}_qkW"] = _tile_lhsT(wqkv[:, 0:1024])
        g[f"f{l}_vW"] = np.ascontiguousarray(wqkv[:, 1024:1536].reshape(NKD, 128, 512))
        g[f"f{l}_qs"] = np.ascontiguousarray(np.tile(f32(inp["L0_ffn_q_scale"][l]), 2)[:, None])
        g[f"f{l}_ks"] = np.ascontiguousarray(np.tile(f32(inp["L0_ffn_k_scale"][l]), 2)[:, None])
        g[f"f{l}_woW"] = _tile_lhsT(f32(inp["L0_ffn_wo"][l]))
        g[f"f{l}_lng"] = _cols(inp["L0_ffn_ln_g"][l])
        g[f"f{l}_lnb"] = _cols(inp["L0_ffn_ln_b"][l])
        w1 = f32(inp["L0_ffn_w1"][l])
        w1a = np.zeros((D, HIDP), np.float32); w1a[:, :HID] = w1[:, :HID]
        w1g = np.zeros((D, HIDP), np.float32); w1g[:, :HID] = w1[:, HID:]
        g[f"f{l}_w1a"] = _tile_lhsT(w1a)
        g[f"f{l}_w1g"] = _tile_lhsT(w1g)
        b1 = f32(inp["L0_ffn_b1"][l])
        b1a = np.zeros(HIDP, np.float32); b1a[:HID] = b1[:HID]
        b1g = np.zeros(HIDP, np.float32); b1g[:HID] = b1[HID:]
        g[f"f{l}_b1a"] = _cols(b1a); g[f"f{l}_b1g"] = _cols(b1g)
        w2p = np.zeros((HIDP, D), np.float32); w2p[:HID] = f32(inp["L0_ffn_w2"][l])
        g[f"f{l}_w2W"] = _tile_lhsT(w2p)
        g[f"f{l}_b2"] = _cols(inp["L0_ffn_b2"][l])
    p = lambda k: f32(inp[f"L0_attn_b_{k}"])
    g["b_gamma"] = _cols(p("gamma")); g["b_beta"] = _cols(p("beta"))
    g["b_inW"] = _tile_lhsT(p("W_in"))
    cw3 = p("conv_w")[:, 0, :]
    g["b_convW"] = np.ascontiguousarray(
        cw3.T.reshape(NKD, 128, 3).transpose(1, 0, 2).reshape(128, NKD * 3))
    g["b_convB"] = _cols(p("conv_b"))
    g["b_xpW"] = np.ascontiguousarray(p("W_xp").reshape(NKD, 128, 64))
    g["b_BgW"] = np.ascontiguousarray(p("W_Bg").reshape(NKD, 128, 16))
    g["b_CgW"] = np.ascontiguousarray(p("W_Cg").reshape(NKD, 128, 16))
    g["b_dtW"] = f32(p("W_dt"))
    g["b_dtB"] = _cols(p("b_dt"))
    g["b_BpW"] = f32(p("W_Bp")); g["b_CpW"] = f32(p("W_Cp"))
    g["b_A"] = _cols(p("A")); g["b_Dsk"] = _cols(p("D_skip"))
    g["b_outW"] = _tile_lhsT(p("W_out"))
    g["c_eye"] = np.eye(128, dtype=np.float32).astype(BF)
    g["c_swap"] = _swap_sign().astype(BF)
    es = np.zeros((DST, DST, 128), np.float32)
    for s in range(DST):
        es[:, s][s] = 1.0
    es = np.zeros((DST, DST * 128), np.float32)
    for s in range(DST):
        es[s, s * 128:(s + 1) * 128] = 1.0
    g["c_Es"] = es.astype(BF)
    g["c_masks"] = np.ascontiguousarray(
        _band_masks().transpose(1, 0, 2).reshape(128, 8 * 512)).astype(BF)
    g["c_rot"] = np.ascontiguousarray(
        np.concatenate(_rotary_tables(), axis=1)).astype(BF)
    g["c_ones"] = np.ones((128, 2), np.float32)
    g["c_orow"] = np.ones((128, 128), np.float32)
    g["c_orow_bf"] = np.ones((128, 128), np.float32).astype(BF)
    g["c_ones_bf"] = np.ones((128, 2), np.float32).astype(BF)
    hsel = np.zeros((128, 2), np.float32)
    hsel[0:64, 0] = 1.0; hsel[64:128, 1] = 1.0
    g["c_hsel"] = hsel.astype(BF)
    s2b = np.zeros((2, 128), np.float32)
    s2b[0, 0:64] = 1.0; s2b[1, 64:128] = 1.0
    g["c_s2b"] = s2b
    g["c_s2b_bf"] = s2b.astype(BF)
    rsel = np.zeros((2, 256), np.float32)
    rsel[0, 0:128] = 1.0; rsel[1, 128:256] = 1.0
    g["c_rsel"] = rsel.astype(BF)
    return g



_DTSZ = {"dt.float32": 4, "dt.bfloat16": 2, "dt.float32r": 4, "dt.int32": 4,
         "dt.float16": 2, "dt.int8": 1, "dt.uint8": 1}


def _ap_rect(ap):
    h = ap.tensor
    kind = type(h).__name__
    if "DRam" in kind or "PSum" in kind or "Psum" in kind:
        return None
    sz = _DTSZ[str(ap.dtype)]
    C = h.shape[-1]
    p0 = ap.offset // C + getattr(h, "base_partition", 0)
    c0 = ap.offset % C
    pext = 0
    cext = 0
    for s, n in ap.ap:
        if n == 1:
            continue
        if s % C == 0:
            pext += (s // C) * (n - 1)
        else:
            cext += s * (n - 1)
    return (h.name, p0, p0 + pext + 1, c0 * sz, (c0 + cext + 1) * sz)


class _RecInst:
    def then_inc(self, *a, **k):
        return self


class _Rec:
    def __init__(self):
        self.wr = []
        self.rd = []
        self.reads_psum = False

    def __getattr__(self, mname):
        def call(*args, **kwargs):
            aps = [a for a in args if hasattr(a, "tensor")]
            aps += [v for v in kwargs.values() if hasattr(v, "tensor")]
            if mname == "dma_start":
                aps = [kwargs["out"], kwargs["in_"]]
            if aps:
                r = _ap_rect(aps[0])
                if r:
                    self.wr.append(r)
                for a in aps[1:]:
                    r = _ap_rect(a)
                    if r:
                        self.rd.append(r)
                    elif "Sum" in type(a.tensor).__name__ or \
                            "sum" in type(a.tensor).__name__:
                        self.reads_psum = True
            return _RecInst()
        return call


def _overlap(a, b):
    return (a[0] == b[0] and a[1] < b[2] and b[1] < a[2]
            and a[3] < b[4] and b[3] < a[4])


class EQ:
    HIST = 64

    def __init__(self, key, drain=False):
        self.key = key
        self.drain = drain
        self.items = []
        self.count = 0
        self.n_uncov = 0
        self.waited = {}
        self.hist = []
        self.self_waited = 0

    def wait(self, tok):
        if tok is None:
            return
        if not (isinstance(tok, tuple) and len(tok) == 2 and hasattr(tok[0], "key")):
            for t in tok:
                self.wait(t)
            return
        eq2, v = tok
        if eq2 is self:
            return
        if self.waited.get(eq2.key, 0) >= v:
            return
        self.waited[eq2.key] = v
        self.items.append(("w", eq2, v))

    def do_sem(self, fn, sem_eq, inc, *deps):
        for dp in deps:
            self.wait(dp)
        self.items.append(("i", fn, inc, sem_eq))

    def do(self, fn, *deps, inc=0):
        for dp in deps:
            self.wait(dp)
        if self.drain:
            if getattr(self, "blanket", False) or \
                    (self.key == "dve" and _DBL <= self.count < _DBH):
                if self.count:
                    self.items.append(("w", self, self.count))
                self.items.append(("i", fn, 1))
                self.count += 1
                self.n_uncov = 0
                return (self, self.count) if inc else None
            rec = _Rec()
            fn(rec)
            need = self.count if rec.reads_psum else 0
            for (cum, hw, hr) in self.hist:
                if cum <= self.self_waited:
                    continue
                hit = False
                for a in rec.rd:
                    for b2 in hw:
                        if _overlap(a, b2):
                            hit = True
                            break
                    if hit:
                        break
                if not hit:
                    for a in rec.wr:
                        for b2 in hw:
                            if _overlap(a, b2):
                                hit = True
                                break
                        if not hit:
                            for b2 in hr:
                                if _overlap(a, b2):
                                    hit = True
                                    break
                        if hit:
                            break
                if hit and cum > need:
                    need = cum
            if need > self.self_waited:
                self.items.append(("w", self, need))
                self.self_waited = need
            self.items.append(("i", fn, 1))
            self.count += 1
            self.n_uncov = 0
            self.hist.append((self.count, rec.wr, rec.rd))
            if len(self.hist) > self.HIST:
                self.hist.pop(0)
            return (self, self.count) if inc else None
        self.items.append(("i", fn, inc))
        if inc:
            self.count += inc
            self.n_uncov = 0
            return (self, self.count)
        self.n_uncov += 1
        return None

    def tok(self):
        assert self.n_uncov == 0, f"{self.key}: {self.n_uncov} uncovered instrs at fence"
        return (self, self.count)


class Lane:
    def __init__(self, key):
        self.key = key
        self.count = 0


class Prog:
    NLANES = 8

    def __init__(self, nc):
        self.nc = nc
        blanket = set(os.environ.get("DRAIN_BLANKET", "").split(","))
        self.pe, self.act, self.dve, self.pool, self.sp = (
            EQ("pe"), EQ("act", drain=True), EQ("dve", drain=True),
            EQ("pool", drain=True), EQ("sp"))
        for q in (self.act, self.dve, self.pool):
            if q.key in blanket:
                q.blanket = True
        self.all = [self.pe, self.act, self.dve, self.pool, self.sp]
        self.lanes = [Lane(f"d{i}") for i in range(self.NLANES)]
        self._rr = 0

    def dma(self, out, in_, *deps):
        ln = self.lanes[self._rr % self.NLANES]
        self._rr += 1
        if ln.count:
            self.sp.items.append(("w", ln, ln.count))
        for dp in deps:
            self.sp.wait(dp)
        self.sp.items.append(("i", lambda e: e.dma_start(out=out, in_=in_), 16, ln))
        ln.count += 16
        return (ln, ln.count)

    def barrier(self):
        toks = [q.tok() for q in self.all] + [(ln, ln.count) for ln in self.lanes if ln.count]
        for q in self.all:
            for t in toks:
                q.wait(t)

    def mm(self, out_ap, lhsT, rhs, start, stop, deps=(), inc=0):
        n = rhs.shape[-1]
        maxm = 512 if rhs.dtype != BF16 else 512
        if n <= maxm:
            return self.pe.do(
                lambda e: e.matmul(out_ap, lhsT, rhs, start=start, stop=stop),
                *deps, inc=inc)
        t = None
        nch = (n + maxm - 1) // maxm
        for i in range(nch):
            a, b2 = i * maxm, min((i + 1) * maxm, n)
            def f(e, a=a, b2=b2):
                return e.matmul(out_ap[:, a:b2], lhsT, rhs[:, a:b2], start=start, stop=stop)
            t = self.pe.do(f, *(deps if i == 0 else ()), inc=(inc if i == nch - 1 else 0))
        return t


    def emit(self, block, sems):
        def runner(q):
            def run(eng):
                for it in q.items:
                    if it[0] == "w":
                        eng.wait_ge(sems[it[1].key], it[2])
                    else:
                        inst = it[1](eng)
                        if it[2]:
                            key = it[3].key if len(it) > 3 else q.key
                            inst.then_inc(sems[key], it[2])
            return run
        block.tensor(runner(self.pe))
        block.scalar(runner(self.act))
        block.vector(runner(self.dve))
        block.gpsimd(runner(self.pool))
        block.sync(runner(self.sp))


class Arena:
    def __init__(self, ap):
        self.ap = ap
        self.ap_bf = ap.bitcast(BF16)
        self.frames = [{}]
        self.top = 0
        self.marks = []

    def push(self):
        self.marks.append(self.top)

    def pop(self):
        self.top = self.marks.pop()

    def alloc(self, cols):
        base = self.top
        self.top += cols
        assert self.top <= ARENA, f"arena overflow: {self.top} > {ARENA}"
        return self.ap[:, base:base + cols]

    def alloc_bf(self, bfcols):
        cols = (bfcols + 1) // 2
        f = self.alloc(cols)
        base = 0
        return f.bitcast(BF16)



class Builder:
    def __init__(self, nc, shared, upto="full"):
        self.nc = nc
        self.upto = upto
        self.p = Prog(nc)
        self.dr = {}
        for k, v in shared.items():
            dt = BF16 if v.dtype == BF else F32
            self.dr[k] = nc.dram_tensor(k, list(v.shape), dt, kind="ExternalInput").ap()
        self.dr["xT"] = nc.dram_tensor("xT", [D, T], F32, kind="ExternalInput").ap()
        self.out = nc.dram_tensor("outT", [D, T], F32, kind="ExternalOutput").ap()
        self.hbm_s = nc.dram_tensor("hbm_s", [D, T], F32, kind="Internal").ap()
        self.hbm_f = nc.dram_tensor("hbm_f", [D, T], F32, kind="Internal").ap()
        if FF_DBG:
            self.dbgT = nc.dram_tensor("dbgT", [8 * D, T], F32,
                                       kind="ExternalOutput").ap()
        self.wpre1 = nc.dram_tensor("wpre1", [2 * NOGH * 128, 2048], BF16,
                                    kind="Internal").ap()
        self.wpre2 = nc.dram_tensor("wpre2", [2 * 8 * 128, 2816], BF16,
                                    kind="Internal").ap()
        self.wpreq = nc.dram_tensor("wpreq", [2 * 8 * 128, 1024], BF16,
                                    kind="Internal").ap()
        self.wprea = nc.dram_tensor("wprea", [2 * 32 * 128, 1024], BF16,
                                    kind="Internal").ap()
        self.wpreo = nc.dram_tensor("wpreo", [2 * 8 * 128, 2048], BF16,
                                    kind="Internal").ap()
        self.wprew = nc.dram_tensor("wprew", [2 * 8 * 128, 512], BF16,
                                    kind="Internal").ap()
        self.wpreb = nc.dram_tensor("wpreb", [24 * 128, 1024], BF16,
                                    kind="Internal").ap()
        self.t_w1pre = {}
        self.t_w2pre = {}
        self.t_qkpre = {}
        self.t_apre = {}
        self.t_opre = {}
        self.t_wopre = {}
        self.t_bpre = {}
        self._ctx = []
        sb = self._sb = lambda n, s, dt=F32: self._keep(nc.sbuf_tensor(n, s, dt))
        self.AR = Arena(sb("arena", [128, ARENA]))
        self.rows_a = sb("rows_a", [128, 2048])
        self.rows_b = sb("rows_b", [128, 2048])
        self.rows_c = sb("rows_c", [128, 2048])
        self.states_a = sb("states_a", [128, 512])
        self.states_b = sb("states_b", [128, 8])
        self.tails = sb("tails", [128, 48])
        self.eye = sb("eye", [128, 128], BF16)
        self.swap = sb("swap", [128, 128], BF16)
        self.Es = sb("Es", [16, 2048], BF16)
        self.ones = sb("ones", [128, 2])
        self.orow_bf = sb("orow_bf", [128, 128], BF16)
        self.ones_bf = sb("ones_bf", [128, 2], BF16)
        self.hsel = sb("hsel", [128, 2], BF16)
        self.s2b = sb("s2b", [2, 128])
        self.s2b_bf = sb("s2b_bf", [2, 128], BF16)
        self.rsel = sb("rsel", [2, 256], BF16)
        self.aw_conv = sb("aw_conv", [128, 64])
        self.aw_cvB = sb("aw_cvB", [128, 16])
        self.aw_dtB = sb("aw_dtB", [128, 16])
        self.aw_D = sb("aw_D", [128, 16])
        self.fw_qs = sb("fw_qs", [128, 1])
        self.fw_ks = sb("fw_ks", [128, 1])
        self.col16 = sb("col16", [128, 16])
        self.col8a = sb("col8a", [128, 8])
        self.col8b = sb("col8b", [128, 8])
        self.col8c = sb("col8c", [128, 8])
        self.col8d = sb("col8d", [128, 8])
        self.konst = sb("konst", [128, 4])
        self.negb_t = sb("negb_t", [2, 512], BF16)
        self.bc_rows = sb("bc_rows", [2, 1024])
        self._t_bc = None
        self.PS = self._keep(nc.psum_tensor("PS", [128, 4096], F32))
        self.sems = {}

    def _keep(self, cm):
        v = cm.__enter__()
        self._ctx.append(cm)
        return v.ap() if hasattr(v, "ap") else v

    def pb(self, i, rows=128, cols=512):
        return self.PS[0:rows, 512 * i:512 * i + cols]

    def load_consts(self):
        p = self.p
        tl = []
        for dst, src in [(self.eye, "c_eye"), (self.swap, "c_swap"), (self.Es, "c_Es"),
                         (self.ones, "c_ones"), (self.ones_bf, "c_ones_bf"),
                         (self.orow_bf, "c_orow_bf"),
                         (self.hsel, "c_hsel"), (self.s2b, "c_s2b"),
                         (self.s2b_bf, "c_s2b_bf"), (self.rsel, "c_rsel")]:
            tl.append(p.dma(dst[:], self.dr[src][:]))
        self.t_consts = tuple(tl)
        p.dve.do(lambda e: e.memset(self.konst[:, 0:1], float(EPS)))
        p.dve.do(lambda e: e.memset(self.konst[:, 1:2], float(np.log(QK_SCALE))))
        p.dve.do(lambda e: e.memset(self.states_a[:], 0.0))
        self.t_k0 = p.dve.do(lambda e: e.memset(self.states_b[:], 0.0), inc=1)

    def row(self, tensor, slot, cols=slice(0, T), nrows=1):
        return tensor[32 * slot:32 * slot + nrows, cols]

    def emit_stats(self, src_hbm, mu_row, r_row, eps, n=1024.0, sig_row=None):
        p, A = self.p, self.AR
        A.push()
        xt = [A.alloc(512) for _ in range(2)]
        xb = [A.alloc_bf(512) for _ in range(2)]
        sq = [A.alloc_bf(512) for _ in range(2)]
        tr = A.alloc(1024)
        t_cons = {0: None, 1: None}
        prev_read = None
        for tc in range(4):
            cs = slice(512 * tc, 512 * (tc + 1))
            s0 = self.pb(6, 1); s1 = self.pb(7, 1)
            tok_mm = None
            for g in range(8):
                i = g % 2
                td = p.dma(xt[i][:], src_hbm[g * 128:(g + 1) * 128, cs], t_cons[i])
                tcast = p.pool.do(lambda e, i=i: e.tensor_copy(xb[i][:, 0:512], xt[i][:]), td, inc=1)
                tsq = p.act.do(lambda e, i=i: e.activation(sq[i][:, 0:512], xt[i][:], AF.Square),
                               td, inc=1)
                deps = [tcast, self.t_consts] + ([prev_read] if g == 0 else [])
                p.pe.do(lambda e, i=i, g=g: e.matmul(
                    s0, self.ones_bf[:, 0:1], xb[i][:, 0:512], start=(g == 0), stop=(g == 7)),
                    *deps)
                tok_mm = p.pe.do(lambda e, i=i, g=g: e.matmul(
                    s1, self.ones_bf[:, 0:1], sq[i][:, 0:512], start=(g == 0), stop=(g == 7)),
                    tsq, inc=1)
                t_cons[i] = tok_mm
            mu = mu_row[:, cs]; rr = r_row[:, cs]
            va = tr[0:1, 0:512]; vb = tr[0:1, 512:1024]
            p.dve.do(lambda e, mu=mu, s0=s0: e.tensor_scalar_mul(mu, s0, 1.0 / n), tok_mm)
            p.dve.do(lambda e, s1=s1: e.tensor_scalar_mul(va, s1, 1.0 / n))
            p.dve.do(lambda e, mu=mu: e.tensor_tensor(vb, mu, mu, op=ALU.mult))
            prev_read = p.dve.do(lambda e: e.tensor_sub(va, va, vb), inc=1)
            p.act.do(lambda e: e.activation(vb, va, AF.Ln, bias=self.konst[0:1, 0:1]),
                     prev_read, self.t_k0)
            ts_ = p.act.do(lambda e, rr=rr: e.activation(rr, vb, AF.Exp, scale=-0.5), inc=1)
            prev_read = (prev_read[0], prev_read[1])
            p.dve.do(lambda e: e.engine_nop(), ts_, inc=1)
        A.pop()
        return p.dve.tok()

    def bcast_row(self, bank, row_ap, deps=(), rows=128):
        p = self.p
        n = row_ap.shape[-1]
        np_ = row_ap.shape[0]
        hi = self.bc_rows[0:np_, 0:256].bitcast(BF16)[:, 0:n]
        lo = self.bc_rows[0:np_, 256:512].bitcast(BF16)[:, 0:n]
        tmp = self.bc_rows[0:np_, 512:512 + n]
        p.dve.do(lambda e: e.tensor_copy(tmp, row_ap),
                 *deps, *([self._t_bc] if self._t_bc else []))
        p.dve.do(lambda e: e.tensor_copy(hi, tmp))
        p.dve.do(lambda e: e.tensor_tensor(tmp, tmp, hi, op=ALU.subtract))
        t2 = p.dve.do(lambda e: e.tensor_copy(lo, tmp), inc=1)
        sel = self.orow_bf[0:1, 0:rows] if np_ == 1 else self.s2b_bf[0:2, 0:rows]
        p.pe.do(lambda e: e.matmul(self.pb(bank, rows, n), sel, hi,
                                   start=True, stop=False), t2, *deps, inc=0)
        tok = p.pe.do(lambda e: e.matmul(self.pb(bank, rows, n), sel, lo,
                                         start=False, stop=True), inc=1)
        self._t_bc = tok
        return tok

    def bank_write(self, i):
        return self._bank_r.get(i, [])

    def bank_written(self, i, tok):
        self._bank_w[i] = tok
        self._bank_r[i] = []

    def bank_read(self, i):
        return self._bank_w.get(i)

    def bank_did_read(self, i, tok):
        self._bank_r.setdefault(i, []).append(tok)

    def emit_attn_a(self):
        p, A = self.p, self.AR
        self._bank_r, self._bank_w = {}, {}
        A.push()
        self.aw_xp1 = A.alloc_bf(1280)[:, 0:1280]
        self.aw_xp2 = A.alloc_bf(256)[:, 0:256]
        self.aw_dt = A.alloc_bf(2048)[0:64, 0:2048]
        self.aw_dt_st = A.alloc(2048)
        self.aw_A = A.alloc(256)
        self.aw_At = A.alloc(256)
        xcf = [A.alloc_bf(518) for _ in range(NDI)]
        Bb = A.alloc_bf(16 * 512)
        Cb = A.alloc_bf(16 * 512)
        siluz = A.alloc_bf(16 * 512)
        gated = A.alloc_bf(16 * 512)
        NHR = 4
        hring = A.alloc(NHR * 512)
        dtqb = A.alloc_bf(512)
        brow = A.alloc_bf(512)
        crow = A.alloc_bf(512)
        xt2 = [A.alloc(512) for _ in range(2)]
        ot2 = [A.alloc(512) for _ in range(2)]
        SH = A.alloc(6144)
        xnq = SH[:, 0:2048].bitcast(BF16)
        wstage = SH[:, 2048:4096]
        wbf = [SH[:, 4096:5120].bitcast(BF16), SH[:, 5120:6144].bitcast(BF16)]
        delta = SH[:, 0:512]; du = SH[:, 512:1024]
        spA = SH[:, 1024:1536]; spB = SH[:, 1536:2048]; sp4 = SH[:, 2048:2560]
        g1b = SH[:, 2560:3072]
        dAb = [SH[:, 3072:3584], SH[:, 3584:4096], xt2[0], xt2[1]]
        xov = A.alloc(1024)
        du2 = A.alloc(1024)
        dX2 = [SH[:, 4096:5120], xov]
        ypb2 = [SH[:, 5120:5632].bitcast(BF16), SH[:, 5632:6144].bitcast(BF16)]
        wst2 = [wstage, A.alloc(2048)]
        cc0, cc1 = ot2[0], ot2[1]

        def xc(d):
            return xcf[d][:, 0:518]

        mu1 = self.rows_a[0:1, :]; r1 = self.rows_a[32:33, :]
        t_stats = self.emit_stats(self.dr["xT"], mu1, r1, EPS)
        self._bank_r[6] = [t_stats]
        self._bank_r[7] = [t_stats]

        t_xt2c = {0: None, 1: None}
        t_yp2c = {0: None, 1: None}
        t_dAc = {0: None, 1: None, 2: None, 3: None}
        t_dX2c = {0: None, 1: None}
        t_hr = {i: None for i in range(NHR)}
        t_hrp = {i: None for i in range(NHR)}
        t_wbfc = {0: None, 1: None}
        t_ot2c = {0: None, 1: None}
        t_sZ = {}
        t_gat = {}
        t_conv_done = {}

        for dir_ in ("f", "b"):
            fwd = dir_ == "f"
            sbase = 0 if fwd else 256
            tw = {}
            for dst, key in [(self.aw_dt_st[0:64, :], f"a{dir_}_dtW"), (self.aw_conv, f"a{dir_}_convW"),
                             (self.aw_cvB, f"a{dir_}_convB"), (self.aw_dtB, f"a{dir_}_dtB"),
                             (self.aw_D, f"a{dir_}_D"), (self.aw_At, f"a{dir_}_Alog")]:
                tw[key.split("_", 1)[1]] = p.dma(dst[:], self.dr[key][:])
            t1 = p.dma(wstage[:, 0:1280], self.dr[f"a{dir_}_xpW1"][:])
            t2 = p.dma(wstage[:, 1280:1536], self.dr[f"a{dir_}_xpW2"][:])
            tw["xpW1"] = p.pool.do(lambda e: e.tensor_copy(self.aw_xp1[:], wstage[:, 0:1280]),
                                   t1, inc=1)
            tw["xpW2"] = p.pool.do(lambda e: e.tensor_copy(self.aw_xp2[:], wstage[:, 1280:1536]),
                                   t2, inc=1)
            tw["dtW"] = p.pool.do(lambda e: e.tensor_copy(self.aw_dt[:], self.aw_dt_st[0:64, :]),
                                  tw["dtW"], inc=1)
            ta = p.act.do(lambda e: e.activation(self.aw_A[:], self.aw_At[:], AF.Exp),
                          tw["Alog"], inc=1)
            t_A = p.dve.do(lambda e: e.tensor_scalar_mul(self.aw_A[:], self.aw_A[:], -1.0),
                           ta, inc=1)
            tz = p.dve.do(lambda e: e.memset(self.tails[:], 0.0), inc=1)

            quarters = range(4) if fwd else range(3, -1, -1)
            for qi, q in enumerate(quarters):
                if (0 if fwd else 4) + qi >= AA_LIM:
                    break
                cs = slice(512 * q, 512 * (q + 1))
                tmu = self.bcast_row(6, mu1[:, cs], (self.t_consts, *self.bank_write(6)))
                self.bank_written(6, tmu)
                trr = self.bcast_row(7, r1[:, cs], tuple(self.bank_write(7)))
                self.bank_written(7, trr)
                t_xn = None
                for g in range(8):
                    i = g % 2
                    td = p.dma(xt2[i][:], self.dr["xT"][g * 128:(g + 1) * 128, cs], t_xt2c[i])
                    p.dve.do(lambda e, i=i: e.tensor_tensor(
                        xt2[i][:], xt2[i][:], self.pb(6), op=ALU.subtract),
                        td, tmu)
                    t_xn = p.dve.do(lambda e, g=g, i=i: e.tensor_tensor(
                        xnq[:, g * 512:(g + 1) * 512], xt2[i][:], self.pb(7), op=ALU.mult),
                        trr, inc=1)
                    t_xt2c[i] = t_xn
                self.bank_did_read(6, t_xn)
                self.bank_did_read(7, t_xn)

                if AA_PH < 1:
                    continue
                wst_war = {0: [tw["xpW1"], tw["xpW2"]], 1: None}
                tdw_pair = {}
                d_i = 0 if fwd else 1
                for og in range(32):
                    i = og % 2
                    bank = i
                    keya = (dir_, og)
                    if keya not in self.t_apre:
                        if og % 2 == 0:
                            pg = og // 2
                            sb_ = pg % 2
                            tdw_pair[sb_] = p.dma(
                                wst2[sb_][:, 0:2048].rearrange("p (a x) -> p a x", a=2),
                                self.dr[f"a{dir_}_inW"][og:og + 2]
                                    .transpose([1, 0, 2, 3])
                                    .rearrange("p a b c -> p a (b c)"),
                                wst_war[sb_])
                        sb_ = (og // 2) % 2
                        tcast = p.act.do(lambda e, i=i, sb_=sb_, half=og % 2: e.copy(
                            wbf[i][:, 0:1024],
                            wst2[sb_][:, half * 1024:half * 1024 + 1024]),
                            tdw_pair[sb_], t_wbfc[i],
                            getattr(self, "_t_app%d" % i, None), inc=1)
                        wst_war[sb_] = tcast
                        r0w = (d_i * 32 + og) * 128
                        tst_ = p.dma(self.wprea[r0w:r0w + 128, :],
                                     wbf[i][:, 0:1024], tcast)
                        self.t_apre[keya] = tst_
                        setattr(self, "_t_app%d" % i, tst_)
                    else:
                        r0w = (d_i * 32 + og) * 128
                        tcast = p.dma(wbf[i][:, 0:1024], self.wprea[r0w:r0w + 128, :],
                                      self.t_apre[keya], t_wbfc[i],
                                      getattr(self, "_t_app%d" % i, None))
                    tm = None
                    for kc in range(8):
                        def fmm(e, kc=kc, i=i, bank=bank):
                            return e.matmul(self.pb(bank), wbf[i][:, kc * 128:(kc + 1) * 128],
                                            xnq[:, kc * 512:(kc + 1) * 512],
                                            start=(kc == 0), stop=(kc == 7))
                        deps = [tcast, t_xn] + self.bank_write(bank) if kc == 0 else []
                        tm = p.pe.do(fmm, *deps, inc=(1 if kc == 7 else 0))
                    t_wbfc[i] = tm
                    self.bank_written(bank, tm)
                    if og < 16:
                        d = og
                        te = p.act.do(lambda e, d=d, bank=bank: e.copy(
                            xc(d)[:, 3:515], self.pb(bank)),
                            tm, t_conv_done.get(d), inc=1)
                        self.bank_did_read(bank, te)
                        if fwd:
                            tt1 = p.dve.do(lambda e, d=d: e.tensor_copy(
                                xc(d)[:, 0:3], self.tails[:, 3 * d:3 * d + 3].bitcast(BF16)[:, 0:3]),
                                te, tz, inc=1)
                            tt2 = p.dve.do(lambda e, d=d: e.tensor_copy(
                                self.tails[:, 3 * d:3 * d + 3].bitcast(BF16)[:, 0:3],
                                xc(d)[:, 512:515]), inc=1)
                        else:
                            tt1 = p.dve.do(lambda e, d=d: e.tensor_copy(
                                xc(d)[:, 515:518], self.tails[:, 3 * d:3 * d + 3].bitcast(BF16)[:, 0:3]),
                                te, tz, inc=1)
                            tt2 = p.dve.do(lambda e, d=d: e.tensor_copy(
                                self.tails[:, 3 * d:3 * d + 3].bitcast(BF16)[:, 0:3],
                                xc(d)[:, 3:6]), inc=1)
                        base = 0 if fwd else 3
                        tp = p.dve.do(lambda e, d=d, base=base: e.tensor_scalar_mul(
                            cc0[:], xc(d)[:, base:base + 512],
                            self.aw_conv[:, 4 * d:4 * d + 1]),
                            tt2, tw["convW"], t_ot2c[0], inc=1)
                        tp = p.dve.do(lambda e, d=d, base=base: e.scalar_tensor_tensor(
                            cc1[:], xc(d)[:, base + 1:base + 513],
                            self.aw_conv[:, 4 * d + 1:4 * d + 2], cc0[:],
                            op0=ALU.mult, op1=ALU.add), t_ot2c[1], inc=1)
                        tp = p.dve.do(lambda e, d=d, base=base: e.scalar_tensor_tensor(
                            cc0[:], xc(d)[:, base + 2:base + 514],
                            self.aw_conv[:, 4 * d + 2:4 * d + 3], cc1[:],
                            op0=ALU.mult, op1=ALU.add), inc=1)
                        tp = p.dve.do(lambda e, d=d, base=base: e.scalar_tensor_tensor(
                            cc1[:], xc(d)[:, base + 3:base + 515],
                            self.aw_conv[:, 4 * d + 3:4 * d + 4], cc0[:],
                            op0=ALU.mult, op1=ALU.add), inc=1)
                        t_ot2c[0] = tp
                        t_ot2c[1] = tp
                        tcv = p.act.do(lambda e, d=d: e.activation(
                            xc(d)[:, 3:515], cc1[:], AF.Silu,
                            bias=self.aw_cvB[:, d:d + 1]), tp, tw["convB"], inc=1)
                        t_conv_done[d] = tcv
                    else:
                        d = og - 16
                        te = p.act.do(lambda e, d=d, bank=bank: e.activation(
                            siluz[:, d * 512:(d + 1) * 512], self.pb(bank), AF.Silu),
                            tm, t_sZ.get(d), inc=1)
                        self.bank_did_read(bank, te)
                        t_sZ[d] = te
                t_inproj_w_done = [t_wbfc[0], t_wbfc[1], wst_war[0], wst_war[1],
                                   getattr(self, "_t_app0", None),
                                   getattr(self, "_t_app1", None)]

                if AA_PH < 2:
                    continue
                tx1 = tx2 = None
                for d in range(NDI):
                    deps1 = [t_conv_done[d], tw["xpW1"]] + (self.bank_write(2) if d == 0 else [])
                    tx1 = p.pe.do(lambda e, d=d: e.matmul(
                        self.pb(2, 80), self.aw_xp1[:, d * 80:(d + 1) * 80],
                        xc(d)[:, 3:515], start=(d == 0), stop=(d == NDI - 1)),
                        *deps1, inc=(1 if d == NDI - 1 else 0))
                    deps2 = [tw["xpW2"]] + (self.bank_write(3) if d == 0 else [])
                    tx2 = p.pe.do(lambda e, d=d: e.matmul(
                        self.pb(3, 16), self.aw_xp2[:, d * 16:(d + 1) * 16],
                        xc(d)[:, 3:515], start=(d == 0), stop=(d == NDI - 1)),
                        *deps2, inc=(1 if d == NDI - 1 else 0))
                self.bank_written(2, tx1)
                self.bank_written(3, tx2)
                tdq = p.act.do(lambda e: e.copy(dtqb[0:64, 0:512], self.pb(2, 80)[0:64, :]),
                               tx1, inc=1)
                tbr = p.act.do(lambda e: e.copy(brow[0:16, 0:512], self.pb(2, 80)[64:80, :]),
                               inc=1)
                self.bank_did_read(2, tbr)
                tcr = p.act.do(lambda e: e.copy(crow[0:16, 0:512], self.pb(3, 16)),
                               tx2, inc=1)
                self.bank_did_read(3, tcr)

                if AA_PH < 3:
                    continue
                t_bc = {}
                for which, rowt, tdep, dst in (("B", brow, tbr, Bb), ("C", crow, tcr, Cb)):
                    for s in range(16):
                        bank = 2 + (s % 2)
                        tb = p.pe.do(lambda e, s=s, bank=bank, rowt=rowt: e.matmul(
                            self.pb(bank), self.Es[:, s * 128:(s + 1) * 128],
                            rowt[0:16, 0:512], start=True, stop=True),
                            tdep, self.t_consts, *self.bank_write(bank), inc=1)
                        self.bank_written(bank, tb)
                        tb2 = p.act.do(lambda e, s=s, bank=bank, dst=dst: e.copy(
                            dst[:, s * 512:(s + 1) * 512], self.pb(bank)),
                            tb, t_bc.get((which, s)), inc=1)
                        self.bank_did_read(bank, tb2)
                        t_bc[(which, s)] = tb2
                t_Ball = p.act.tok()

                if AA_PH < 4:
                    continue
                t_dAc[2], t_dAc[3] = t_xt2c[0], t_xt2c[1]
                tglast = None
                for d in range(NDI):
                    if d >= AA_D:
                        continue
                    dd = ([tdq, tw["dtW"]] if d == 0 else []) + self.bank_write(4)
                    tdt = tr4 = trA = trC = tdl = tdu = None
                    if AA_S & 1:
                        tdt = p.pe.do(lambda e, d=d: e.matmul(
                            self.pb(4), self.aw_dt[:, d * 128:(d + 1) * 128], dtqb[0:64, 0:512],
                            start=True, stop=True), *dd, inc=1)
                        self.bank_written(4, tdt)
                    dcol = self.aw_dtB[:, d:d + 1]
                    aliased = t_inproj_w_done if d == 0 else []
                    if AA_S & 2:
                        tr4 = p.dve.do(lambda e, dcol=dcol: e.tensor_scalar(
                            sp4[:], self.pb(4), dcol, 0.0, op0=ALU.add, op1=ALU.max),
                            tdt, tw["dtB"], *aliased, inc=1)
                        self.bank_did_read(4, tr4)
                    if AA_S & 4:
                        trA = p.act.do(lambda e, dcol=dcol: e.activation(
                            spA[:], self.pb(4), AF.Abs, bias=dcol),
                            tdt, tr4, *aliased, inc=1)
                        self.bank_did_read(4, trA)
                        p.act.do(lambda e: e.activation(spB[:], spA[:], AF.Exp, scale=-1.0))
                        trC = p.act.do(lambda e: e.activation(spA[:], spB[:], AF.Ln, bias=1.0), inc=1)
                    if AA_S & 8:
                        tdl = p.dve.do(lambda e: e.tensor_add(delta[:], sp4[:], spA[:]),
                                       trC, tr4, inc=1)
                    if AA_S & 16:
                        p.pool.do(lambda e, d=d: e.tensor_tensor(
                            du2[:, 0:512], delta[:], xc(d)[:, 3:515], op=ALU.mult),
                            tdl, *aliased)
                        tdu = p.pool.do(lambda e, d=d: e.tensor_tensor(
                            du2[:, 512:1024], delta[:], xc(d)[:, 3:515], op=ALU.mult),
                            inc=1)
                    if AA_G < 1:
                        continue
                    for s in range(0, 16, 2):
                        i0, i1 = s % 4, (s + 1) % 4
                        ip = (s // 2) % 2
                        hi = s % NHR
                        tdx = p.pool.do(lambda e, s=s, ip=ip: e.tensor_tensor(
                            dX2[ip][:], du2[:], Bb[:, s * 512:(s + 2) * 512], op=ALU.mult),
                            tdu, t_Ball, t_dX2c[ip], inc=1)
                        acol0 = self.aw_A[:, d * 16 + s:d * 16 + s + 1]
                        acol1 = self.aw_A[:, d * 16 + s + 1:d * 16 + s + 2]
                        tda0 = p.act.do(lambda e, i0=i0, acol0=acol0: e.activation(
                            dAb[i0][:], delta[:], AF.Exp, scale=acol0),
                            tdl, t_A, t_dAc[i0], inc=1)
                        tda1 = p.act.do(lambda e, i1=i1, acol1=acol1: e.activation(
                            dAb[i1][:], delta[:], AF.Exp, scale=acol1),
                            t_dAc[i1], inc=1)
                        scol0 = self.states_a[:, sbase + d * 16 + s:sbase + d * 16 + s + 1]
                        scol1 = self.states_a[:, sbase + d * 16 + s + 1:sbase + d * 16 + s + 2]
                        if AA_G < 2:
                            continue
                        if fwd:
                            tsc0 = p.dve.do(lambda e, i0=i0, hi=hi, ip=ip, scol0=scol0:
                                            e.tensor_tensor_scan(
                                hring[:, hi * 512:(hi + 1) * 512], dAb[i0][:],
                                dX2[ip][:, 0:512], scol0, op0=ALU.mult, op1=ALU.add),
                                tda0, tdx, t_hr[hi], t_hrp[hi], inc=1)
                            tsc1 = p.dve.do(lambda e, i1=i1, hi=hi, ip=ip, scol1=scol1:
                                            e.tensor_tensor_scan(
                                hring[:, (hi + 1) * 512:(hi + 2) * 512], dAb[i1][:],
                                dX2[ip][:, 512:1024], scol1, op0=ALU.mult, op1=ALU.add),
                                tda1, t_hr[hi + 1], t_hrp[hi + 1], inc=1)
                        else:
                            tsc0 = p.dve.do(lambda e, i0=i0, hi=hi, ip=ip, scol0=scol0:
                                            e.tensor_tensor_scan(
                                hring[:, hi * 512:(hi + 1) * 512][:, ::-1],
                                dAb[i0][:, ::-1], dX2[ip][:, 0:512][:, ::-1],
                                scol0, op0=ALU.mult, op1=ALU.add),
                                tda0, tdx, t_hr[hi], t_hrp[hi], inc=1)
                            tsc1 = p.dve.do(lambda e, i1=i1, hi=hi, ip=ip, scol1=scol1:
                                            e.tensor_tensor_scan(
                                hring[:, (hi + 1) * 512:(hi + 2) * 512][:, ::-1],
                                dAb[i1][:, ::-1], dX2[ip][:, 512:1024][:, ::-1],
                                scol1, op0=ALU.mult, op1=ALU.add),
                                tda1, t_hr[hi + 1], t_hrp[hi + 1], inc=1)
                        t_dAc[i0] = tsc0
                        t_dAc[i1] = tsc1
                        t_dX2c[ip] = tsc1
                        if AA_G < 3:
                            continue
                        ypeng = p.pool if (s // 2) % 2 == 0 else p.dve
                        typ = ypeng.do(lambda e, s=s, ip=ip, hi=hi: e.tensor_tensor(
                            ypb2[ip][:], hring[:, hi * 512:(hi + 2) * 512],
                            Cb[:, s * 512:(s + 2) * 512], op=ALU.mult),
                            tsc0, tsc1, t_yp2c[ip], inc=1)
                        t_hrp[hi] = typ
                        t_hrp[hi + 1] = typ
                        p.pe.do(lambda e, ip=ip, s=s: e.matmul(
                            self.pb(5), self.eye[:], ypb2[ip][:, 0:512],
                            start=(s == 0), stop=False),
                            typ, *(self.bank_write(5) + [self.t_consts] if s == 0 else []))
                        tya = p.pe.do(lambda e, ip=ip, s=s: e.matmul(
                            self.pb(5), self.eye[:], ypb2[ip][:, 512:1024],
                            start=False, stop=(s == 14)), inc=1)
                        t_yp2c[ip] = tya
                        if AA_G < 4:
                            continue
                        s1 = s + 1
                        hi2 = hi + 1
                        if hi2 == NHR - 1 or s1 == 15:
                            n = hi2 + 1
                            off = 511 if fwd else 0
                            s0_ = s1 - hi2
                            def fsv(e, d=d, s0_=s0_, n=n, off=off):
                                return e.tensor_copy(
                                    self.states_a[:, sbase + d * 16 + s0_:sbase + d * 16 + s0_ + n],
                                    hring[:, off::512][:, 0:n])
                            tsv = p.dve.do(fsv, inc=1)
                            for hh in range(n):
                                t_hr[hh] = tsv
                    if AA_G < 5:
                        continue
                    self.bank_written(5, tya)
                    tg1 = p.dve.do(lambda e, d=d: e.scalar_tensor_tensor(
                        g1b[:], xc(d)[:, 3:515], self.aw_D[:, d:d + 1], self.pb(5),
                        op0=ALU.mult, op1=ALU.add),
                        tya, tw["D"], t_gat.get(d), inc=1)
                    self.bank_did_read(5, tg1)
                    tgt = p.pool.do(lambda e, d=d: e.tensor_tensor(
                        gated[:, d * 512:(d + 1) * 512], g1b[:], siluz[:, d * 512:(d + 1) * 512],
                        op=ALU.mult),
                        tg1, t_sZ.get(d), t_gat.get(d), inc=1)
                    t_gat[d] = tgt
                    tglast = tgt
                t_scan_end = [t for t in [p.dve.tok(), t_gat.get(15), p.act.tok(),
                                          t_yp2c.get(0), t_yp2c.get(1)] if t]

                if AA_PH < 5:
                    continue
                ost_war = {0: t_scan_end, 1: t_scan_end}
                d_i = 0 if fwd else 1
                for og in range(8):
                    i = og % 2
                    bank = i
                    keyo = (dir_, og)
                    r0w = (d_i * 8 + og) * 128
                    if keyo not in self.t_opre:
                        sb_ = og % 2
                        dd = ost_war[sb_]
                        if not isinstance(dd, list):
                            dd = [dd]
                        tdw = p.dma(wst2[sb_][:, 0:2048],
                                    self.dr[f"a{dir_}_outW"][og]
                                        .rearrange("p a b -> p (a b)"),
                                    *[x for x in dd if x])
                        tcast = p.act.do(lambda e, i=i, sb_=sb_: e.copy(
                            wbf[i][:, 0:2048], wst2[sb_][:, 0:2048]), tdw, t_wbfc[i],
                            getattr(self, "_t_opp%d" % i, None),
                            getattr(self, "_t_app%d" % i, None), inc=1)
                        ost_war[sb_] = tcast
                        tso = p.dma(self.wpreo[r0w:r0w + 128, :],
                                    wbf[i][:, 0:2048], tcast)
                        self.t_opre[keyo] = tso
                        setattr(self, "_t_opp%d" % i, tso)
                    else:
                        tcast = p.dma(wbf[i][:, 0:2048], self.wpreo[r0w:r0w + 128, :],
                                      self.t_opre[keyo], t_wbfc[i],
                                      getattr(self, "_t_opp%d" % i, None),
                                      getattr(self, "_t_app%d" % i, None),
                                      *([x for x in (t_scan_end if isinstance(
                                          t_scan_end, list) else [t_scan_end]) if x]
                                        if og < 2 else []))
                    tm = None
                    for kc in range(16):
                        def fmm(e, kc=kc, i=i, bank=bank):
                            return e.matmul(self.pb(bank), wbf[i][:, kc * 128:(kc + 1) * 128],
                                            gated[:, kc * 512:(kc + 1) * 512],
                                            start=(kc == 0), stop=(kc == 15))
                        deps = [tcast, t_gat.get(15)] + self.bank_write(bank) if kc == 0 else []
                        tm = p.pe.do(fmm, *deps, inc=(1 if kc == 15 else 0))
                    t_wbfc[i] = tm
                    self.bank_written(bank, tm)
                    rows = slice(og * 128, (og + 1) * 128)
                    if fwd:
                        tev = p.act.do(lambda e, i=i, bank=bank: e.copy(ot2[i][:], self.pb(bank)),
                                       tm, t_ot2c[i], inc=1)
                        self.bank_did_read(bank, tev)
                        tst = p.dma(self.hbm_f[rows, cs], ot2[i][:], tev)
                        t_ot2c[i] = tst
                    else:
                        tx = p.dma(xt2[i][:], self.dr["xT"][rows, cs], t_xt2c[i])
                        tf = p.dma(ot2[i][:], self.hbm_f[rows, cs], t_ot2c[i])
                        p.dve.do(lambda e, i=i, bank=bank: e.tensor_tensor(
                            xt2[i][:], xt2[i][:], self.pb(bank), op=ALU.add),
                            tx, tm)
                        ta2 = p.dve.do(lambda e, i=i: e.tensor_tensor(
                            xt2[i][:], xt2[i][:], ot2[i][:], op=ALU.add), tf, inc=1)
                        self.bank_did_read(bank, ta2)
                        tst = p.dma(self.hbm_s[rows, cs], xt2[i][:], ta2)
                        t_xt2c[i] = tst
                        t_ot2c[i] = ta2
                p.barrier()
        A.pop()

    def emit_hbm_copy(self, dst, src):
        p, A = self.p, self.AR
        A.push()
        buf = [A.alloc(2048) for _ in range(2)]
        tc_ = {0: None, 1: None}
        for g in range(8):
            i = g % 2
            rows = slice(g * 128, (g + 1) * 128)
            td = p.dma(buf[i][:], src[rows, :], tc_[i])
            ts_ = p.dma(dst[rows, :], buf[i][:], td)
            tc_[i] = ts_
        A.pop()

    def emit_dump(self, slot, src_tiles):
        p = self.p
        p.barrier()
        for g in range(8):
            p.dma(self.dbgT[slot * D + g * 128:slot * D + (g + 1) * 128, :],
                  src_tiles(g))
        p.barrier()

    def finish(self):
        self.p.barrier()


def build_nc(shared, upto):
    nc = bass.Bass("TRN2", target_bir_lowering=False,
                   detect_race_conditions=bool(int(os.environ.get("RACE", "0"))))
    b = Builder(nc, shared, upto)
    sems = {}
    cms = []
    for k in ("pe", "act", "dve", "pool", "sp",
              "d0", "d1", "d2", "d3", "d4", "d5", "d6", "d7"):
        cm = nc.semaphore("S_" + k)
        sems[k] = cm.__enter__()
        cms.append(cm)
    blk = nc.Block()
    block = blk.__enter__()
    b.load_consts()
    b.emit_attn_a()
    if FF_DBG & 8 and upto != "attn_a":
        b.p.barrier()
        b.emit_hbm_copy(b.dbgT[6 * D:7 * D, :], b.hbm_s)
        b.p.barrier()
    if upto == "attn_a":
        b.emit_hbm_copy(b.out, b.hbm_s)
    elif upto == "ffn1":
        b.emit_ffn(1, b.hbm_s, None)
    elif upto == "attn_b":
        b.emit_ffn(1, b.hbm_s, b.hbm_f)
        b.emit_attn_b(b.hbm_f, b.out)
    else:
        b.emit_ffn(1, b.hbm_s, b.hbm_f)
        b.emit_attn_b(b.hbm_f, b.hbm_s)
        b.emit_ffn(2, b.hbm_s, None)
    b.finish()
    b.p.emit(block, sems)
    blk.__exit__(None, None, None)
    for cm in cms:
        cm.__exit__(None, None, None)
    return b, nc


def kernel(**inputs):
    key = "full"
    shared = stage_shared(inputs)
    if key not in _built:
        b, nc = build_nc(shared, key)
        _built[key] = nc
    nc = _built[key]
    xs = np.asarray(inputs["L0_attn_a_x"], np.float32)
    in_maps = []
    for c in range(B):
        m = dict(shared)
        m["xT"] = np.ascontiguousarray(xs[c].T)
        in_maps.append(m)
    res = run_bass_kernel_spmd(nc, in_maps, core_ids=list(range(B)))
    out = np.stack([np.asarray(r["outT"]).T for r in res.results], axis=0)
    return out


def emit_ffn(self, app, src_hbm, dst_hbm):
    p, A = self.p, self.AR
    dst = dst_hbm if dst_hbm is not None else self.out
    self._bank_r, self._bank_w = {}, {}
    self._t_bc = None
    self._t_ftc = {0: None, 1: None}
    A.push()
    ft = A.alloc(16384)
    ftv = lambda g: ft[:, g * 2048:(g + 1) * 2048]
    tr1 = A.alloc(512); tr2 = A.alloc(512); tr3 = A.alloc(512)
    trb = A.alloc_bf(1024)
    rnt = A.alloc(512)
    rnt_bf = rnt.bitcast(BF16)
    mu2 = self.rows_a[0:1, :]
    r2_ = self.rows_a[32:33, :]
    t_stats = self.emit_stats(src_hbm, mu2, r2_, EPS)
    U = A.alloc(23824)
    Ub = U.bitcast(BF16)
    ftbf = lambda k: Ub[:, k * 1026:(k + 1) * 1026]
    qh = lambda g: Ub[:, 8208 + g * 2048:8208 + (g + 1) * 2048]
    vt = lambda t: Ub[:, 24592 + t * 512:24592 + (t + 1) * 512]
    vw = lambda k: Ub[:, 32784 + k * 512:32784 + (k + 1) * 512]
    rot = Ub[:, 36880:40976]
    wstqk = U[:, 20488:21512]
    wbfqk = [Ub[:, 43024:44048], Ub[:, 44048:45072]]
    OT = lambda g: Ub[:, g * 2048:(g + 1) * 2048]
    msk = lambda j: Ub[:, 32784 + j * 512:32784 + (j + 1) * 512]
    tsb = [U[:, 18440:18952], U[:, 18952:19464]]
    ptb = [Ub[:, 38928:39440], Ub[:, 39440:39952]]
    rbsb = U[0:64, 19976:20488]
    hs2n = ftbf
    w2st = U[:, 4104:6920]
    w2bf = [Ub[:, 13840:16656], Ub[:, 39184:42000]]
    h2 = lambda k: Ub[:, 16656 + k * 1024:16656 + (k + 1) * 1024]
    w1st = U[:, 19592:21640]
    w1bf = [Ub[:, 43280:45328], Ub[:, 45328:47376]]
    qn_slot = [(self.rows_a, 64), (self.rows_b, 0), (self.rows_b, 32), (self.rows_b, 64)]
    kn_slot = [(self.rows_c, 0), (self.rows_c, 32), (self.rows_c, 64), (self.rows_a, 0)]
    row = lambda sl, cs, n=1: sl[0][sl[1]:sl[1] + n, cs]

    self._bank_r[6] = [t_stats]
    self._bank_r[7] = [t_stats]
    for tc in range(4):
        cs = slice(512 * tc, 512 * (tc + 1))
        tmu = self.bcast_row(6, mu2[:, cs], tuple(self.bank_write(6)) + (t_stats,))
        self.bank_written(6, tmu)
        trr = self.bcast_row(7, r2_[:, cs], tuple(self.bank_write(7)))
        self.bank_written(7, trr)
        tl = None
        if not hasattr(self, "_t_ftc"):
            self._t_ftc = {0: None, 1: None}
        tc_c = self._t_ftc
        for g in range(8):
            i = g % 2
            src = tr1 if i == 0 else tr2
            td = p.dma(src[:], src_hbm[g * 128:(g + 1) * 128, cs], tc_c[i])
            p.dve.do(lambda e, src=src: e.tensor_tensor(src[:], src[:], self.pb(6),
                                                        op=ALU.subtract), td, tmu)
            tl = p.dve.do(lambda e, g=g, cs=cs, src=src: e.tensor_tensor(
                ftv(g)[:, cs], src[:], self.pb(7), op=ALU.mult), trr, inc=1)
            tc_c[i] = tl
        self.bank_did_read(6, tl)
        self.bank_did_read(7, tl)
    p.barrier()
    if FF_DBG & 1:
        self.emit_dump(0, ftv)
        p.dma(self.dbgT[7 * D:7 * D + 128, :], self.rows_a[0:128, :])
        p.barrier()

    for l in range(FF_L):
        pf = f"f{l}_"
        tqs = p.dma(self.fw_qs[:], self.dr[pf + "qs"][:])
        tks = p.dma(self.fw_ks[:], self.dr[pf + "ks"][:])
        tlg = p.dma(self.col16[:, 0:8], self.dr[pf + "lng"][:])
        tlb = p.dma(self.col16[:, 8:16], self.dr[pf + "lnb"][:])
        tb1 = (p.dma(self.aw_dtB[:, 0:16], self.dr[pf + "b1a"][:, 0:16]),
               p.dma(self.aw_D[:, 0:6], self.dr[pf + "b1a"][:, 16:22]),
               p.dma(self.aw_cvB[:, 0:16], self.dr[pf + "b1g"][:, 0:16]),
               p.dma(self.aw_conv[:, 0:6], self.dr[pf + "b1g"][:, 16:22]))
        tb2 = p.dma(self.col8b[:], self.dr[pf + "b2"][:])
        lnb_bf = self.col8c[:, 0:4].bitcast(BF16)[:, 0:8]
        tlnb = p.pool.do(lambda e: e.tensor_copy(lnb_bf, self.col16[:, 8:16]), tlb, inc=1)
        t_vw = None
        for kc in range(8):
            tdv = p.dma(w2st[:, 0:512], self.dr[pf + "vW"][kc], t_vw)
            t_vw = p.pool.do(lambda e, kc=kc: e.tensor_copy(vw(kc), w2st[:, 0:512]),
                             tdv, inc=1)
        t_vw_all = t_vw

        for h in range(2):
            hc = slice(1024 * h, 1024 * (h + 1))
            trot = tuple(
                p.dma(rot[:, tb * 1024:(tb + 1) * 1024],
                      self.dr["c_rot"][:, tb * 2048 + 1024 * h:
                                       tb * 2048 + 1024 * h + 1024])
                for tb in range(4))
            for kc in range(8):
                p.pool.do(lambda e, kc=kc, hc=hc: e.tensor_copy(
                    ftbf(kc)[:, 1:1025], ftv(kc)[:, hc]))
                if h == 0:
                    p.pool.do(lambda e, kc=kc: e.memset(ftbf(kc)[:, 0:1], 0.0),
                              inc=(1 if kc == 7 else 0))
                else:
                    p.pool.do(lambda e, kc=kc: e.tensor_copy(
                        ftbf(kc)[:, 0:1], ftv(kc)[:, 1023:1024]),
                        inc=(1 if kc == 7 else 0))
            t_r1_all = p.pool.tok()
            p.barrier()

            for og in range(8):
                i = og % 2
                keyq = (l, og)
                if keyq not in self.t_qkpre:
                    tdw = p.dma(wstqk[:],
                                self.dr[pf + "qkW"][og].rearrange("p a b -> p (a b)"),
                                getattr(self, "_t_wstqk", None))
                    tcw = p.pool.do(lambda e, i=i: e.tensor_copy(
                        wbfqk[i][:, 0:1024], wstqk[:]),
                        tdw, getattr(self, "_t_wbq%d" % i, None),
                        getattr(self, "_t_qkpp%d" % i, None),
                        getattr(self, "_t_wopp%d" % i, None), inc=1)
                    self._t_wstqk = tcw
                    r0w = (l * 8 + og) * 128
                    tsq_ = p.dma(self.wpreq[r0w:r0w + 128, :], wbfqk[i][:, 0:1024], tcw)
                    self.t_qkpre[keyq] = tsq_
                    setattr(self, "_t_qkpp%d" % i, tsq_)
                else:
                    r0w = (l * 8 + og) * 128
                    tcw = p.dma(wbfqk[i][:, 0:1024], self.wpreq[r0w:r0w + 128, :],
                                self.t_qkpre[keyq],
                                getattr(self, "_t_wbq%d" % i, None),
                                getattr(self, "_t_qkpp%d" % i, None),
                                getattr(self, "_t_wopp%d" % i, None))
                g = og % 4
                isq = og < 4
                for tcl in range(2):
                    tc = 2 * h + tcl
                    bank = tcl % 2
                    tm = None
                    for kc in range(8):
                        def fmm(e, kc=kc, i=i, bank=bank, tcl=tcl):
                            sh = 1 if kc < 4 else 0
                            return e.matmul(self.pb(bank),
                                            wbfqk[i][:, kc * 128:(kc + 1) * 128],
                                            ftbf(kc)[:, tcl * 512 + sh:tcl * 512 + sh + 512],
                                            start=(kc == 0), stop=(kc == 7))
                        deps = ([tcw, t_r1_all, trot] + self.bank_write(bank)) if kc == 0 else []
                        tm = p.pe.do(fmm, *deps, inc=(1 if kc == 7 else 0))
                    setattr(self, "_t_wbq%d" % i, tm)
                    self.bank_written(bank, tm)
                    tsq = p.act.do(lambda e, bank=bank: e.activation(
                        trb[:, 0:512], self.pb(bank), AF.Square), tm,
                        getattr(self, "_t_trb0", None), inc=1)
                    self.bank_did_read(bank, tsq)
                    tn = p.pe.do(lambda e: e.matmul(self.pb(2, 2), self.hsel[:],
                                                    trb[:, 0:512], start=True, stop=True),
                                 tsq, self.t_consts, *self.bank_write(2), inc=1)
                    self.bank_written(2, tn)
                    tmx = p.dve.do(lambda e: e.tensor_scalar(
                        rnt[0:2, 0:512], self.pb(2, 2), 1e-24, None, op0=ALU.max),
                        tn, getattr(self, "_t_rnt", None), inc=1)
                    self.bank_did_read(2, tmx)
                    p.act.do(lambda e: e.activation(rnt[0:2, 0:512], rnt[0:2, 0:512], AF.Ln),
                             tmx, self.t_k0)
                    if isq:
                        tre = p.act.do(lambda e: e.activation(
                            rnt[0:2, 0:512], rnt[0:2, 0:512], AF.Exp, scale=-0.5,
                            bias=self.konst[0:2, 1:2]), inc=1)
                    else:
                        tre = p.act.do(lambda e: e.activation(
                            rnt[0:2, 0:512], rnt[0:2, 0:512], AF.Exp, scale=-0.5), inc=1)
                    trn = self.bcast_row(3, rnt[0:2, 0:512],
                                         tuple(self.bank_write(3)) + (tre,))
                    self.bank_written(3, trn)
                    self._t_rnt = trn
                    tev = p.act.do(lambda e: e.copy(tr1[:], self.pb(3)), trn,
                                   getattr(self, "_t_tr1f", None), inc=1)
                    self.bank_did_read(3, tev)
                    qs_col = self.fw_qs[:, 0:1] if isq else self.fw_ks[:, 0:1]
                    tqh = p.dve.do(lambda e, bank=bank, qs_col=qs_col: e.scalar_tensor_tensor(
                        trb[:, 512:1024], self.pb(bank), qs_col, tr1[:],
                        op0=ALU.mult, op1=ALU.mult),
                        tev, tqs if isq else tks, tsq,
                        getattr(self, "_t_trb1", None), inc=1)
                    self.bank_did_read(bank, tqh)
                    self._t_tr1f = tqh
                    self._t_trb0 = tqh
                    tsw = p.pe.do(lambda e: e.matmul(self.pb(4), self.swap[:],
                                                     trb[:, 512:1024], start=True, stop=True),
                                  tqh, *self.bank_write(4), inc=1)
                    self.bank_written(4, tsw)
                    self._t_trb1 = tsw
                    tboff = 0 if isq else 2048
                    cs_t = rot[:, tboff + tcl * 512:tboff + tcl * 512 + 512]
                    ss_t = rot[:, tboff + 1024 + tcl * 512:tboff + 1024 + tcl * 512 + 512]
                    tt1 = p.pool.do(lambda e, cs_t=cs_t: e.tensor_tensor(
                        tr2[:], trb[:, 512:1024], cs_t, op=ALU.mult),
                        tqh, trot, getattr(self, "_t_tr2f", None), inc=1)
                    tt2 = p.dve.do(lambda e, ss_t=ss_t: e.tensor_tensor(
                        tr3[:], self.pb(4), ss_t, op=ALU.mult),
                        tsw, getattr(self, "_t_tr3f", None), inc=1)
                    self.bank_did_read(4, tt2)
                    gq = g if isq else 4 + g
                    tqp = p.dve.do(lambda e, gq=gq, tc=tc: e.tensor_tensor(
                        qh(gq)[:, tc * 512:(tc + 1) * 512], tr2[:], tr3[:], op=ALU.add),
                        tt1, getattr(self, "_t_qh%d_%d" % (gq, tc), None), inc=1)
                    setattr(self, "_t_qh%d_%d" % (gq, tc), tqp)
                    self._t_tr2f = tqp
                    self._t_tr3f = tqp
                    tsq2 = p.act.do(lambda e: e.activation(
                        trb[:, 0:512], trb[:, 512:1024], AF.Square),
                        tqh, inc=1)
                    self._t_trb0 = tsq2
                    self._t_trb1 = (tsw, tsq2)
                    tn2 = p.pe.do(lambda e: e.matmul(self.pb(2, 2), self.hsel[:],
                                                     trb[:, 0:512], start=True, stop=True),
                                  tsq2, *self.bank_write(2), inc=1)
                    self.bank_written(2, tn2)
                    slot = (qn_slot if isq else kn_slot)[g]
                    dstrow = row(slot, slice(tc * 512, (tc + 1) * 512), 2)
                    p.act.do(lambda e, dstrow=dstrow: e.activation(
                        dstrow, self.pb(2, 2), AF.Ln), tn2)
                    tnr = p.act.do(lambda e, dstrow=dstrow: e.activation(
                        dstrow, dstrow, AF.Exp, scale=0.5), inc=1)
                    self.bank_did_read(2, tnr)
            for tt in range(8):
                gt = 8 * h + tt
                tm = None
                for kc in range(8):
                    def fvm(e, kc=kc, tt=tt):
                        sh = 1 if kc < 4 else 0
                        return e.matmul(self.pb(5),
                                        ftbf(kc)[:, tt * 128 + sh:tt * 128 + sh + 128],
                                        vw(kc), start=(kc == 0), stop=(kc == 7))
                    deps = ([t_vw_all, t_r1_all] + self.bank_write(5)) if kc == 0 else []
                    tm = p.pe.do(fvm, *deps, inc=(1 if kc == 7 else 0))
                self.bank_written(5, tm)
                tev = p.act.do(lambda e, gt=gt: e.copy(vt(gt), self.pb(5)), tm, inc=1)
                self.bank_did_read(5, tev)
            p.barrier()

        if FF_PH < 2:
            continue
        self._bank_r, self._bank_w = {}, {}
        tmk = tuple(p.dma(msk(j), self.dr["c_masks"][:, j * 512:(j + 1) * 512])
                    for j in range(8))
        p.barrier()
        for g in range(4):
            for w in range(4):
                q0 = 512 * w
                j0 = 4 if w == 0 else 0
                kwin = slice(max(512 * (w - 1), 0), q0 + 512)
                knr2 = row(kn_slot[g], kwin, 2)
                qnr2 = row(qn_slot[g], slice(q0, q0 + 512), 2)
                negb2 = self.negb_t[0:2, :]
                tmx = p.dve.do(lambda e, knr2=knr2: e.tensor_reduce(
                    rnt[0:2, 0:1], knr2, axis=mybir.AxisListType.X, op=ALU.max),
                    getattr(self, "_t_negb", None), inc=1)
                p.dve.do(lambda e, qnr2=qnr2: e.tensor_copy(tr3[0:2, 0:512], qnr2))
                p.dve.do(lambda e: e.tensor_scalar(
                    tr3[0:2, 0:512], tr3[0:2, 0:512], rnt[0:2, 0:1], -1.0,
                    op0=ALU.mult, op1=ALU.mult))
                tnb = p.dve.do(lambda e, negb2=negb2: e.tensor_copy(
                    negb2, tr3[0:2, 0:512]), inc=1)
                for sub in range(2):
                    hd = 2 * g + sub
                    r0 = 64 * sub
                    tav = tdn = None
                    js = list(range(j0, 8))
                    QB = (0, 1, 5, 6)
                    tqk_l = {}
                    for idx in range(len(js) + 2):
                        if idx < len(js):
                            j = js[idx]
                            bank = QB[idx % 4]
                            p.pe.do(lambda e, j=j, bank=bank, g=g, r0=r0, q0=q0,
                                    w=w: e.matmul(
                                self.pb(bank),
                                qh(4 + g)[r0:r0 + 64,
                                          512 * (w - 1) + j * 128:
                                          512 * (w - 1) + j * 128 + 128],
                                qh(g)[r0:r0 + 64, q0:q0 + 512],
                                start=True, stop=False),
                                *self.bank_write(bank), inc=0)
                            tqk = p.pe.do(lambda e, bank=bank, sub=sub, negb2=negb2:
                                          e.matmul(
                                self.pb(bank), self.rsel[:, sub * 128:(sub + 1) * 128],
                                negb2, start=False, stop=True), tnb, inc=1)
                            self.bank_written(bank, tqk)
                            tqk_l[idx] = tqk
                        if idx >= 2:
                            i2 = idx - 2
                            j = js[i2]
                            bank = QB[i2 % 4]
                            kt = 4 * (w - 1) + j
                            ii = i2 % 2
                            tst = p.dve.do(lambda e, bank=bank, j=j, ii=ii:
                                           e.scalar_tensor_tensor(
                                tsb[ii][:], self.pb(bank), -84.0, msk(j),
                                op0=ALU.max, op1=ALU.add),
                                tqk_l[i2], tmk,
                                getattr(self, "_t_tsb%d" % ii, None), inc=1)
                            self.bank_did_read(bank, tst)
                            tex = p.act.do(lambda e, ii=ii: e.activation(
                                ptb[ii][:], tsb[ii][:], AF.Exp),
                                tst, getattr(self, "_t_pt%d" % ii, None), inc=1)
                            setattr(self, "_t_tsb%d" % ii, tex)
                            tav = p.pe.do(lambda e, kt=kt, hd=hd, ii=ii, i2=i2,
                                          lj=len(js): e.matmul(
                                self.pb(2, 64), vt(kt)[:, hd * 64:(hd + 1) * 64],
                                ptb[ii][:],
                                start=(i2 == 0), stop=(i2 == lj - 1)),
                                tex, *(self.bank_write(2) if i2 == 0 else []), inc=1)
                            tdn = p.pe.do(lambda e, ii=ii, i2=i2, lj=len(js): e.matmul(
                                self.pb(3, 1), self.ones_bf[:, 0:1], ptb[ii][:],
                                start=(i2 == 0), stop=(i2 == lj - 1)),
                                *(self.bank_write(3) if i2 == 0 else []), inc=1)
                            setattr(self, "_t_pt%d" % ii, tdn)
                    self.bank_written(2, tav)
                    self.bank_written(3, tdn)
                    trc = p.dve.do(lambda e: e.reciprocal(rnt[0:1, 0:512], self.pb(3, 1)),
                                   tdn, tmx, inc=1)
                    self.bank_did_read(3, trc)
                    trb_ = self.bcast_row(4, rnt[0:1, 0:512],
                                          tuple(self.bank_write(4)) + (trc,), rows=64)
                    self.bank_written(4, trb_)
                    tevr = p.act.do(lambda e: e.copy(rbsb[:, 0:512], self.pb(4, 64)), trb_,
                                    getattr(self, "_t_rbsb", None), inc=1)
                    self.bank_did_read(4, tevr)
                    tot = p.dve.do(lambda e, g=g, r0=r0, q0=q0: e.tensor_tensor(
                        OT(g)[r0:r0 + 64, q0:q0 + 512], self.pb(2, 64), rbsb[:, 0:512],
                        op=ALU.mult),
                        tav, tevr, getattr(self, "_t_OT%d" % g, None), inc=1)
                    self.bank_did_read(2, tot)
                    self._t_rbsb = tot
                    setattr(self, "_t_OT%d" % g, tot)
                self._t_negb = p.pe.tok()
        p.barrier()

        if FF_PH < 3:
            continue
        self._bank_r, self._bank_w = {}, {}
        for og in range(8):
            i = og % 2
            keyw = (l, og)
            r0w = (l * 8 + og) * 128
            if keyw not in self.t_wopre:
                tdw = p.dma(wstqk[:, 0:512],
                            self.dr[pf + "woW"][og].rearrange("p a b -> p (a b)"),
                            getattr(self, "_t_wstqk", None))
                tcw = p.pool.do(lambda e, i=i: e.tensor_copy(
                    wbfqk[i][:, 0:512], wstqk[:, 0:512]),
                    tdw, getattr(self, "_t_wbq%d" % i, None),
                    getattr(self, "_t_wopp%d" % i, None),
                    getattr(self, "_t_qkpp%d" % i, None), inc=1)
                self._t_wstqk = tcw
                tsw = p.dma(self.wprew[r0w:r0w + 128, :], wbfqk[i][:, 0:512], tcw)
                self.t_wopre[keyw] = tsw
                setattr(self, "_t_wopp%d" % i, tsw)
            else:
                tcw = p.dma(wbfqk[i][:, 0:512], self.wprew[r0w:r0w + 128, :],
                            self.t_wopre[keyw],
                            getattr(self, "_t_wbq%d" % i, None),
                            getattr(self, "_t_wopp%d" % i, None),
                            getattr(self, "_t_qkpp%d" % i, None))
            for tc in range(4):
                bank = tc % 2
                tm = None
                for kc in range(4):
                    def fwo(e, kc=kc, i=i, bank=bank, tc=tc):
                        return e.matmul(self.pb(bank), wbfqk[i][:, kc * 128:(kc + 1) * 128],
                                        OT(kc)[:, tc * 512:(tc + 1) * 512],
                                        start=(kc == 0), stop=(kc == 3))
                    deps = ([tcw] + self.bank_write(bank)) if kc == 0 else []
                    tm = p.pe.do(fwo, *deps, inc=(1 if kc == 3 else 0))
                setattr(self, "_t_wbq%d" % i, tm)
                self.bank_written(bank, tm)
                tup = p.dve.do(lambda e, og=og, tc=tc, bank=bank: e.tensor_tensor(
                    ftv(og)[:, tc * 512:(tc + 1) * 512],
                    ftv(og)[:, tc * 512:(tc + 1) * 512], self.pb(bank), op=ALU.add),
                    tm, inc=1)
                self.bank_did_read(bank, tup)
        p.barrier()

        if FF_DBG & 2:
            self.emit_dump(1 + 3 * l, ftv)
        if FF_PH < 4:
            continue
        self._bank_r, self._bank_w = {}, {}
        s_lo = self.rows_a[0:1, :]; s_hi = self.rows_a[32:33, :]
        q_lo = self.rows_b[0:1, :]; q_hi = self.rows_b[32:33, :]
        mus = self.rows_a[64:65, :]; rs_ = self.rows_b[64:65, :]
        for tc in range(4):
            cs = slice(512 * tc, 512 * (tc + 1))
            tml = None
            for og in range(8):
                sb_ = 0 if og < 4 else 1
                qb_ = 2 if og < 4 else 3
                tcp = p.pool.do(lambda e, og=og, cs=cs: e.tensor_copy(
                    trb[:, 0:512], ftv(og)[:, cs]), getattr(self, "_t_trbW", None), inc=1)
                tsq = p.act.do(lambda e, og=og, cs=cs: e.activation(
                    trb[:, 512:1024], ftv(og)[:, cs], AF.Square),
                    getattr(self, "_t_trbW2", None), inc=1)
                tm1 = p.pe.do(lambda e, sb_=sb_, og=og: e.matmul(
                    self.pb(sb_, 1), self.ones_bf[:, 0:1], trb[:, 0:512],
                    start=(og % 4 == 0), stop=(og % 4 == 3)),
                    tcp, *(self.bank_write(sb_) if og % 4 == 0 else []), inc=1)
                self._t_trbW = tm1
                tm2 = p.pe.do(lambda e, qb_=qb_, og=og: e.matmul(
                    self.pb(qb_, 1), self.ones_bf[:, 0:1], trb[:, 512:1024],
                    start=(og % 4 == 0), stop=(og % 4 == 3)),
                    tsq, *(self.bank_write(qb_) if og % 4 == 0 else []), inc=1)
                self._t_trbW2 = tm2
                if og % 4 == 3:
                    self.bank_written(sb_, tm1)
                    self.bank_written(qb_, tm2)
                tml = tm2
            for bnk, dstr in ((0, s_lo), (1, s_hi), (2, q_lo), (3, q_hi)):
                tev = p.dve.do(lambda e, bnk=bnk, dstr=dstr, cs=cs: e.tensor_copy(
                    dstr[:, cs], self.pb(bnk, 1)), tml, inc=1)
                self.bank_did_read(bnk, tev)
        p.barrier()
        tmpA = self.bc_rows[0:1, 512:1024]
        tmpB = rnt[0:1, 0:512]
        for tc in range(4):
            cs = slice(512 * tc, 512 * (tc + 1))
            if tc == 0:
                p.dve.do(lambda e: e.memset(tmpA[:, 0:1], 0.0))
                p.dve.do(lambda e: e.tensor_copy(tmpA[:, 1:512], s_hi[:, 0:511]))
            else:
                p.dve.do(lambda e, tc=tc: e.tensor_copy(
                    tmpA[:], s_hi[:, tc * 512 - 1:(tc + 1) * 512 - 1]))
            p.dve.do(lambda e, cs=cs: e.tensor_tensor(tmpA[:], tmpA[:], s_lo[:, cs],
                                                      op=ALU.add))
            p.dve.do(lambda e, cs=cs: e.tensor_scalar_mul(mus[:, cs], tmpA[:], 1.0 / 1024))
            if tc == 0:
                p.dve.do(lambda e: e.memset(tmpA[:, 0:1], 0.0))
                p.dve.do(lambda e: e.tensor_copy(tmpA[:, 1:512], q_hi[:, 0:511]))
            else:
                p.dve.do(lambda e, tc=tc: e.tensor_copy(
                    tmpA[:], q_hi[:, tc * 512 - 1:(tc + 1) * 512 - 1]))
            p.dve.do(lambda e, cs=cs: e.tensor_tensor(tmpA[:], tmpA[:], q_lo[:, cs],
                                                      op=ALU.add))
            p.dve.do(lambda e: e.tensor_scalar_mul(tmpA[:], tmpA[:], 1.0 / 1024))
            p.dve.do(lambda e, cs=cs: e.tensor_tensor(tmpB[:], mus[:, cs], mus[:, cs],
                                                      op=ALU.mult))
            tva = p.dve.do(lambda e: e.tensor_tensor(tmpA[:], tmpA[:], tmpB[:],
                                                     op=ALU.subtract), inc=1)
            p.act.do(lambda e: e.activation(tmpA[:], tmpA[:], AF.Ln,
                                            bias=self.konst[0:1, 0:1]), tva, self.t_k0)
            tse = p.act.do(lambda e, cs=cs: e.activation(rs_[:, cs], tmpA[:],
                                                         AF.Exp, scale=-0.5), inc=1)
            p.dve.do(lambda e: e.engine_nop(), tse, inc=1)
        p.barrier()

        if FF_PH < 5:
            continue
        for kc in range(4, 8):
            p.dve.do(lambda e, kc=kc: e.tensor_copy(
                self.col8d[:, kc - 4:kc - 3], ftv(kc)[:, 1023:1024]))
        for h in range(2):
            for tcl in range(2):
                tc = 2 * h + tcl
                cs = slice(512 * tc, 512 * (tc + 1))
                tmu = self.bcast_row(6, mus[:, cs], tuple(self.bank_write(6)))
                self.bank_written(6, tmu)
                trr = self.bcast_row(7, rs_[:, cs], tuple(self.bank_write(7)))
                self.bank_written(7, trr)
                tl = None
                for kc in range(8):
                    sh = 0 if kc < 4 else 1
                    a0 = 512 * tc - sh
                    gcol = self.col16[:, kc:kc + 1]
                    if h == 1 and tcl == 0 and kc >= 4:
                        def f1(e, kc=kc):
                            return e.tensor_tensor(tr1[:, 1:512], ftv(kc)[:, 1024:1535],
                                                   self.pb(6)[:, 1:512], op=ALU.subtract)
                        p.dve.do(f1, tmu, getattr(self, "_t_tr1W", None))
                        tl = p.dve.do(lambda e, kc=kc, tcl=tcl, gcol=gcol: e.scalar_tensor_tensor(
                            hs2n(kc)[:, tcl * 512 + 1:tcl * 512 + 512], tr1[:, 1:512], gcol,
                            self.pb(7)[:, 1:512], op0=ALU.mult, op1=ALU.mult),
                            trr, tlg, getattr(self, "_t_hs%d" % kc, None), inc=1)
                        p.dve.do(lambda e, kc=kc: e.tensor_tensor(
                            tr1[:, 0:1], self.col8d[:, kc - 4:kc - 3], self.pb(6)[:, 0:1],
                            op=ALU.subtract))
                        tl = p.dve.do(lambda e, kc=kc, tcl=tcl, gcol=gcol: e.scalar_tensor_tensor(
                            hs2n(kc)[:, tcl * 512:tcl * 512 + 1], tr1[:, 0:1], gcol,
                            self.pb(7)[:, 0:1], op0=ALU.mult, op1=ALU.mult), inc=1)
                        setattr(self, "_t_hs%d" % kc, tl)
                        self._t_tr1W = tl
                        continue
                    if a0 < 0:
                        def f1(e, kc=kc):
                            return e.tensor_tensor(tr1[:, 1:512], ftv(kc)[:, 0:511],
                                                   self.pb(6)[:, 1:512], op=ALU.subtract)
                        p.dve.do(f1, tmu, getattr(self, "_t_tr1W", None))
                        tl = p.dve.do(lambda e, kc=kc, tcl=tcl, gcol=gcol: e.scalar_tensor_tensor(
                            hs2n(kc)[:, tcl * 512 + 1:tcl * 512 + 512], tr1[:, 1:512], gcol,
                            self.pb(7)[:, 1:512], op0=ALU.mult, op1=ALU.mult),
                            trr, tlg, getattr(self, "_t_hs%d" % kc, None), inc=1)
                        p.dve.do(lambda e: e.tensor_copy(tr2[:, 0:1], self.pb(6)[:, 0:1]),
                                 tmu, getattr(self, "_t_tr2W", None))
                        ta2 = p.dve.do(lambda e: e.tensor_copy(tr2[:, 1:2], self.pb(7)[:, 0:1]),
                                       trr, inc=1)
                        p.dve.do(lambda e: e.tensor_tensor(tr2[:, 2:3], tr2[:, 0:1],
                                                           tr2[:, 1:2], op=ALU.mult), ta2)
                        p.dve.do(lambda e: e.tensor_scalar_mul(tr2[:, 3:4], tr2[:, 2:3], -1.0))
                        tl2 = p.dve.do(lambda e, kc=kc, tcl=tcl, gcol=gcol: e.tensor_scalar(
                            hs2n(kc)[:, tcl * 512:tcl * 512 + 1], tr2[:, 3:4], gcol, None,
                            op0=ALU.mult), tlg, inc=1)
                        self._t_tr2W = tl2
                        tl = tl2
                    else:
                        def f1(e, kc=kc, a0=a0):
                            return e.tensor_tensor(tr1[:], ftv(kc)[:, a0:a0 + 512],
                                                   self.pb(6), op=ALU.subtract)
                        p.dve.do(f1, tmu, getattr(self, "_t_tr1W", None))
                        tl = p.dve.do(lambda e, kc=kc, tcl=tcl, gcol=gcol: e.scalar_tensor_tensor(
                            hs2n(kc)[:, tcl * 512:(tcl + 1) * 512], tr1[:], gcol,
                            self.pb(7), op0=ALU.mult, op1=ALU.mult),
                            trr, tlg, getattr(self, "_t_hs%d" % kc, None), inc=1)
                    setattr(self, "_t_hs%d" % kc, tl)
                    self._t_tr1W = tl
                self.bank_did_read(6, tl)
                self.bank_did_read(7, tl)
            p.barrier()
            self._bank_r, self._bank_w = {}, {}
            for og in range(NOGH):
                i = og % 2
                key1 = (l, og)
                if key1 not in self.t_w1pre:
                    tda = p.dma(w1st[:, 0:1024],
                                self.dr[pf + "w1a"][og].rearrange("p a b -> p (a b)"),
                                getattr(self, "_t_w1st", None))
                    tdg = p.dma(w1st[:, 1024:2048],
                                self.dr[pf + "w1g"][og].rearrange("p a b -> p (a b)"))
                    tca = p.act.do(lambda e, i=i: e.copy(w1bf[i][:, 0:1024],
                                                         w1st[:, 0:1024]),
                                   tda, getattr(self, "_t_w1b%d" % i, None),
                                   getattr(self, "_t_w1pp%d" % i, None), inc=1)
                    tcg = p.act.do(lambda e, i=i: e.copy(w1bf[i][:, 1024:2048],
                                                         w1st[:, 1024:2048]), tdg, inc=1)
                    self._t_w1st = tcg
                    r0w = (l * NOGH + og) * 128
                    tsa = p.dma(self.wpre1[r0w:r0w + 128, :], w1bf[i][:], tca, tcg)
                    self.t_w1pre[key1] = tsa
                    setattr(self, "_t_w1pp%d" % i, tsa)
                else:
                    r0w = (l * NOGH + og) * 128
                    tld = p.dma(w1bf[i][:], self.wpre1[r0w:r0w + 128, :],
                                self.t_w1pre[key1],
                                getattr(self, "_t_w1b%d" % i, None),
                                getattr(self, "_t_w1pp%d" % i, None))
                    tca = tcg = tld
                tcbA = tcbG = None
                for kc in range(8):
                    da = ([tca, tlnb] + self.bank_write(4)) if kc == 0 else []
                    tcbA = p.pe.do(lambda e, kc=kc, i=i: e.matmul(
                        self.pb(4, 128, 1), w1bf[i][:, kc * 128:(kc + 1) * 128],
                        lnb_bf[:, kc:kc + 1], start=(kc == 0), stop=(kc == 7)),
                        *da, inc=(1 if kc == 7 else 0))
                    dg = ([tcg] + self.bank_write(5)) if kc == 0 else []
                    tcbG = p.pe.do(lambda e, kc=kc, i=i: e.matmul(
                        self.pb(5, 128, 1), w1bf[i][:, 1024 + kc * 128:1024 + (kc + 1) * 128],
                        lnb_bf[:, kc:kc + 1], start=(kc == 0), stop=(kc == 7)),
                        *dg, inc=(1 if kc == 7 else 0))
                self.bank_written(4, tcbA)
                self.bank_written(5, tcbG)
                b1a_col = self.aw_dtB[:, og:og + 1] if og < 16 else self.aw_D[:, og - 16:og - 15]
                b1g_col = self.aw_cvB[:, og:og + 1] if og < 16 else self.aw_conv[:, og - 16:og - 15]
                tca2 = p.dve.do(lambda e, b1a_col=b1a_col: e.tensor_scalar(
                    self.col8a[:, 0:1], self.pb(4, 128, 1), b1a_col, None, op0=ALU.add),
                    tcbA, tb1, getattr(self, "_t_cbA", None), inc=1)
                self.bank_did_read(4, tca2)
                tcg2 = p.dve.do(lambda e, b1g_col=b1g_col: e.tensor_scalar(
                    self.col8a[:, 1:2], self.pb(5, 128, 1), b1g_col, None, op0=ALU.add),
                    tcbG, getattr(self, "_t_cbG", None), inc=1)
                self.bank_did_read(5, tcg2)
                self._t_cbA = tca2
                self._t_cbG = tcg2
                for tcl in range(2):
                    tc = 2 * h + tcl
                    bA = tcl % 2
                    bG = 2 + tcl % 2
                    tmA = tmG = None
                    for kc in range(8):
                        dA_ = ([tca] + self.bank_write(bA)) if kc == 0 else []
                        tmA = p.pe.do(lambda e, kc=kc, i=i, bA=bA, tcl=tcl: e.matmul(
                            self.pb(bA), w1bf[i][:, kc * 128:(kc + 1) * 128],
                            hs2n(kc)[:, tcl * 512:(tcl + 1) * 512],
                            start=(kc == 0), stop=(kc == 7)),
                            *dA_, inc=(1 if kc == 7 else 0))
                        dG_ = ([tcg] + self.bank_write(bG)) if kc == 0 else []
                        tmG = p.pe.do(lambda e, kc=kc, i=i, bG=bG, tcl=tcl: e.matmul(
                            self.pb(bG), w1bf[i][:, 1024 + kc * 128:1024 + (kc + 1) * 128],
                            hs2n(kc)[:, tcl * 512:(tcl + 1) * 512],
                            start=(kc == 0), stop=(kc == 7)),
                            *dG_, inc=(1 if kc == 7 else 0))
                    self.bank_written(bA, tmA)
                    self.bank_written(bG, tmG)
                    tgg = p.act.do(lambda e, bG=bG: e.activation(
                        trb[:, 0:512], self.pb(bG), AF.Gelu, bias=self.col8a[:, 1:2]),
                        tmG, tcg2, getattr(self, "_t_ggW", None), inc=1)
                    self.bank_did_read(bG, tgg)
                    th2 = p.dve.do(lambda e, og=og, tcl=tcl, bA=bA: e.scalar_tensor_tensor(
                        h2(og)[:, tcl * 512:(tcl + 1) * 512], self.pb(bA),
                        self.col8a[:, 0:1], trb[:, 0:512], op0=ALU.add, op1=ALU.mult),
                        tmA, tca2, tgg, getattr(self, "_t_h2%d" % og, None), inc=1)
                    self.bank_did_read(bA, th2)
                    self._t_ggW = th2
                    setattr(self, "_t_h2%d" % og, th2)
                setattr(self, "_t_w1b%d" % i, (tmA, tmG, tcbA, tcbG))
            p.barrier()
            self._bank_r, self._bank_w = {}, {}
            for og in range(8):
                i = og % 2
                key2 = (l, og)
                if key2 not in self.t_w2pre:
                    tdw = p.dma(w2st[:],
                                self.dr[pf + "w2W"][og].rearrange("p a b -> p (a b)"),
                                getattr(self, "_t_w2st", None))
                    tcw = p.act.do(lambda e, i=i: e.copy(w2bf[i][:], w2st[:]),
                                   tdw, getattr(self, "_t_w2b%d" % i, None),
                                   getattr(self, "_t_w2pp%d" % i, None), inc=1)
                    self._t_w2st = tcw
                    r0w = (l * 8 + og) * 128
                    tsw = p.dma(self.wpre2[r0w:r0w + 128, :], w2bf[i][:], tcw)
                    self.t_w2pre[key2] = tsw
                    setattr(self, "_t_w2pp%d" % i, tsw)
                else:
                    r0w = (l * 8 + og) * 128
                    tcw = p.dma(w2bf[i][:], self.wpre2[r0w:r0w + 128, :],
                                self.t_w2pre[key2],
                                getattr(self, "_t_w2b%d" % i, None),
                                getattr(self, "_t_w2pp%d" % i, None))
                for tcl in range(2):
                    tc = 2 * h + tcl
                    bank = 4 + tcl % 2
                    tm = None
                    for kc in range(NOGH):
                        dd = ([tcw] + self.bank_write(bank)) if kc == 0 else []
                        tm = p.pe.do(lambda e, kc=kc, i=i, bank=bank, tcl=tcl: e.matmul(
                            self.pb(bank), w2bf[i][:, kc * 128:(kc + 1) * 128],
                            h2(kc)[:, tcl * 512:(tcl + 1) * 512],
                            start=(kc == 0), stop=(kc == NOGH - 1)),
                            *dd, inc=(1 if kc == NOGH - 1 else 0))
                    setattr(self, "_t_w2b%d" % i, tm)
                    self.bank_written(bank, tm)
                    b2col = self.col8b[:, og:og + 1]
                    tup = p.dve.do(lambda e, og=og, tc=tc, bank=bank, b2col=b2col:
                                   e.scalar_tensor_tensor(
                                       ftv(og)[:, tc * 512:(tc + 1) * 512], self.pb(bank),
                                       b2col, ftv(og)[:, tc * 512:(tc + 1) * 512],
                                       op0=ALU.add, op1=ALU.add),
                                   tm, tb2, inc=1)
                    self.bank_did_read(bank, tup)
            p.barrier()
        if FF_DBG & 4:
            self.emit_dump(2 + 3 * l, ftv)
    tcc = {0: None, 1: None}
    for g in range(8):
        for tc in range(4):
            i = tc % 2
            src = tr1 if i == 0 else tr2
            cs = slice(512 * tc, 512 * (tc + 1))
            td = p.dma(src[:], src_hbm[g * 128:(g + 1) * 128, cs], tcc[i])
            ta = p.dve.do(lambda e, g=g, cs=cs, src=src: e.tensor_tensor(
                src[:], src[:], ftv(g)[:, cs], op=ALU.add), td, inc=1)
            ts_ = p.dma(dst[g * 128:(g + 1) * 128, cs], src[:], ta)
            tcc[i] = ts_
    p.barrier()
    A.pop()


Builder.emit_ffn = emit_ffn


def emit_attn_b(self, src_hbm, dst_hbm):
    p, A = self.p, self.AR
    self._bank_r, self._bank_w = {}, {}
    self._t_bc = None
    self._t_xnc = {0: None, 1: None}
    A.push()
    xn3n = A.alloc_bf(8 * 2048)
    xb = [A.alloc(1026) for _ in range(8)]
    xab = A.alloc_bf(8 * 1024)
    xabv = lambda k: xab[:, k * 1024:(k + 1) * 1024]
    siluz = A.alloc_bf(8 * 1024)
    szv = lambda k: siluz[:, k * 1024:(k + 1) * 1024]
    gated = A.alloc_bf(8 * 1024)
    gtv = lambda k: gated[:, k * 1024:(k + 1) * 1024]
    t1b = A.alloc_bf(1024)
    t3B = A.alloc_bf(1024)
    t3C = A.alloc_bf(1024)
    wstg = A.alloc(1024)
    wbfg = [A.alloc_bf(1024) for _ in range(2)]
    xpw = A.alloc_bf(512)
    bgw = A.alloc_bf(128)
    cgw = A.alloc_bf(128)
    dtw = A.alloc_bf(1024)
    bpw = A.alloc_bf(1024)
    cpw = A.alloc_bf(1024)
    tra = A.alloc(512); trb2 = A.alloc(512); trc2 = A.alloc(512)
    trd = A.alloc(512); tre2 = A.alloc(512)
    stash = A.alloc(16)

    mu3 = self.rows_a[0:1, :]; r3 = self.rows_a[32:33, :]
    rc = self.rows_b[0:1, :]; mr3 = self.rows_b[32:33, :]
    t_stats = self.emit_stats(src_hbm, mu3, r3, EPS)
    tmpA = self.bc_rows[0:1, 512:1024]
    tmpB = self.rows_c[0:1, :]
    trows = None
    for tc in range(4):
        cs = slice(512 * tc, 512 * (tc + 1))
        p.dve.do(lambda e, cs=cs: e.tensor_tensor(tmpA[:], r3[:, cs], r3[:, cs],
                                                  op=ALU.mult), t_stats)
        p.dve.do(lambda e: e.reciprocal(tmpB[0:1, 0:512], tmpA[:]))
        p.dve.do(lambda e: e.tensor_scalar(tmpB[0:1, 0:512], tmpB[0:1, 0:512],
                                           -float(EPS), None, op0=ALU.add))
        p.dve.do(lambda e: e.tensor_tensor(tmpB[0:1, 0:512], tmpB[0:1, 0:512], tmpA[:],
                                           op=ALU.mult))
        tv = p.dve.do(lambda e: e.engine_nop(), inc=1)
        p.act.do(lambda e: e.activation(tmpB[0:1, 0:512], tmpB[0:1, 0:512], AF.Ln,
                                        bias=self.konst[0:1, 0:1]), tv, self.t_k0)
        ta = p.act.do(lambda e: e.activation(tmpB[0:1, 0:512], tmpB[0:1, 0:512],
                                             AF.Exp, scale=-0.5), inc=1)
        r3c = self.rows_c[0:1, 512:1024]
        p.dve.do(lambda e, cs=cs: e.tensor_copy(r3c[:], r3[:, cs]))
        p.dve.do(lambda e, cs=cs: e.tensor_tensor(rc[:, cs], r3c[:], tmpB[0:1, 0:512],
                                                  op=ALU.mult), ta)
        trows = p.dve.do(lambda e, cs=cs: e.tensor_tensor(mr3[:, cs], mu3[:, cs], r3c[:],
                                                          op=ALU.mult), inc=1)
    tg = p.dma(self.col16[:, 0:8], self.dr["b_gamma"][:])
    tbt = p.dma(self.col16[:, 8:16], self.dr["b_beta"][:])
    lnb_bf = self.col8c[:, 0:4].bitcast(BF16)[:, 0:8]
    tbt2 = p.pool.do(lambda e: e.tensor_copy(lnb_bf, self.col16[:, 8:16]), tbt, inc=1)
    tcv = p.dma(self.aw_conv[:, 0:24], self.dr["b_convW"][:])
    tcb = p.dma(self.aw_cvB[:, 0:8], self.dr["b_convB"][:])
    tdb = p.dma(self.aw_cvB[:, 8:16], self.dr["b_dtB"][:])
    tAc = p.dma(self.aw_dtB[:, 0:8], self.dr["b_A"][:])
    tDc = p.dma(self.aw_D[:, 0:8], self.dr["b_Dsk"][:])
    tw1 = p.dma(wstg[:, 0:512].rearrange("p (a b) -> p a b", a=8),
                self.dr["b_xpW"][:].transpose([1, 0, 2]))
    txp = p.pool.do(lambda e: e.tensor_copy(xpw[:], wstg[:, 0:512]), tw1, inc=1)
    tw2 = p.dma(wstg[:, 0:128].rearrange("p (a b) -> p a b", a=8),
                self.dr["b_BgW"][:].transpose([1, 0, 2]), txp)
    tbg = p.pool.do(lambda e: e.tensor_copy(bgw[:], wstg[:, 0:128]), tw2, inc=1)
    tw3 = p.dma(wstg[:, 0:128].rearrange("p (a b) -> p a b", a=8),
                self.dr["b_CgW"][:].transpose([1, 0, 2]), tbg)
    tcg = p.pool.do(lambda e: e.tensor_copy(cgw[:], wstg[:, 0:128]), tw3, inc=1)
    tw4 = p.dma(wstg[0:64, 0:1024], self.dr["b_dtW"][:], tcg)
    tdt = p.pool.do(lambda e: e.tensor_copy(dtw[0:64, :], wstg[0:64, 0:1024]), tw4, inc=1)
    tw5 = p.dma(wstg[0:16, 0:1024], self.dr["b_BpW"][:], tdt)
    tbp = p.pool.do(lambda e: e.tensor_copy(bpw[0:16, :], wstg[0:16, 0:1024]), tw5, inc=1)
    tw6 = p.dma(wstg[0:16, 0:1024], self.dr["b_CpW"][:], tbp)
    tcp = p.pool.do(lambda e: e.tensor_copy(cpw[0:16, :], wstg[0:16, 0:1024]), tw6, inc=1)

    for tc in range(4):
        cs = slice(512 * tc, 512 * (tc + 1))
        tmu = self.bcast_row(6, mu3[:, cs], tuple(self.bank_write(6)) + (t_stats,))
        self.bank_written(6, tmu)
        trr = self.bcast_row(7, rc[:, cs], tuple(self.bank_write(7)) + (trows,))
        self.bank_written(7, trr)
        tl = None
        if not hasattr(self, "_t_xnc"):
            self._t_xnc = {0: None, 1: None}
        tcc = self._t_xnc
        for g in range(8):
            i = g % 2
            src = tra if i == 0 else trb2
            td = p.dma(src[:], src_hbm[g * 128:(g + 1) * 128, cs], tcc[i])
            p.dve.do(lambda e, src=src: e.tensor_tensor(src[:], src[:], self.pb(6),
                                                        op=ALU.subtract), td, tmu)
            gcol = self.col16[:, g:g + 1]
            tl = p.dve.do(lambda e, g=g, cs=cs, src=src, gcol=gcol: e.scalar_tensor_tensor(
                xn3n[:, g * 2048:(g + 1) * 2048][:, cs], src[:], gcol, self.pb(7),
                op0=ALU.mult, op1=ALU.mult), trr, tg, inc=1)
            tcc[i] = tl
        self.bank_did_read(6, tl)
        self.bank_did_read(7, tl)
    p.barrier()

    tzero = p.dve.do(lambda e: e.memset(self.states_b[:], 0.0), inc=1)
    for h in range(2):
        self._bank_r, self._bank_w = {}, {}
        for og in range(16):
            i = og % 2
            keyb = ("bin", og)
            r0w = og * 128
            if keyb not in self.t_bpre:
                tdw = p.dma(wstg[:], self.dr["b_inW"][og].rearrange("p a b -> p (a b)"),
                            getattr(self, "_t_wstb", None))
                tcw = p.pool.do(lambda e, i=i: e.tensor_copy(wbfg[i][:], wstg[:]),
                                tdw, getattr(self, "_t_wbg%d" % i, None),
                                getattr(self, "_t_bpp%d" % i, None), inc=1)
                self._t_wstb = tcw
                tsb_ = p.dma(self.wpreb[r0w:r0w + 128, :], wbfg[i][:], tcw)
                self.t_bpre[keyb] = tsb_
                setattr(self, "_t_bpp%d" % i, tsb_)
            else:
                tcw = p.dma(wbfg[i][:], self.wpreb[r0w:r0w + 128, :],
                            self.t_bpre[keyb],
                            getattr(self, "_t_wbg%d" % i, None),
                            getattr(self, "_t_bpp%d" % i, None))
            tcbX = None
            for kc in range(8):
                dd = ([tcw, tbt2] + self.bank_write(2)) if kc == 0 else []
                tcbX = p.pe.do(lambda e, kc=kc, i=i: e.matmul(
                    self.pb(2, 128, 1), wbfg[i][:, kc * 128:(kc + 1) * 128],
                    lnb_bf[:, kc:kc + 1], start=(kc == 0), stop=(kc == 7)),
                    *dd, inc=(1 if kc == 7 else 0))
            self.bank_written(2, tcbX)
            tcb2 = p.act.do(lambda e: e.copy(self.col8b[:, 0:1], self.pb(2, 128, 1)),
                            tcbX, getattr(self, "_t_cbb", None), inc=1)
            self.bank_did_read(2, tcb2)
            self._t_cbb = tcb2
            for tcl in range(3):
                if tcl == 2:
                    if og >= 8:
                        continue
                    nc_ = 1024 * h + 1024
                    if nc_ >= T:
                        tmz = p.dve.do(lambda e, og=og: e.memset(xb[og][:, 1025:1026], 0.0),
                                       getattr(self, "_t_xb%d" % og, None), inc=1)
                        setattr(self, "_t_xb%d" % og, tmz)
                        continue
                    tm = None
                    for kc in range(8):
                        dd = ([tcw] + self.bank_write(3)) if kc == 0 else []
                        tm = p.pe.do(lambda e, kc=kc, i=i, nc_=nc_: e.matmul(
                            self.pb(3, 128, 1), wbfg[i][:, kc * 128:(kc + 1) * 128],
                            xn3n[:, kc * 2048 + nc_:kc * 2048 + nc_ + 1],
                            start=(kc == 0), stop=(kc == 7)), *dd, inc=(1 if kc == 7 else 0))
                    self.bank_written(3, tm)
                    tev = p.act.do(lambda e, og=og: e.activation(
                        xb[og][:, 1025:1026], self.pb(3, 128, 1), AF.Identity,
                        bias=self.col8b[:, 0:1]),
                        tm, tcb2, getattr(self, "_t_xb%d" % og, None), inc=1)
                    self.bank_did_read(3, tev)
                    setattr(self, "_t_xb%d" % og, tev)
                    continue
                tc = 2 * h + tcl
                bank = tcl % 2
                tm = None
                for kc in range(8):
                    dd = ([tcw] + self.bank_write(bank)) if kc == 0 else []
                    tm = p.pe.do(lambda e, kc=kc, i=i, bank=bank, tc=tc: e.matmul(
                        self.pb(bank), wbfg[i][:, kc * 128:(kc + 1) * 128],
                        xn3n[:, kc * 2048 + tc * 512:kc * 2048 + (tc + 1) * 512],
                        start=(kc == 0), stop=(kc == 7)), *dd, inc=(1 if kc == 7 else 0))
                self.bank_written(bank, tm)
                if og < 8:
                    if tcl == 0:
                        if h == 0:
                            tpd = p.dve.do(lambda e, og=og: e.memset(xb[og][:, 0:1], 0.0),
                                           getattr(self, "_t_xb%d" % og, None), inc=1)
                        else:
                            tpd = p.dve.do(lambda e, og=og: e.tensor_copy(
                                xb[og][:, 0:1], stash[:, og:og + 1]),
                                getattr(self, "_t_xb%d" % og, None),
                                getattr(self, "_t_st%d" % og, None), inc=1)
                        setattr(self, "_t_xb%d" % og, tpd)
                    tev = p.act.do(lambda e, og=og, tcl=tcl, bank=bank: e.activation(
                        xb[og][:, 1 + tcl * 512:1 + (tcl + 1) * 512], self.pb(bank),
                        AF.Identity, bias=self.col8b[:, 0:1]),
                        tm, tcb2, getattr(self, "_t_xb%d" % og, None), inc=1)
                    self.bank_did_read(bank, tev)
                    setattr(self, "_t_xb%d" % og, tev)
                    if tcl == 1 and h == 0:
                        tst_ = p.dve.do(lambda e, og=og: e.tensor_copy(
                            stash[:, og:og + 1], xb[og][:, 1024:1025]), tev, inc=1)
                        setattr(self, "_t_st%d" % og, tst_)
                else:
                    og2 = og - 8
                    tev = p.act.do(lambda e, og2=og2, tcl=tcl, bank=bank: e.activation(
                        szv(og2)[:, tcl * 512:(tcl + 1) * 512], self.pb(bank),
                        AF.Silu, bias=self.col8b[:, 0:1]),
                        tm, tcb2, getattr(self, "_t_sz%d" % og2, None), inc=1)
                    self.bank_did_read(bank, tev)
                    setattr(self, "_t_sz%d" % og2, tev)
            setattr(self, "_t_wbg%d" % i, tm)
        p.barrier()
        for og in range(8):
            for cc in range(2):
                a0 = cc * 512
                p.dve.do(lambda e, og=og, a0=a0: e.tensor_scalar_mul(
                    tra[:], xb[og][:, a0:a0 + 512], self.aw_conv[:, og * 3:og * 3 + 1]),
                    tcv, getattr(self, "_t_cnv", None))
                p.dve.do(lambda e, og=og, a0=a0: e.scalar_tensor_tensor(
                    trb2[:], xb[og][:, a0 + 1:a0 + 513],
                    self.aw_conv[:, og * 3 + 1:og * 3 + 2], tra[:],
                    op0=ALU.mult, op1=ALU.add))
                tcc2 = p.dve.do(lambda e, og=og, a0=a0: e.scalar_tensor_tensor(
                    tra[:], xb[og][:, a0 + 2:a0 + 514],
                    self.aw_conv[:, og * 3 + 2:og * 3 + 3], trb2[:],
                    op0=ALU.mult, op1=ALU.add), inc=1)
                tsl = p.act.do(lambda e, og=og, a0=a0: e.activation(
                    xabv(og)[:, a0:a0 + 512], tra[:], AF.Silu,
                    bias=self.aw_cvB[:, og:og + 1]), tcc2, tcb,
                    getattr(self, "_t_xa%d" % og, None), inc=1)
                setattr(self, "_t_xa%d" % og, tsl)
                self._t_cnv = tsl
        p.barrier()
        for tcl in range(2):
            tmx1 = tmx2 = tmx3 = None
            for kc in range(8):
                d1 = ([txp] + self.bank_write(0)) if kc == 0 else []
                tmx1 = p.pe.do(lambda e, kc=kc, tcl=tcl: e.matmul(
                    self.pb(0, 64), xpw[:, kc * 64:(kc + 1) * 64],
                    xabv(kc)[:, tcl * 512:(tcl + 1) * 512],
                    start=(kc == 0), stop=(kc == 7)), *d1, inc=(1 if kc == 7 else 0))
                d2 = ([tbg] + self.bank_write(1)) if kc == 0 else []
                tmx2 = p.pe.do(lambda e, kc=kc, tcl=tcl: e.matmul(
                    self.pb(1, 16), bgw[:, kc * 16:(kc + 1) * 16],
                    xabv(kc)[:, tcl * 512:(tcl + 1) * 512],
                    start=(kc == 0), stop=(kc == 7)), *d2, inc=(1 if kc == 7 else 0))
                d3 = ([tcg] + self.bank_write(2)) if kc == 0 else []
                tmx3 = p.pe.do(lambda e, kc=kc, tcl=tcl: e.matmul(
                    self.pb(2, 16), cgw[:, kc * 16:(kc + 1) * 16],
                    xabv(kc)[:, tcl * 512:(tcl + 1) * 512],
                    start=(kc == 0), stop=(kc == 7)), *d3, inc=(1 if kc == 7 else 0))
            self.bank_written(0, tmx1)
            self.bank_written(1, tmx2)
            self.bank_written(2, tmx3)
            te1 = p.act.do(lambda e, tcl=tcl: e.copy(
                t1b[0:64, tcl * 512:(tcl + 1) * 512], self.pb(0, 64)), tmx1,
                getattr(self, "_t_t1b", None), inc=1)
            self.bank_did_read(0, te1)
            self._t_t1b = te1
            te2 = p.act.do(lambda e, tcl=tcl: e.copy(
                t3B[0:16, tcl * 512:(tcl + 1) * 512], self.pb(1, 16)), tmx2,
                getattr(self, "_t_t3B", None), inc=1)
            self.bank_did_read(1, te2)
            self._t_t3B = te2
            te3 = p.act.do(lambda e, tcl=tcl: e.copy(
                t3C[0:16, tcl * 512:(tcl + 1) * 512], self.pb(2, 16)), tmx3,
                getattr(self, "_t_t3C", None), inc=1)
            self.bank_did_read(2, te3)
            self._t_t3C = te3
        p.barrier()
        for og in range(8):
            acol = self.aw_dtB[:, og:og + 1]
            dcol = self.aw_cvB[:, 8 + og:9 + og]
            Dcol = self.aw_D[:, og:og + 1]
            for tcl in range(2):
                tdm = p.pe.do(lambda e, og=og, tcl=tcl: e.matmul(
                    self.pb(3), dtw[0:64, og * 128:(og + 1) * 128],
                    t1b[0:64, tcl * 512:(tcl + 1) * 512], start=True, stop=True),
                    tdt, *self.bank_write(3), inc=1)
                self.bank_written(3, tdm)
                tr4 = p.dve.do(lambda e, dcol=dcol: e.tensor_scalar(
                    trd[:], self.pb(3), dcol, 0.0, op0=ALU.add, op1=ALU.max),
                    tdm, tdb, getattr(self, "_t_trd", None), inc=1)
                trA = p.act.do(lambda e, dcol=dcol: e.activation(
                    tra[:], self.pb(3), AF.Abs, bias=dcol), tdm, tr4,
                    getattr(self, "_t_spb", None), inc=1)
                self.bank_did_read(3, trA)
                self.bank_did_read(3, tr4)
                p.act.do(lambda e: e.activation(trb2[:], tra[:], AF.Exp, scale=-1.0))
                trC = p.act.do(lambda e: e.activation(tra[:], trb2[:], AF.Ln, bias=1.0),
                               inc=1)
                tdl = p.dve.do(lambda e: e.tensor_add(trd[:], trd[:], tra[:]), trC, inc=1)
                self._t_trd = tdl
                self._t_spb = tdl
                tBm = p.pe.do(lambda e, og=og, tcl=tcl: e.matmul(
                    self.pb(4), bpw[0:16, og * 128:(og + 1) * 128],
                    t3B[0:16, tcl * 512:(tcl + 1) * 512], start=True, stop=True),
                    tbp, *self.bank_write(4), inc=1)
                self.bank_written(4, tBm)
                tCm = p.pe.do(lambda e, og=og, tcl=tcl: e.matmul(
                    self.pb(5), cpw[0:16, og * 128:(og + 1) * 128],
                    t3C[0:16, tcl * 512:(tcl + 1) * 512], start=True, stop=True),
                    tcp, *self.bank_write(5), inc=1)
                self.bank_written(5, tCm)
                tda = p.act.do(lambda e, acol=acol: e.activation(
                    trb2[:], trd[:], AF.Exp, scale=acol), tdl, tAc,
                    getattr(self, "_t_da", None), inc=1)
                tbx = p.dve.do(lambda e: e.tensor_tensor(trc2[:], trd[:], self.pb(4),
                                                         op=ALU.mult),
                               tBm, tdl, getattr(self, "_t_bx", None), inc=1)
                self.bank_did_read(4, tbx)
                tbx2 = p.pool.do(lambda e, og=og, tcl=tcl: e.tensor_tensor(
                    tre2[:], trc2[:], xabv(og)[:, tcl * 512:(tcl + 1) * 512],
                    op=ALU.mult), tbx, getattr(self, "_t_bx2", None), inc=1)
                scol = self.states_b[:, og:og + 1]
                tsc = p.dve.do(lambda e, scol=scol: e.tensor_tensor_scan(
                    trc2[:], trb2[:], tre2[:], scol, op0=ALU.mult, op1=ALU.add),
                    tda, tbx2, tzero, inc=1)
                self._t_bx = tsc
                self._t_da = tsc
                tsv = p.dve.do(lambda e, scol=scol: e.tensor_copy(scol, trc2[:, 511:512]),
                               inc=1)
                ty = p.dve.do(lambda e: e.tensor_tensor(trc2[:], trc2[:], self.pb(5),
                                                        op=ALU.mult), tCm, inc=1)
                self.bank_did_read(5, ty)
                tg1 = p.dve.do(lambda e, og=og, tcl=tcl, Dcol=Dcol: e.scalar_tensor_tensor(
                    trd[:], xabv(og)[:, tcl * 512:(tcl + 1) * 512], Dcol, trc2[:],
                    op0=ALU.mult, op1=ALU.add), tDc, inc=1)
                self._t_bx2 = tg1
                tgt = p.pool.do(lambda e, og=og, tcl=tcl: e.tensor_tensor(
                    gtv(og)[:, tcl * 512:(tcl + 1) * 512], trd[:],
                    szv(og)[:, tcl * 512:(tcl + 1) * 512], op=ALU.mult),
                    tg1, getattr(self, "_t_gt%d" % og, None), inc=1)
                setattr(self, "_t_gt%d" % og, tgt)
                self._t_trd = tgt
        p.barrier()
        self._bank_r, self._bank_w = {}, {}
        for tcl in range(2):
            tc = 2 * h + tcl
            cs = slice(512 * tc, 512 * (tc + 1))
            tb0 = self.bcast_row(0, r3[:, cs], tuple(self.bank_write(0)))
            self.bank_written(0, tb0)
            tb1 = self.bcast_row(1, mr3[:, cs], tuple(self.bank_write(1)) + (trows,))
            self.bank_written(1, tb1)
            for og in range(8):
                i = og % 2
                bank = 6 + (og % 2)
                keyb = ("bout", og)
                r0w = (16 + og) * 128
                if keyb not in self.t_bpre:
                    tdw = p.dma(wstg[:],
                                self.dr["b_outW"][og].rearrange("p a b -> p (a b)"),
                                getattr(self, "_t_wstb", None))
                    tcw = p.pool.do(lambda e, i=i: e.tensor_copy(wbfg[i][:], wstg[:]),
                                    tdw, getattr(self, "_t_wbg%d" % i, None),
                                    getattr(self, "_t_bpp%d" % i, None), inc=1)
                    self._t_wstb = tcw
                    tsb_ = p.dma(self.wpreb[r0w:r0w + 128, :], wbfg[i][:], tcw)
                    self.t_bpre[keyb] = tsb_
                    setattr(self, "_t_bpp%d" % i, tsb_)
                else:
                    tcw = p.dma(wbfg[i][:], self.wpreb[r0w:r0w + 128, :],
                                self.t_bpre[keyb],
                                getattr(self, "_t_wbg%d" % i, None),
                                getattr(self, "_t_bpp%d" % i, None))
                tm = None
                for kc in range(8):
                    dd = ([tcw] + self.bank_write(bank)) if kc == 0 else []
                    tm = p.pe.do(lambda e, kc=kc, i=i, bank=bank, tcl=tcl: e.matmul(
                        self.pb(bank), wbfg[i][:, kc * 128:(kc + 1) * 128],
                        gtv(kc)[:, tcl * 512:(tcl + 1) * 512],
                        start=(kc == 0), stop=(kc == 7)), *dd, inc=(1 if kc == 7 else 0))
                setattr(self, "_t_wbg%d" % i, tm)
                self.bank_written(bank, tm)
                rows = slice(og * 128, (og + 1) * 128)
                td = p.dma(tra[:] if og % 2 == 0 else trb2[:], src_hbm[rows, cs],
                           getattr(self, "_t_sup%d" % (og % 2), None))
                src = tra if og % 2 == 0 else trb2
                p.dve.do(lambda e, src=src: e.tensor_tensor(trc2[:], src[:], self.pb(0),
                                                            op=ALU.mult),
                         td, tb0, getattr(self, "_t_trc2b", None))
                p.dve.do(lambda e, src=src: e.tensor_add(trc2[:], trc2[:], src[:]))
                p.dve.do(lambda e: e.tensor_tensor(trc2[:], trc2[:], self.pb(1),
                                                   op=ALU.subtract), tb1)
                tout = p.dve.do(lambda e, bank=bank: e.tensor_tensor(
                    trc2[:], trc2[:], self.pb(bank), op=ALU.add), tm, inc=1)
                self.bank_did_read(bank, tout)
                tst2 = p.dma(dst_hbm[rows, cs], trc2[:], tout)
                self._t_trc2b = tst2
                setattr(self, "_t_sup%d" % (og % 2), tst2)
            self.bank_did_read(0, tout)
            self.bank_did_read(1, tout)
        p.barrier()
    A.pop()


Builder.emit_attn_b = emit_attn_b
